# version 09
# speedup vs baseline: 1.4615x; 1.0766x over previous
import numpy as np
from contextlib import ExitStack

import concourse.bass as bass
import concourse.mybir as mybir
from concourse.bass_types import AP
from concourse.bass_utils import run_bass_kernel_spmd

F32 = mybir.dt.float32
BF16 = mybir.dt.bfloat16
AX = mybir.AxisListType.X
ALU = mybir.AluOpType
AF = mybir.ActivationFunctionType

NC = 8
D = 1024
DC = 8
T_OWN = 2048
HALO = 640
T_EXT = T_OWN + HALO
W_ATT = T_OWN + 128
NEG = -3.0e38
LN_EPS = 1e-5
RMS_EPS = 1e-6
D_INNER = 2048
D_STATE = 16
NE = 16
F0 = 4096
F_SH = 2048
GELU_C = 0.7978845608028654


def _slices(total, step, start=0):
    out = []
    o = start
    while o < total:
        out.append((o, min(step, total - o)))
        o += step
    return out


_GUARDED = {
    "tensor_tensor", "tensor_add", "tensor_sub", "tensor_mul", "tensor_max",
    "tensor_scalar", "tensor_single_scalar", "tensor_scalar_add",
    "tensor_scalar_sub", "tensor_scalar_mul", "tensor_scalar_max",
    "tensor_scalar_min", "tensor_relu", "tensor_reduce", "reduce_max",
    "reduce_sum", "scalar_tensor_tensor", "tensor_tensor_scan", "tensor_copy",
    "reciprocal", "select", "copy_predicated", "activation", "square", "sqrt",
    "mul", "copy", "add", "sign", "memset", "memzero", "bn_stats", "bn_aggr",
    "max", "max_index", "iota", "affine_select", "transpose", "stream_shuffle",
}


def _ap_name(a):
    try:
        return a.name
    except Exception:
        return None


class _EngProxy:

    def __init__(self, eng):
        self._eng = eng
        self._dirty = set()

    def _clear(self):
        self._dirty.clear()

    def __getattr__(self, name):
        real = getattr(self._eng, name)
        if name not in _GUARDED or not callable(real):
            return real

        def wrapped(*args, **kw):
            aps = [a for a in args if isinstance(a, AP)]
            aps += [v for v in kw.values() if isinstance(v, AP)]
            names = {_ap_name(a) for a in aps}
            if names & self._dirty:
                self._eng.drain()
                self._dirty.clear()
            r = real(*args, **kw)
            if aps:
                self._dirty.add(_ap_name(aps[0]))
            ao = kw.get("accum_out")
            if isinstance(ao, AP):
                self._dirty.add(_ap_name(ao))
            return r

        return wrapped


class _NCShim:

    def __init__(self, nc):
        self._nc = nc
        self.vector = _EngProxy(nc.vector)
        self.scalar = _EngProxy(nc.scalar)
        self.gpsimd = _EngProxy(nc.gpsimd)
        self.tensor = nc.tensor
        self.sync = nc.sync

    def __getattr__(self, name):
        return getattr(self._nc, name)

    def proxy_of(self, eng):
        for p in (self.vector, self.scalar, self.gpsimd):
            if p._eng is eng:
                return p
        return None


class B:
    def __init__(self, name="k"):
        self.rawnc = bass.Bass("TRN2", target_bir_lowering=False)
        self.nc = _NCShim(self.rawnc)
        self.ctx = ExitStack()
        self.n = {}
        self.sems = {}
        self._waited = {}
        self._inline = {}
        for key in ("pe", "sc", "ve", "po", "sy", "dl", "st"):
            self.sems[key] = self.ctx.enter_context(self.rawnc.semaphore(key))
            self.n[key] = 0

    def sbuf(self, name, shape, dtype=F32):
        if getattr(self, "_scopes", None):
            return self._scopes[-1].enter_context(
                self.rawnc.sbuf_tensor(name, shape, dtype,
                                       allow_name_mangling=True)).ap()
        return self.nc.alloc_sbuf_tensor(name, shape, dtype).ap()

    def psum(self, name, shape, dtype=F32):
        if getattr(self, "_scopes", None):
            return self._scopes[-1].enter_context(
                self.rawnc.psum_tensor(name, shape, dtype,
                                       allow_name_mangling=True)).ap()
        return self.nc.alloc_psum_tensor(name, shape, dtype).ap()

    from contextlib import contextmanager as _cm

    @_cm
    def scope(self):
        if not hasattr(self, "_scopes"):
            self._scopes = []
        es = ExitStack()
        self._scopes.append(es)
        try:
            with es:
                yield
        finally:
            self._scopes.pop()

    def flush_stores(self):
        self.barrier()
        for eng in self.rawnc.engines.values():
            self.w(eng, "st")
        self.barrier()

    def inline(self, name, arr):
        if name not in self._inline:
            self._inline[name] = self.nc.inline_tensor(
                np.ascontiguousarray(arr), name).ap()
        return self._inline[name]

    def reg_fconst(self, *vals):
        for v in vals:
            ap = self.const_sbuf(f"fc_{v}", np.full((128, 1), v, np.float32))
            self.nc.const_aps.register(F32, float(v), ap)

    def const_sbuf(self, name, arr, dtype=None):
        key = "cs_" + name
        if key not in self._inline:
            dram = self.nc.inline_tensor(np.ascontiguousarray(arr), name).ap()
            if dtype is None:
                dtype = dram.dtype
            sb = self.sbuf(key, list(arr.shape), dtype)
            self.load(sb, dram)
            self.barrier()
            for eng in self.rawnc.engines.values():
                self.w(eng, "dl")
            self.barrier()
            self._inline[key] = sb
        return self._inline[key]

    def inp(self, name, shape, dtype=F32):
        return self.nc.dram_tensor(name, list(shape), dtype, kind="ExternalInput").ap()

    def out(self, name, shape, dtype=F32):
        return self.nc.dram_tensor(name, list(shape), dtype, kind="ExternalOutput").ap()

    def dram(self, name, shape, dtype=F32):
        if not hasattr(self, "_dram_names"):
            self._dram_names = set()
        base, k = name, 1
        while name in self._dram_names:
            k += 1
            name = f"{base}_{k}"
        self._dram_names.add(name)
        return self.nc.dram_tensor(name, list(shape), dtype, kind="Internal").ap()

    def i(self, instr, key, drain=True):
        self.n[key] += 1
        if key == "pe" or not drain:
            instr.then_inc(self.sems[key], 1)
            return self.n[key]
        eng = self.rawnc.engines[instr.ins.engine]
        eng.drain().then_inc(self.sems[key], 1)
        p = self.nc.proxy_of(eng)
        if p is not None:
            p._clear()
        return self.n[key]

    def w(self, eng, key, val=None):
        if isinstance(eng, _EngProxy):
            eng = eng._eng
        if val is None:
            val = self.n[key]
        if val <= 0:
            return
        k = (self._ekey(eng), key)
        if self._waited.get(k, -1) >= val:
            return
        eng.wait_ge(self.sems[key], val)
        self._waited[k] = val

    def load(self, dst, src, q=None, **kw):
        eng = q if q is not None else self.nc.sync
        eng.dma_start(out=dst, in_=src, **kw).then_inc(self.sems["dl"], 16)
        self.n["dl"] += 16
        return self.n["dl"]

    def store(self, dst, src, q=None, **kw):
        eng = q if q is not None else self.nc.sync
        eng.dma_start(out=dst, in_=src, **kw).then_inc(self.sems["st"], 16)
        self.n["st"] += 16
        return self.n["st"]

    def barrier(self):
        self.rawnc.all_engine_barrier()
        for p in (self.nc.vector, self.nc.scalar, self.nc.gpsimd):
            p._clear()
        for key in ("pe", "sc", "ve", "po", "sy"):
            for eng in self.nc.engines.values():
                self._waited[(self._ekey(eng), key)] = self.n[key]

    def _ekey(self, eng):
        return eng.engine if hasattr(eng, "engine") else id(eng)

    def finish(self):
        self.rawnc.sync.wait_ge(self.sems["st"], self.n["st"])
        self.ctx.close()
        return self.rawnc


def _ident128(b):
    return b.const_sbuf("ident128", np.eye(128, dtype=np.float32))


def _ones_row(b):
    return b.const_sbuf("ones_row", np.ones((1, 128), np.float32))


def _stat_lhs(b):
    import ml_dtypes
    arr = np.zeros((128, 2), np.float32)
    arr[:, 0] = 1.0 / D
    arr[:, 1] = 1.0
    return b.const_sbuf("stat_lhs", arr.astype(ml_dtypes.bfloat16))


def emit_wcast(b, jobs):
    nc = b.nc
    CM = 4096
    with ExitStack() as wctx:
        stg = [wctx.enter_context(b.rawnc.sbuf_tensor(f"wcs{i}", [128, CM], F32)).ap()
               for i in range(2)]
        stgb = [wctx.enter_context(b.rawnc.sbuf_tensor(f"wcb{i}", [128, CM], BF16)).ap()
                for i in range(2)]
        last_cast = [0, 0]
        last_store = [0, 0]
        i = 0
        for src_ap, dst in jobs:
            G = src_ap.shape[0]
            C = src_ap.shape[2]
            for g in range(G):
                for (co, cn_w) in _slices(C, CM):
                    k = i % 2
                    if last_cast[k]:
                        b.w(nc.sync, "ve", last_cast[k])
                    ln = b.load(stg[k][:, 0:cn_w], src_ap[g][:, co:co + cn_w])
                    b.w(nc.vector, "dl", ln)
                    if last_store[k]:
                        b.w(nc.vector, "st")
                    cn = b.i(nc.vector.tensor_copy(stgb[k][:, 0:cn_w],
                                                   stg[k][:, 0:cn_w]), "ve")
                    last_cast[k] = cn
                    b.w(nc.sync, "ve", cn)
                    last_store[k] = b.store(dst[g][:, co:co + cn_w],
                                            stgb[k][:, 0:cn_w])
                    i += 1
        b.barrier()


def alloc_ln_scratch(b, W, tag=""):
    return dict(
        xcast=[b.sbuf(f"lnxc{tag}{i}", [128, 512], BF16) for i in range(2)],
        sq=[b.sbuf(f"lnsq{tag}{i}", [128, 512], BF16) for i in range(2)],
        xs=[b.sbuf(f"lnxs{tag}{i}", [128, 512], F32) for i in range(2)],
        alpha_r=b.sbuf(f"lnar{tag}", [1, W], F32),
        beta_r=b.sbuf(f"lnbr{tag}", [1, W], F32),
        alpha_b=b.sbuf(f"lnab{tag}", [128, W], F32),
        beta_b=b.sbuf(f"lnbb{tag}", [128, W], F32),
    )


def emit_ln_stats(b, x_src, W, scr, ps_a, ps_b):
    import os
    sub = int(os.environ.get("KDEBUG_SUB", "9"))
    nc = b.nc
    stat_lhs = _stat_lhs(b)
    ones_row = _ones_row(b)
    mean_r, sq_r = scr["alpha_r"], scr["beta_r"]
    last_mm = [0, 0]
    for (o, n) in _slices(W, 512):
        for c in range(DC):
            k = c % 2
            if last_mm[k]:
                b.w(nc.sync, "pe", last_mm[k])
            ln = b.load(scr["xs"][k][:, 0:n], x_src[:, c, o:o + n])
            b.w(nc.scalar, "dl", ln)
            if last_mm[k]:
                b.w(nc.scalar, "pe", last_mm[k])
            nc.scalar.copy(scr["xcast"][k][:, 0:n], scr["xs"][k][:, 0:n])
            sn = b.i(nc.scalar.activation(scr["sq"][k][:, 0:n],
                                          scr["xcast"][k][:, 0:n], AF.Square), "sc")
            b.w(nc.tensor, "sc", sn)
            nc.tensor.matmul(ps_a[0:1, 0:n], stat_lhs[:, 0:1], scr["xcast"][k][:, 0:n],
                             start=(c == 0), stop=(c == DC - 1), skip_group_check=True)
            mmn = b.i(
                nc.tensor.matmul(ps_b[0:1, 0:n], stat_lhs[:, 1:2], scr["sq"][k][:, 0:n],
                                 start=(c == 0), stop=(c == DC - 1),
                                 skip_group_check=True), "pe")
            last_mm[k] = mmn
        b.w(nc.vector, "pe", last_mm[(DC - 1) % 2])
        nc.vector.tensor_copy(mean_r[:, o:o + n], ps_a[0:1, 0:n])
        vn = b.i(nc.vector.tensor_copy(sq_r[:, o:o + n], ps_b[0:1, 0:n]), "ve")
        b.w(nc.tensor, "ve", vn)
    b.barrier()
    if sub == 0:
        return
    ar, br = scr["alpha_r"], scr["beta_r"]
    tmp_r = scr["alpha_b"][0:1, :]
    nc.scalar.mul(br[:, 0:W], br[:, 0:W], 1.0 / D)
    sn = b.i(nc.scalar.activation(tmp_r[:, 0:W], ar[:, 0:W], AF.Square), "sc")
    b.w(nc.vector, "sc", sn)
    vn = b.i(nc.vector.tensor_sub(br[:, 0:W], br[:, 0:W], tmp_r[:, 0:W]), "ve")
    b.w(nc.scalar, "ve", vn)
    sn = b.i(nc.scalar.activation(br[:, 0:W], br[:, 0:W], AF.Sqrt, bias=LN_EPS), "sc")
    b.w(nc.vector, "sc", sn)
    vn = b.i(nc.vector.reciprocal(br[:, 0:W], br[:, 0:W]), "ve")
    b.w(nc.vector, "ve", vn)
    nc.vector.tensor_mul(ar[:, 0:W], ar[:, 0:W], br[:, 0:W])
    nc.vector.tensor_scalar(ar[:, 0:W], ar[:, 0:W], -1.0, None, ALU.mult)
    nc.vector.tensor_copy(tmp_r[:, 0:W], ar[:, 0:W])
    nc.vector.tensor_copy(ar[:, 0:W], br[:, 0:W])
    vn = b.i(nc.vector.tensor_copy(br[:, 0:W], tmp_r[:, 0:W]), "ve")
    b.barrier()
    if sub == 1:
        return
    for (o, n) in _slices(W, 512):
        mm = b.i(nc.tensor.matmul(ps_a[:, 0:n], ones_row, ar[:, o:o + n],
                                  start=True, stop=True), "pe")
        b.w(nc.vector, "pe", mm)
        vn = b.i(nc.vector.tensor_copy(scr["alpha_b"][:, o:o + n], ps_a[:, 0:n]), "ve")
        b.w(nc.tensor, "ve", vn)
        mm = b.i(nc.tensor.matmul(ps_a[:, 0:n], ones_row, br[:, o:o + n],
                                  start=True, stop=True), "pe")
        b.w(nc.vector, "pe", mm)
        vn = b.i(nc.vector.tensor_copy(scr["beta_b"][:, o:o + n], ps_a[:, 0:n]), "ve")
        b.w(nc.tensor, "ve", vn)
    b.barrier()


def emit_ln_apply(b, x_sl, scr, o, n, out_bf, tmp):
    nc = b.nc
    nc.vector.tensor_mul(tmp[:, 0:n], x_sl[:, 0:n], scr["alpha_b"][:, o:o + n])
    return b.i(nc.vector.tensor_add(out_bf[:, 0:n], tmp[:, 0:n],
                                    scr["beta_b"][:, o:o + n]), "ve")


def build_l0ffn(W):
    b = B()
    x_in = b.inp("x_in", [128, DC, W])
    w1 = b.inp("w1", [8, 128, DC * 512])
    w2 = b.inp("w2", [8, 128, 4 * D])
    rw = b.inp("rw", [D, 8])
    b1 = b.inp("b1", [128, F0 // 128])
    b2 = b.inp("b2", [128, DC])
    x_out = b.out("x_out", [128, DC, W])
    w1b = b.dram("w1b", [8, 128, DC * 512], BF16)
    w2b = b.dram("w2b", [8, 128, 4 * D], BF16)
    b.reg_fconst(LN_EPS)
    emit_wcast(b, [(w1, w1b), (w2, w2b)])
    with b.scope():
        emit_l0ffn_stage(b, x_in, x_out, W, w1b, w2b, rw, b1, b2)
    return b.finish()


def emit_l0ffn_stage(b, x_in, x_out, W, w1b, w2b, rw, b1, b2):
    nc = b.nc
    import os
    phase = int(os.environ.get("KDEBUG_PHASE", "9"))
    ps_a = b.psum("ps_a", [128, 512])
    ps_b = b.psum("ps_b", [128, 512])
    ps_e = b.psum("ps_e", [128, 4, 512])

    scr = alloc_ln_scratch(b, W)
    emit_ln_stats(b, x_in, W, scr, ps_a, ps_b)
    if os.environ.get("KDEBUG_DUMP") and not getattr(b, "_dbg_used", False):
        b._dbg_used = True
        d_ab = b.out("d_ab", [2, W])
        b.store(d_ab[0:1, :], scr["alpha_r"][:, 0:W])
        b.store(d_ab[1:2, :], scr["beta_r"][:, 0:W])
        b.dbg_fac = b.out("d_fac", [8, 512])
        b.dbg_ftm = b.out("d_ftm", [128, 4])
        b.dbg_r = b.out("d_r", [128, 24])
        b.dbg_gh = b.out("d_gh", [128, 4, 512], BF16)
        b.dbg_eo = b.out("d_eo", [128, 2, 256])
        b.dbg_ft2 = b.out("d_ft2", [4, 128])
        b.dbg_xn = b.out("d_xn", [128, W], BF16)
    else:
        b.dbg_fac = None
        b.dbg_xn = None
    if phase == 1:
        return

    rws = b.sbuf("rws", [128, DC, 8], BF16)
    rwf = b.sbuf("rwf", [128, DC, 8], F32)
    for c in range(DC):
        b.load(rwf[:, c, :], rw[c * 128:(c + 1) * 128, :])
    b.w(nc.vector, "dl")
    b.i(nc.vector.tensor_copy(rws[:, :, :], rwf[:, :, :]), "ve")
    b1s = b.sbuf("b1s", [128, F0 // 128])
    b2s = b.sbuf("b2s", [128, DC])
    b.load(b1s, b1)
    b.load(b2s, b2)
    b.barrier()

    emit_l0ffn_slices(b, x_in, x_out, W, scr, w1b, w2b, rws, b1s, b2s,
                      ps_a, ps_b, ps_e)


def emit_l0ffn_slices(b, x_in, x_out, W, scr, w1b, w2b, rws, b1s, b2s,
                      ps_a, ps_b, ps_e):
    nc = b.nc
    ident = _ident128(b)
    ones_row = _ones_row(b)
    NF = F0 // 128

    x_s = [b.sbuf(f"l0x{i}", [128, DC, 512], F32) for i in range(2)]
    xn_s = b.sbuf("l0xn", [128, DC, 512], BF16)
    tmp = b.sbuf("l0tmp", [128, 512], F32)
    gh = b.sbuf("l0gh", [128, NF, 512], BF16)
    w1blk = [b.sbuf(f"l0w1{i}", [128, DC, 512], BF16) for i in range(2)]
    w2g = [b.sbuf(f"l0w2{i}", [128, 4, D], BF16) for i in range(2)]
    eo = b.sbuf("l0eo", [128, 512], F32)
    b2s_dbg = b.sbuf("l0eodbg", [128, 512], F32)
    t1 = b.sbuf("l0t1", [128, 512], F32)
    r_pr = b.sbuf("l0rpr", [128, 8], F32)
    r_m1 = b.sbuf("l0rm1", [128, 1], F32)
    r_m2 = b.sbuf("l0rm2", [128, 1], F32)
    r_s = b.sbuf("l0rs", [128, 1], F32)
    r_ge = b.sbuf("l0rge", [128, 8], F32)
    r_t1 = b.sbuf("l0rt1", [128, 8], F32)
    fac_tm = b.sbuf("l0ftm", [128, 4], F32)
    fac_t2 = b.sbuf("l0ft2", [4, 128], F32)
    fac_row = b.sbuf("l0frow", [1, 512], F32)
    fac_b = b.sbuf("l0fb", [128, 512], F32)
    fac_dram = b.dram("l0fdram", [4, 128], F32)

    import os
    nsl = int(os.environ.get("KDEBUG_NSLICE", "99"))
    parts = os.environ.get("KDEBUG_PARTS", "rab")
    for si, (o, n) in enumerate(_slices(W, 512)):
        if si >= nsl:
            break
        k = si % 2
        b.w(nc.sync, "st")
        ln_x = b.load(x_s[k][:, :, 0:n], x_in[:, :, o:o + n])
        b.w(nc.vector, "dl", ln_x)
        for c in range(DC):
            emit_ln_apply(b, x_s[k][:, c, 0:n], scr, o, n, xn_s[:, c, 0:n], tmp)
        vn_xn = b.n["ve"]
        if b.dbg_xn is not None:
            b.w(nc.sync, "ve", vn_xn)
            b.store(b.dbg_xn[:, o:o + n], xn_s[:, 0, 0:n])

        rstop = int(os.environ.get("KDEBUG_RSTOP", "9"))
        nt = (n + 127) // 128
        for t in range(nt if "r" in parts else 0):
            tn = min(128, n - t * 128)
            b.w(nc.tensor, "ve", vn_xn)
            b.w(nc.tensor, "sc")
            b.w(nc.tensor, "ve")
            for c in range(DC):
                mm = nc.tensor.matmul(ps_b[0:tn, 0:8],
                                      xn_s[:, c, t * 128:t * 128 + tn],
                                      rws[:, c, :], start=(c == 0), stop=(c == DC - 1))
            pe_r = b.i(mm, "pe")
            b.w(nc.vector, "pe", pe_r)
            ve1 = b.i(nc.vector.reduce_max(r_m1[0:tn, 0:1], ps_b[0:tn, 0:8], axis=AX,
                                           negate=True), "ve")
            b.w(nc.scalar, "ve", ve1)
            sc1 = b.i(nc.scalar.activation(r_pr[0:tn, :], ps_b[0:tn, 0:8], AF.Exp,
                                           bias=r_m1[0:tn, 0:1]), "sc")
            b.w(nc.vector, "sc", sc1)
            if b.dbg_fac is not None and si == 0 and t == 0:
                ve_d = b.i(nc.vector.tensor_copy(fac_b[:, 0:8], ps_b[0:tn, 0:8]), "ve")
                b.w(nc.sync, "ve", ve_d)
                b.store(b.dbg_r[:, 0:8], fac_b[:, 0:8])
                b.store(b.dbg_r[:, 8:16], r_pr[:, 0:8])
                b.store(b.dbg_r[:, 16:17], r_m1[:, 0:1], allow_slow_non_contiguous=True)
                b.w(nc.vector, "st")
            if rstop <= 0:
                continue
            ones8 = b.const_sbuf("ones8", np.ones((128, 8), np.float32))
            zero1 = b.const_sbuf("zero1", np.zeros((128, 1), np.float32))
            nc.vector.reduce_max(r_m1[0:tn, 0:1], r_pr[0:tn, :], axis=AX, negate=True)
            nc.vector.scalar_tensor_tensor(r_t1[0:tn, :], ones8[0:tn, :],
                                           r_m1[0:tn, 0:1], r_pr[0:tn, :],
                                           ALU.mult, ALU.add)
            nc.vector.tensor_scalar(r_ge[0:tn, :], r_t1[0:tn, :], 0.0,
                                    None, ALU.is_ge)
            nc.vector.scalar_tensor_tensor(r_t1[0:tn, :], r_ge[0:tn, :], -1e30,
                                           r_pr[0:tn, :], ALU.mult, ALU.add)
            nc.vector.reduce_max(r_m2[0:tn, 0:1], r_t1[0:tn, :], axis=AX)
            nc.vector.tensor_sub(r_s[0:tn, 0:1], r_m2[0:tn, 0:1], r_m1[0:tn, 0:1])
            nc.vector.tensor_sub(r_m1[0:tn, 0:1], zero1[0:tn, 0:1], r_m1[0:tn, 0:1])
            if b.dbg_fac is not None and si == 0 and t == 0:
                ved = b.i(nc.vector.tensor_copy(fac_b[:, 8:16], r_t1[:, 0:8]), "ve")
                b.w(nc.sync, "ve", ved)
                b.store(b.dbg_r[:, 17:18], r_m1[:, 0:1], allow_slow_non_contiguous=True)
                b.store(b.dbg_r[:, 18:19], r_m2[:, 0:1], allow_slow_non_contiguous=True)
                b.store(b.dbg_r[:, 19:20], r_s[:, 0:1], allow_slow_non_contiguous=True)
                b.w(nc.sync, "ve", ved)
                b.store(b.dbg_r[:, 8:16], fac_b[:, 8:16])
                b.w(nc.vector, "st")
            nc.vector.reciprocal(r_s[0:tn, 0:1], r_s[0:tn, 0:1])
            nc.vector.tensor_mul(r_m1[0:tn, 0:1], r_m1[0:tn, 0:1], r_s[0:tn, 0:1])
            nc.vector.tensor_mul(r_m2[0:tn, 0:1], r_m2[0:tn, 0:1], r_s[0:tn, 0:1])
            ve2 = b.i(nc.vector.tensor_add(fac_tm[0:tn, t:t + 1], r_m1[0:tn, 0:1],
                                           r_m2[0:tn, 0:1]), "ve")
            if tn < 128 and si == 0 and t == 0:
                pass
        if "r" not in parts or rstop <= 1:
            nc.vector.memset(fac_tm[:, 0:4], 1.0)
        b.w(nc.tensor, "ve")
        pe_t = b.i(nc.tensor.transpose(ps_a[0:4, 0:128], fac_tm[:, 0:4], ident), "pe")
        b.w(nc.vector, "pe", pe_t)
        ve3 = b.i(nc.vector.tensor_copy(fac_t2[:, :], ps_a[0:4, 0:128]), "ve")
        b.w(nc.sync, "ve", ve3)
        if b.dbg_fac is not None and si == 0:
            b.store(b.dbg_ftm, fac_tm[:, 0:4])
            b.store(b.dbg_ft2, fac_t2)
        if rstop <= 2:
            nc.vector.memset(fac_row[0:1, 0:n], 1.0)
        st1 = b.store(fac_dram, fac_t2)
        nc.sync.wait_ge(b.sems["st"], st1)
        if rstop > 2:
            ln_f = b.load(fac_row[0:1, 0:n],
                          fac_dram.rearrange("a b -> (a b)").unsqueeze(0)[:, 0:n])
            b.w(nc.tensor, "dl", ln_f)
        b.w(nc.vector, "dl")
        b.w(nc.tensor, "ve")
        b.w(nc.tensor, "sc")
        mm = nc.tensor.matmul(ps_b[:, 0:n], ones_row, fac_row[0:1, 0:n],
                              start=True, stop=True)
        pe_f = b.i(mm, "pe")
        b.w(nc.vector, "pe", pe_f)
        ve_f = b.i(nc.vector.tensor_copy(fac_b[:, 0:n], ps_b[:, 0:n]), "ve")
        if b.dbg_fac is not None:
            b.w(nc.sync, "dl")
            b.store(b.dbg_fac[si:si + 1, 0:n], fac_row[0:1, 0:n])

        last_gelu = {0: 0, 1: 0}
        last_w1pe = [0, 0]
        for fgi in (range(F0 // 512) if "a" in parts else []):
            kk = fgi % 2
            if last_w1pe[kk]:
                b.w(nc.sync, "pe", last_w1pe[kk])
            ln_w = b.load(w1blk[kk], w1b[fgi])
            for f4 in range(4):
                f = fgi * 4 + f4
                pp = f % 2
                ps_h = ps_a if pp == 0 else ps_b
                b.w(nc.tensor, "dl", ln_w)
                b.w(nc.tensor, "ve", max(vn_xn, ve_f))
                if last_gelu[pp]:
                    b.w(nc.tensor, "sc", last_gelu[pp])
                for c in range(DC):
                    mm = nc.tensor.matmul(ps_h[:, 0:n],
                                          w1blk[kk][:, c, f4 * 128:(f4 + 1) * 128],
                                          xn_s[:, c, 0:n],
                                          start=(c == 0), stop=(c == DC - 1))
                pe_h = b.i(mm, "pe")
                last_w1pe[kk] = pe_h
                b.w(nc.scalar, "pe", pe_h)
                last_gelu[pp] = b.i(
                    nc.scalar.activation(gh[:, f, 0:n], ps_h[:, 0:n],
                                         AF.Gelu_apprx_tanh, bias=b1s[:, f:f + 1]),
                    "sc")
        sc_gh = b.n["sc"]
        if b.dbg_fac is not None and si == 0:
            b.w(nc.sync, "sc", sc_gh)
            for ff in range(4):
                b.store(b.dbg_gh[:, ff, :], gh[:, ff * 8, 0:n])

        last_w2pe = [0, 0]
        for half in (range(2) if "b" in parts else []):
            ve_before = b.n["ve"]
            for fg in range(NF // 4):
                kk = fg % 2
                if last_w2pe[kk]:
                    b.w(nc.sync, "pe", last_w2pe[kk])
                ln2 = b.load(w2g[kk], w2b[fg])
                b.w(nc.tensor, "dl", ln2)
                b.w(nc.tensor, "ve", ve_before)
                b.w(nc.tensor, "sc", sc_gh)
                for fr in range(4):
                    f = fg * 4 + fr
                    for ci in range(4):
                        c = half * 4 + ci
                        mm = nc.tensor.matmul(ps_e[:, ci, 0:n],
                                              w2g[kk][:, fr, c * 128:(c + 1) * 128],
                                              gh[:, f, 0:n],
                                              start=(f == 0), stop=(f == NF - 1),
                                              skip_group_check=True)
                last_w2pe[kk] = b.i(mm, "pe")
            pe_eo = b.n["pe"]
            for ci in range(4):
                c = half * 4 + ci
                b.w(nc.vector, "pe", pe_eo)
                nc.vector.tensor_scalar(eo[:, 0:n], ps_e[:, ci, 0:n],
                                        b2s[:, c:c + 1], None, ALU.add)
                nc.vector.tensor_mul(t1[:, 0:n], eo[:, 0:n], fac_b[:, 0:n])
                if b.dbg_fac is not None and si == 0 and c == 0:
                    ved2 = b.i(nc.vector.tensor_copy(b2s_dbg[:, 0:n], eo[:, 0:n]), "ve")
                    b.w(nc.sync, "ve", ved2)
                    b.store(b.dbg_eo[:, 0, 0:256], b2s_dbg[:, 0:256])
                    b.store(b.dbg_eo[:, 1, 0:256], b2s_dbg[:, 256:512])
                    b.w(nc.vector, "st")
                ve_o = b.i(
                    nc.vector.scalar_tensor_tensor(x_s[k][:, c, 0:n],
                                                   t1[:, 0:n], 0.5,
                                                   x_s[k][:, c, 0:n],
                                                   ALU.mult, ALU.add), "ve")
                b.w(nc.sync, "ve", ve_o)
                b.store(x_out[:, c, o:o + n], x_s[k][:, c, 0:n])
        b.barrier()


def _pm(w, *resh_perm):
    resh, perm, out = resh_perm
    a = np.ascontiguousarray(np.asarray(w, np.float32).reshape(resh).transpose(perm))
    return np.ascontiguousarray(a.reshape(out))


def _fm_vec(vec, nchunk):
    return np.ascontiguousarray(
        np.asarray(vec, np.float32).reshape(nchunk, 128).T)


def _shard_x(x0):
    x0T = np.ascontiguousarray(np.asarray(x0, np.float32).transpose(0, 2, 1))
    out = []
    for c in range(NC):
        bb, kk = c // 4, c % 4
        lo = kk * T_OWN - HALO
        sl = np.zeros((D, T_EXT), np.float32)
        src_lo = max(lo, 0)
        sl[:, src_lo - lo:] = x0T[bb, :, src_lo:(kk + 1) * T_OWN]
        out.append(np.ascontiguousarray(
            sl.reshape(DC, 128, T_EXT).transpose(1, 0, 2)))
    return out


def _unshard_x(parts, width, col0):
    x = np.zeros((2, 8192, D), np.float32)
    for c in range(NC):
        bb, kk = c // 4, c % 4
        sl = np.asarray(parts[c], np.float32)[:, :, col0:col0 + T_OWN]
        x[bb, kk * T_OWN:(kk + 1) * T_OWN, :] = (
            sl.transpose(1, 0, 2).reshape(D, T_OWN).T)
    return x


def run_l0ffn_stage(xparts, W, col_off, inputs, counter=[0]):
    nc = build_l0ffn(W)
    wp = _prep_weights(inputs)
    in_maps = []
    for c in range(NC):
        in_maps.append(dict(
            x_in=np.ascontiguousarray(xparts[c][:, :, col_off:col_off + W]),
            w1=wp["w1"], w2=wp["w2"], rw=wp["rw0"], b1=wp["b1"], b2=wp["b2"]))
    res = run_bass_kernel_spmd(nc, in_maps, core_ids=list(range(NC)))
    return [np.asarray(res.results[c]["x_out"]) for c in range(NC)]


def _prep_weights(inputs):
    wp = {}
    wp["w1"] = _pm(inputs["L0_ffn_w1"], (DC, 128, 8, 512), (2, 1, 0, 3),
                   (8, 128, DC * 512))
    wp["w2"] = _pm(inputs["L0_ffn_w2"], (8, 4, 128, D), (0, 2, 1, 3),
                   (8, 128, 4 * D))
    wp["rw0"] = np.ascontiguousarray(np.asarray(inputs["L0_ffn_router_w"], np.float32))
    wp["b1"] = _fm_vec(inputs["L0_ffn_b1"], F0 // 128)
    wp["b2"] = _fm_vec(inputs["L0_ffn_b2"], DC)
    for nm, key in (("wq", "L0_attn_Wq"), ("wk", "L0_attn_Wk"),
                    ("wv", "L0_attn_Wv"), ("wo", "L0_attn_Wo")):
        wp[nm] = _pm(np.asarray(inputs[key], np.float32).T,
                     (DC, 128, D), (1, 0, 2), (1, 128, DC * D))
    wp["bq"] = _fm_vec(inputs["L0_attn_bq"], DC)
    wp["bk"] = _fm_vec(inputs["L0_attn_bk"], DC)
    wp["bv"] = np.asarray(inputs["L0_attn_bv"], np.float32).reshape(1, D)
    wp["bo"] = _fm_vec(inputs["L0_attn_bo"], DC)
    wp["rw1"] = _pm(np.asarray(inputs["L1_ffn_router_w"], np.float32).T,
                    (DC, 128, NE), (1, 0, 2), (1, 128, DC * NE))
    wp["rbias"] = np.ascontiguousarray(np.broadcast_to(
        np.asarray(inputs["L1_ffn_bias"], np.float32).reshape(1, NE), (128, NE)))
    wp["gw"] = _pm(inputs["L1_ffn_gate_w"], (NE, DC, 128, D), (0, 2, 1, 3),
                   (NE, 128, DC * D))
    wp["uw"] = _pm(inputs["L1_ffn_up_w"], (NE, DC, 128, D), (0, 2, 1, 3),
                   (NE, 128, DC * D))
    wp["dw"] = _pm(inputs["L1_ffn_down_w"], (NE, DC, 128, D), (0, 2, 1, 3),
                   (NE, 128, DC * D))
    wp["shg"] = _pm(inputs["L1_ffn_sh_gate_w"], (DC, 128, 2, D), (2, 1, 0, 3),
                    (2, 128, DC * D))
    wp["shu"] = _pm(inputs["L1_ffn_sh_up_w"], (DC, 128, 2, D), (2, 1, 0, 3),
                    (2, 128, DC * D))
    wp["shd"] = _pm(inputs["L1_ffn_sh_down_w"], (2, DC, 128, D), (0, 2, 1, 3),
                    (2, 128, DC * D))
    return wp


def kernel_multilaunch(inputs):
    x0 = np.asarray(inputs["L0_attn_hidden_state"], np.float32)
    xp = _shard_x(x0)
    x1 = run_l0ffn_stage(xp, T_EXT, 0, inputs)
    x2 = run_attn_stage(x1, inputs)
    x3 = run_l0ffn_stage(x2, W_ATT, 0, inputs)
    x4 = run_l1ffn_stage(x3, W_ATT, 0, inputs)
    x4m = [p[:, :, W_ATT - W_M:] for p in x4]
    x5 = run_mamba_stage(x4m, inputs)
    x6 = run_l1ffn_stage(x5, T_OWN, 0, inputs)
    return _unshard_x(x6, T_OWN, 0)


def build_attn_a():
    b = B()
    W = T_EXT
    b.reg_fconst(LN_EPS)
    x_in = b.inp("x_in", [128, DC, W])
    wq = b.inp("wq", [1, 128, DC * D])
    wk = b.inp("wk", [1, 128, DC * D])
    wv = b.inp("wv", [1, 128, DC * D])
    bq = b.inp("bq", [128, DC])
    bk = b.inp("bk", [128, DC])
    bv = b.inp("bv", [1, D])
    cosq = b.inp("cosq", [128, 4, W])
    sinq = b.inp("sinq", [128, 4, W])
    cosk = b.inp("cosk", [128, 4, W])
    sink = b.inp("sink", [128, 4, W])
    q_out = b.out("q_out", [128, DC, W_ATT], BF16)
    k_out = b.out("k_out", [128, DC, W], BF16)
    v_out = b.out("v_out", [128, W // 128, D], BF16)
    wqb = b.dram("wqb", [1, 128, DC * D], BF16)
    wkb = b.dram("wkb", [1, 128, DC * D], BF16)
    wvb = b.dram("wvb", [1, 128, DC * D], BF16)
    emit_wcast(b, [(wq, wqb), (wk, wkb), (wv, wvb)])
    with b.scope():
        emit_attn_a_stage(b, x_in, q_out, k_out, v_out, wqb, wkb, wvb,
                          bq, bk, bv, cosq, sinq, cosk, sink)
    return b.finish()


def emit_attn_a_stage(b, x_in, q_out, k_out, v_out, wqb, wkb, wvb,
                      bq, bk, bv, cosq, sinq, cosk, sink):
    nc = b.nc
    W = T_EXT
    QOFF = HALO - 128
    ones_row = _ones_row(b)
    ps_a = b.psum("ps_a", [128, 512])
    ps_b = b.psum("ps_b", [128, 512])
    ps_v = b.psum("ps_v", [128, 1024])

    scr = alloc_ln_scratch(b, W)
    emit_ln_stats(b, x_in, W, scr, ps_a, ps_b)

    bqs = b.sbuf("bqs", [128, DC]); b.load(bqs, bq)
    bks = b.sbuf("bks", [128, DC]); b.load(bks, bk)
    bvs = b.sbuf("bvs", [1, D]);    b.load(bvs, bv)
    b.barrier()

    x_s = b.sbuf("a_x", [128, DC, 512])
    xn_s = b.sbuf("a_xn", [128, DC, 512], BF16)
    tmp = b.sbuf("a_tmp", [128, 512])
    tmp2 = b.sbuf("a_tmp2", [128, 512])
    kraw = b.sbuf("a_kraw", [128, DC, 512])
    rope_o = b.sbuf("a_ro", [128, DC, 512], BF16)
    trig = [b.sbuf(f"a_trig{i}", [128, 512]) for i in range(2)]
    wcur = b.sbuf("a_wcur", [128, DC, D], BF16)
    v_bf = b.sbuf("a_vbf", [128, 1024], BF16)

    for (o, n) in _slices(W, 512):
        st_before = b.n["st"]
        b.w(nc.sync, "ve")
        b.w(nc.sync, "pe")
        ln_x = b.load(x_s[:, :, 0:n], x_in[:, :, o:o + n])
        b.w(nc.vector, "dl", ln_x)
        b.w(nc.vector, "pe")
        for c in range(DC):
            emit_ln_apply(b, x_s[:, c, 0:n], scr, o, n, xn_s[:, c, 0:n], tmp)
        ve_xn = b.n["ve"]

        for proj, wb, bias_s, cos_in, sin_in, out_hbm, q_off in (
                ("k", wkb, bks, cosk, sink, k_out, 0),
                ("q", wqb, bqs, cosq, sinq, q_out, QOFF)):
            if o + n <= q_off:
                continue
            b.w(nc.sync, "pe")
            lw = b.load(wcur, wb[0])
            b.w(nc.tensor, "dl", lw)
            for oc in range(DC):
                kk = oc % 2
                ps = ps_a if kk == 0 else ps_b
                b.w(nc.tensor, "ve", ve_xn)
                b.w(nc.tensor, "ve")
                for ic in range(DC):
                    mm = nc.tensor.matmul(ps[:, 0:n],
                                          wcur[:, ic, oc * 128:(oc + 1) * 128],
                                          xn_s[:, ic, 0:n],
                                          start=(ic == 0), stop=(ic == DC - 1))
                pe_k = b.i(mm, "pe")
                b.w(nc.vector, "pe", pe_k)
                b.i(nc.vector.tensor_scalar(kraw[:, oc, 0:n], ps[:, 0:n],
                                            bias_s[:, oc:oc + 1], None, ALU.add),
                    "ve")
            oo = o - q_off
            for c in range(4):
                b.w(nc.sync, "ve")
                lc = b.load(trig[0][:, 0:n], cos_in[:, c, o:o + n])
                ls = b.load(trig[1][:, 0:n], sin_in[:, c, o:o + n])
                b.w(nc.vector, "dl", ls)
                b.w(nc.vector, "st", st_before)
                nc.vector.tensor_mul(tmp[:, 0:n], kraw[:, c, 0:n], trig[0][:, 0:n])
                nc.vector.tensor_mul(tmp2[:, 0:n], kraw[:, c + 4, 0:n], trig[1][:, 0:n])
                b.i(nc.vector.tensor_sub(rope_o[:, c, 0:n], tmp[:, 0:n],
                                         tmp2[:, 0:n]), "ve")
                nc.vector.tensor_mul(tmp[:, 0:n], kraw[:, c, 0:n], trig[1][:, 0:n])
                nc.vector.tensor_mul(tmp2[:, 0:n], kraw[:, c + 4, 0:n], trig[0][:, 0:n])
                b.i(nc.vector.tensor_add(rope_o[:, c + 4, 0:n], tmp[:, 0:n],
                                         tmp2[:, 0:n]), "ve")
            b.w(nc.sync, "ve")
            b.store(out_hbm[:, :, oo:oo + n], rope_o[:, :, 0:n])

        b.w(nc.sync, "pe")
        lw = b.load(wcur, wvb[0])
        b.w(nc.tensor, "dl", lw)
        nt = (n + 127) // 128
        for t in range(nt):
            tn = min(128, n - t * 128)
            vt = (o + t * 128) // 128
            for ov in range(2):
                b.w(nc.tensor, "ve", ve_xn)
                b.w(nc.tensor, "ve")
                for ic in range(DC):
                    mm = nc.tensor.matmul(ps_v[0:tn, ov * 512:(ov + 1) * 512],
                                          xn_s[:, ic, t * 128:t * 128 + tn],
                                          wcur[:, ic, ov * 512:(ov + 1) * 512],
                                          start=(ic == 0), stop=False,
                                          skip_group_check=True)
                mm = nc.tensor.matmul(ps_v[0:tn, ov * 512:(ov + 1) * 512],
                                      ones_row[:, 0:tn],
                                      bvs[:, ov * 512:(ov + 1) * 512],
                                      start=False, stop=True, skip_group_check=True)
                pe_v = b.i(mm, "pe")
                b.w(nc.vector, "pe", pe_v)
                ve_v = b.i(nc.vector.tensor_copy(v_bf[0:tn, ov * 512:(ov + 1) * 512],
                                                 ps_v[0:tn, ov * 512:(ov + 1) * 512]),
                           "ve")
                b.w(nc.sync, "ve", ve_v)
                b.store(v_out[0:tn, vt, ov * 512:(ov + 1) * 512],
                        v_bf[0:tn, ov * 512:(ov + 1) * 512])


def build_attn_b():
    b = B()
    W = T_EXT
    NT = W_ATT // 128
    b.reg_fconst(1.0)
    x_in = b.inp("x_in", [128, DC, W])
    q_in = b.inp("q_in", [128, DC, W_ATT], BF16)
    k_in = b.inp("k_in", [128, DC, W], BF16)
    v_in = b.inp("v_in", [128, W // 128, D], BF16)
    wo = b.inp("wo", [1, 128, DC * D])
    bo = b.inp("bo", [128, DC])
    maskA = b.inp("maskA", [NT, 128, HALO])
    maskT = b.inp("maskT", [NT, 128, HALO])
    x_out = b.out("x_out", [128, DC, W_ATT])
    wob = b.dram("wob", [1, 128, DC * D], BF16)
    emit_wcast(b, [(wo, wob)])
    with b.scope():
        emit_attn_b_stage(b, x_in, q_in, k_in, v_in, x_out, wob, bo,
                          maskA, maskT)
    return b.finish()


def emit_attn_b_stage(b, x_in, q_in, k_in, v_in, x_out, wob, bo, maskA, maskT):
    nc = b.nc
    W = T_EXT
    NT = W_ATT // 128
    ident = _ident128(b)
    ones_row = _ones_row(b)
    ps_a = b.psum("ps_a", [128, 512])
    ps_A = b.psum("ps_A", [128, 1024])
    ps_T = b.psum("ps_T", [128, 1024])
    ps_o = b.psum("ps_o", [128, 256])

    bos = b.sbuf("bos", [128, DC]); b.load(bos, bo)
    wo_r = b.sbuf("b_wo", [128, DC, D], BF16)
    b.load(wo_r, wob[0])

    k_t = [b.sbuf(f"b_k{i}", [128, DC, HALO], BF16) for i in range(2)]
    v_t = [b.sbuf(f"b_v{i}", [128, 5, D], BF16) for i in range(2)]
    q_t = [b.sbuf(f"b_q{i}", [128, DC, 128], BF16) for i in range(2)]
    mA_t = [b.sbuf(f"b_mA{i}", [128, HALO]) for i in range(2)]
    mT_t = [b.sbuf(f"b_mT{i}", [128, HALO]) for i in range(2)]
    x_t = [b.sbuf(f"b_x{i}", [128, DC, 128]) for i in range(2)]
    pT = b.sbuf("b_pT", [128, HALO], BF16)
    pA_junk = b.sbuf("b_pA", [128, HALO], BF16)
    av = b.sbuf("b_av", [128, DC, 128], BF16)
    Zr = b.sbuf("b_Z", [128, 1])
    mneg = b.sbuf("b_mneg", [128, 1])
    cneg = b.sbuf("b_cneg", [128, 1])
    crow = b.sbuf("b_crow", [1, 128])
    t1 = b.sbuf("b_t1", [128, 128])
    b.barrier()

    last_use = [0, 0]
    for ti in range(NT):
        kk = ti % 2
        if last_use[kk]:
            b.w(nc.sync, "pe", last_use[kk])
            b.w(nc.sync, "ve")
            b.w(nc.sync, "st")
        kv0 = ti * 128
        ln1 = b.load(k_t[kk][:, :, :], k_in[:, :, kv0:kv0 + HALO])
        b.load(v_t[kk].rearrange("p f d -> p (f d)"),
               v_in[:, kv0 // 128:kv0 // 128 + 5, :].rearrange("p f d -> p (f d)"))
        b.load(q_t[kk][:, :, :], q_in[:, :, ti * 128:(ti + 1) * 128])
        b.load(mA_t[kk], maskA[ti])
        b.load(mT_t[kk], maskT[ti])
        ln2 = b.load(x_t[kk][:, :, :], x_in[:, :, HALO - 128 + ti * 128:
                                            HALO - 128 + (ti + 1) * 128])
        b.w(nc.tensor, "dl", ln2)
        for h in range(16):
            hc, hr = h // 2, (h % 2) * 64
            b.w(nc.tensor, "ve")
            b.w(nc.tensor, "sc")
            for j, (ko, kn) in enumerate(_slices(HALO, 512)):
                mm = nc.tensor.matmul(ps_A[:, ko:ko + kn],
                                      q_t[kk][hr:hr + 64, hc, :],
                                      k_t[kk][hr:hr + 64, hc, ko:ko + kn],
                                      start=True, stop=True)
            pe_s = b.i(mm, "pe")
            b.w(nc.vector, "pe", pe_s)
            nc.vector.tensor_add(ps_A[:, 0:HALO], ps_A[:, 0:HALO], mA_t[kk])
            b.i(nc.vector.reduce_max(mneg, ps_A[:, 0:HALO], axis=AX, negate=True),
                "ve")
            b.w(nc.scalar, "ve")
            sc_e = b.i(nc.scalar.activation(pA_junk, ps_A[:, 0:HALO], AF.Exp,
                                            bias=mneg, accum_out=Zr), "sc")
            b.w(nc.scalar, "sc", sc_e)
            sc_l = b.i(nc.scalar.activation(Zr, Zr, AF.Ln), "sc")
            b.w(nc.vector, "sc", sc_l)
            nc.vector.tensor_sub(cneg, mneg, Zr)
            ve_c = b.i(nc.vector.tensor_scalar(cneg, cneg, 1e37, None, ALU.min),
                       "ve")
            b.w(nc.tensor, "ve", ve_c)
            pe_t = b.i(nc.tensor.transpose(ps_o[0:1, 128:256], cneg, ident), "pe")
            b.w(nc.vector, "pe", pe_t)
            ve_cr = b.i(nc.vector.tensor_copy(crow, ps_o[0:1, 128:256]), "ve")
            b.w(nc.tensor, "ve", ve_cr)
            b.w(nc.tensor, "sc")
            for j in range(5):
                nc.tensor.matmul(ps_T[:, j * 128:(j + 1) * 128],
                                 k_t[kk][hr:hr + 64, hc, j * 128:(j + 1) * 128],
                                 q_t[kk][hr:hr + 64, hc, :],
                                 start=True, stop=False, skip_group_check=True)
                mm = nc.tensor.matmul(ps_T[:, j * 128:(j + 1) * 128],
                                      ones_row, crow, start=False, stop=True,
                                      skip_group_check=True)
            pe_st = b.i(mm, "pe")
            b.w(nc.vector, "pe", pe_st)
            ve_m = b.i(nc.vector.tensor_add(ps_T[:, 0:HALO], ps_T[:, 0:HALO],
                                            mT_t[kk]), "ve")
            b.w(nc.scalar, "ve", ve_m)
            sc_p = b.i(nc.scalar.activation(pT, ps_T[:, 0:HALO], AF.Exp), "sc")
            b.w(nc.tensor, "sc", sc_p)
            if h % 2 == 0:
                b.w(nc.tensor, "ve")
            for j in range(5):
                mm = nc.tensor.matmul(ps_o[hr:hr + 64, 0:128],
                                      v_t[kk][:, j, h * 64:(h + 1) * 64],
                                      pT[:, j * 128:(j + 1) * 128],
                                      start=(j == 0), stop=(j == 4),
                                      skip_group_check=True)
            pe_pv = b.i(mm, "pe")
            if h % 2 == 1:
                b.w(nc.vector, "pe", pe_pv)
                b.i(nc.vector.tensor_copy(av[:, hc, :], ps_o[:, 0:128]), "ve")
        ve_av = b.n["ve"]
        for oc in range(DC):
            b.w(nc.tensor, "ve", ve_av)
            b.w(nc.tensor, "ve")
            for fc in range(DC):
                mm = nc.tensor.matmul(ps_a[:, 0:128], wo_r[:, fc, oc * 128:(oc + 1) * 128],
                                      av[:, fc, :], start=(fc == 0), stop=(fc == DC - 1))
            pe_o2 = b.i(mm, "pe")
            b.w(nc.vector, "pe", pe_o2)
            nc.vector.tensor_scalar(t1, ps_a[:, 0:128], bos[:, oc:oc + 1], None,
                                    ALU.add)
            ve_o = b.i(nc.vector.tensor_add(x_t[kk][:, oc, :], x_t[kk][:, oc, :],
                                            t1), "ve")
            b.w(nc.sync, "ve", ve_o)
            b.store(x_out[:, oc, ti * 128:(ti + 1) * 128], x_t[kk][:, oc, :])
        last_use[kk] = b.n["pe"]


def _rope_tables(positions, scale_q):
    half = 512
    inv = 1.0 / (10000.0 ** (np.arange(half, dtype=np.float64) / half))
    ang = positions.astype(np.float64)[None, :] * inv[:, None]
    cos = np.cos(ang).astype(np.float32) * scale_q
    sin = np.sin(ang).astype(np.float32) * scale_q
    W = positions.shape[0]
    return (np.ascontiguousarray(cos.reshape(4, 128, W).transpose(1, 0, 2)),
            np.ascontiguousarray(sin.reshape(4, 128, W).transpose(1, 0, 2)))


def _attn_masks(core):
    NT = W_ATT // 128
    bb, kk = core // 4, core % 4
    t0g = kk * T_OWN
    mA = np.zeros((NT, 128, HALO), np.float32)
    mT = np.zeros((NT, 128, HALO), np.float32)
    for ti in range(NT):
        qg = t0g - 128 + ti * 128 + np.arange(128)
        kg = t0g - HALO + ti * 128 + np.arange(HALO)
        valid = ((kg[None, :] <= qg[:, None])
                 & (kg[None, :] >= qg[:, None] - 512)
                 & (kg[None, :] >= 0) & (qg[:, None] >= 0))
        mA[ti] = np.where(valid, 0.0, NEG)
        vT = valid.reshape(128, 5, 128)
        mT[ti] = np.where(vT.transpose(2, 1, 0).reshape(128, HALO)
                          [: , :], 0.0, NEG)
    return mA, mT


def run_attn_stage(xparts, inputs):
    nc_a = build_attn_a()
    nc_b = build_attn_b()
    wp = _prep_weights(inputs)
    im_a, masks = [], []
    for c in range(NC):
        kk = c % 4
        pos = np.maximum(kk * T_OWN - HALO + np.arange(T_EXT), 0)
        cq, sq = _rope_tables(pos, 0.125)
        ck, sk = _rope_tables(pos, 1.0)
        mA, mT = _attn_masks(c)
        masks.append((mA, mT))
        im_a.append(dict(x_in=np.ascontiguousarray(xparts[c]), wq=wp["wq"],
                         wk=wp["wk"], wv=wp["wv"], bq=wp["bq"], bk=wp["bk"],
                         bv=wp["bv"], cosq=cq, sinq=sq, cosk=ck, sink=sk))
    res_a = run_bass_kernel_spmd(nc_a, im_a, core_ids=list(range(NC)))
    im_b = []
    for c in range(NC):
        ra = res_a.results[c]
        mA, mT = masks[c]
        im_b.append(dict(x_in=np.ascontiguousarray(xparts[c]),
                         q_in=np.asarray(ra["q_out"]), k_in=np.asarray(ra["k_out"]),
                         v_in=np.asarray(ra["v_out"]), wo=wp["wo"], bo=wp["bo"],
                         maskA=mA, maskT=mT))
    res_b = run_bass_kernel_spmd(nc_b, im_b, core_ids=list(range(NC)))
    return [np.asarray(res_b.results[c]["x_out"]) for c in range(NC)]


def emit_ln_to_xn(b, x_src, W, xn, ps_a, ps_b, tag="lnx"):
    nc = b.nc
    with ExitStack() as lctx:
        def sb(name, shape, dtype=F32):
            return lctx.enter_context(
                b.rawnc.sbuf_tensor(f"{tag}{name}", shape, dtype)).ap()
        scr = dict(
            xcast=[sb(f"xc{i}", [128, 512], BF16) for i in range(2)],
            sq=[sb(f"sq{i}", [128, 512], BF16) for i in range(2)],
            xs=[sb(f"xs{i}", [128, 512], F32) for i in range(2)],
            alpha_r=sb("ar", [1, W]),
            beta_r=sb("br", [1, W]),
            alpha_b=sb("ab", [128, W]),
            beta_b=sb("bb", [128, W]),
        )
        emit_ln_stats(b, x_src, W, scr, ps_a, ps_b)
        tmp = sb("tmp", [128, 512])
        xs2 = sb("xs2", [128, 512])
        for (o, n) in _slices(W, 512):
            for c in range(DC):
                ln = b.load(xs2[:, 0:n], x_src[:, c, o:o + n])
                b.w(nc.vector, "dl", ln)
                ve = emit_ln_apply(b, xs2[:, 0:n], scr, o, n, xn[:, c, o:o + n], tmp)
                b.w(nc.sync, "ve", ve)
        b.barrier()


def build_l1ffn(W):
    b = B()
    b.reg_fconst(LN_EPS)
    x_in = b.inp("x_in", [128, DC, W])
    rw = b.inp("rw", [1, 128, DC * NE])
    rbias = b.inp("rbias", [128, NE])
    gw = b.inp("gw", [NE, 128, DC * D])
    uw = b.inp("uw", [NE, 128, DC * D])
    dw = b.inp("dw", [NE, 128, DC * D])
    shg = b.inp("shg", [2, 128, DC * D])
    shu = b.inp("shu", [2, 128, DC * D])
    shd = b.inp("shd", [2, 128, DC * D])
    x_out = b.out("x_out", [128, DC, W])
    rwb = b.dram("rwb", [1, 128, DC * NE], BF16)
    gwb = b.dram("gwb", [NE, 128, DC * D], BF16)
    uwb = b.dram("uwb", [NE, 128, DC * D], BF16)
    dwb = b.dram("dwb", [NE, 128, DC * D], BF16)
    shgb = b.dram("shgb", [2, 128, DC * D], BF16)
    shub = b.dram("shub", [2, 128, DC * D], BF16)
    shdb = b.dram("shdb", [2, 128, DC * D], BF16)
    emit_wcast(b, [(rw, rwb), (gw, gwb), (uw, uwb), (dw, dwb),
                   (shg, shgb), (shu, shub), (shd, shdb)])
    with b.scope():
        emit_l1ffn_stage(b, x_in, x_out, W, rwb, gwb, uwb, dwb,
                         shgb, shub, shdb, rbias, tag="l1f")
    return b.finish()


def emit_l1ffn_stage(b, x_in, x_out, W, rwb, gwb, uwb, dwb, shgb, shub, shdb,
                     rbias, tag="l1f"):
    nc = b.nc
    NT = W // 128
    ident = _ident128(b)
    ones_row = _ones_row(b)
    ps_h = [b.psum(f"ps_h{i}", [128, 512]) for i in range(2)]
    ps_u = [b.psum(f"ps_u{i}", [128, 512]) for i in range(2)]
    ps_y = [b.psum(f"ps_y{i}", [128, 512]) for i in range(2)]

    _stat_lhs(b)
    ones4 = b.const_sbuf("ones4", np.ones((128, 4), np.float32))
    ones16 = b.const_sbuf("ones16", np.ones((128, 16), np.float32))
    xn = b.sbuf("f_xn", [128, DC, W], BF16)
    emit_ln_to_xn(b, x_in, W, xn, ps_h[0], ps_u[0])

    rws = b.sbuf("f_rws", [128, DC, NE], BF16)
    b.load(rws, rwb[0])
    bias_b = b.sbuf("f_biasb", [128, NE])
    ln_b = b.load(bias_b, rbias)
    sc_s = b.sbuf("f_sc", [128, NE])
    s_s = b.sbuf("f_s", [128, NE])
    ms_s = b.sbuf("f_ms", [128, NE])
    t16 = b.sbuf("f_t16", [128, NE])
    ge16 = b.sbuf("f_ge16", [128, NE])
    cur16 = b.sbuf("f_cur", [128, NE])
    gsc = b.sbuf("f_gsc", [128, 4])
    t4 = b.sbuf("f_t4", [128, 4])
    ge4 = b.sbuf("f_ge4", [128, 4])
    gmask = b.sbuf("f_gm", [128, 4])
    m1n = b.sbuf("f_m1n", [128, 1])
    m2r = b.sbuf("f_m2", [128, 1])
    tsum = b.sbuf("f_tsum", [128, 1])
    comb = b.sbuf("f_comb", [128, NE])
    c_T = b.sbuf("f_cT", [16, W])
    c_dram = b.dram(f"f_cdram_{tag}", [NE, W])
    b.w(nc.vector, "dl")
    b.barrier()

    for ti in range(NT):
        t0 = ti * 128
        b.w(nc.tensor, "ve")
        b.w(nc.tensor, "sc")
        for ic in range(DC):
            mm = nc.tensor.matmul(ps_h[0][:, 0:NE], xn[:, ic, t0:t0 + 128],
                                  rws[:, ic, :], start=(ic == 0),
                                  stop=(ic == DC - 1))
        pe_r = b.i(mm, "pe")
        b.w(nc.scalar, "pe", pe_r)
        sc1 = b.i(nc.scalar.activation(sc_s, ps_h[0][:, 0:NE], AF.Sigmoid), "sc")
        b.w(nc.vector, "sc", sc1)
        nc.vector.tensor_add(s_s, sc_s, bias_b)
        for g in range(4):
            sg = s_s[:, g * 4:(g + 1) * 4]
            nc.vector.reduce_max(m1n, sg, axis=AX, negate=True)
            nc.vector.scalar_tensor_tensor(t4, ones4[:, 0:4], m1n, sg,
                                           ALU.mult, ALU.add)
            nc.vector.tensor_scalar(ge4, t4, 0.0, None, ALU.is_ge)
            nc.vector.scalar_tensor_tensor(t4, ge4, -1e30, sg, ALU.mult, ALU.add)
            nc.vector.reduce_max(m2r, t4, axis=AX)
            nc.vector.tensor_sub(gsc[:, g:g + 1], m2r, m1n)
        nc.vector.reduce_max(m1n, gsc, axis=AX, negate=True)
        nc.vector.scalar_tensor_tensor(t4, ones4[:, 0:4], m1n, gsc,
                                       ALU.mult, ALU.add)
        nc.vector.tensor_scalar(ge4, t4, 0.0, None, ALU.is_ge)
        nc.vector.scalar_tensor_tensor(t4, ge4, -1e30, gsc, ALU.mult, ALU.add)
        nc.vector.reduce_max(m2r, t4, axis=AX, negate=True)
        nc.vector.scalar_tensor_tensor(t4, ones4[:, 0:4], m2r, gsc,
                                       ALU.mult, ALU.add)
        nc.vector.tensor_scalar(gmask, t4, 0.0, None, ALU.is_ge)
        nc.vector.tensor_scalar(t16, s_s, 1.0, None, ALU.add)
        for g in range(4):
            nc.vector.tensor_scalar(ge16[:, g * 4:(g + 1) * 4],
                                    t16[:, g * 4:(g + 1) * 4],
                                    gmask[:, g:g + 1], None, ALU.mult)
        nc.vector.tensor_scalar(ms_s, ge16, -1.0, None, ALU.add)
        nc.vector.tensor_copy(cur16, ms_s)
        for it in range(3):
            nc.vector.reduce_max(m1n, cur16, axis=AX, negate=True)
            nc.vector.scalar_tensor_tensor(t16, ones16, m1n, cur16,
                                           ALU.mult, ALU.add)
            nc.vector.tensor_scalar(ge16, t16, 0.0, None, ALU.is_ge)
            nc.vector.scalar_tensor_tensor(cur16, ge16, -1e30, cur16,
                                           ALU.mult, ALU.add)
        nc.vector.reduce_max(m1n, cur16, axis=AX, negate=True)
        nc.vector.scalar_tensor_tensor(t16, ones16, m1n, ms_s, ALU.mult, ALU.add)
        nc.vector.tensor_scalar(ge16, t16, 0.0, None, ALU.is_ge)
        nc.vector.tensor_mul(t16, sc_s, ge16)
        nc.vector.reduce_sum(tsum, t16, axis=AX)
        nc.vector.tensor_scalar(tsum, tsum, 1e-20, None, ALU.add)
        nc.vector.reciprocal(tsum, tsum)
        ve_c = b.i(nc.vector.tensor_scalar(comb, t16, tsum, 2.5,
                                           ALU.mult, ALU.mult), "ve")
        b.w(nc.tensor, "ve", ve_c)
        pe_t = b.i(nc.tensor.transpose(ps_u[0][0:16, 0:128], comb, ident), "pe")
        b.w(nc.vector, "pe", pe_t)
        b.i(nc.vector.tensor_copy(c_T[:, t0:t0 + 128], ps_u[0][0:16, 0:128]), "ve")
    b.w(nc.sync, "ve")
    st_c = b.store(c_dram, c_T)
    nc.sync.wait_ge(b.sems["st"], st_c)
    b.barrier()

    NEX = NE + 2
    w_g = [b.sbuf(f"f_wg{i}", [128, DC, D], BF16) for i in range(2)]
    w_u = [b.sbuf(f"f_wu{i}", [128, DC, D], BF16) for i in range(2)]
    w_d = [b.sbuf(f"f_wd{i}", [128, DC, D], BF16) for i in range(2)]
    gbuf = b.sbuf("f_gbuf", [128, DC, 512], BF16)
    sh_b = b.sbuf("f_shb", [128, 512], BF16)
    yc = [b.sbuf(f"f_yc{i}", [128, 512]) for i in range(2)]
    c_row = b.sbuf("f_crow", [1, W])
    WH = (W + 1023) // 1024 * 512 if False else (W + 1) // 2
    c_b = b.sbuf("f_cb", [128, WH + 8])
    racc_t = [b.sbuf(f"f_racc{c}", [128, WH + 8]) for c in range(DC)]
    xs2 = b.sbuf("f_xs2", [128, 512])

    def load_expert(e, kb):
        if e >= NEX:
            return 0
        if e >= NE:
            h = e - NE
            b.load(w_g[kb], shgb[h])
            b.load(w_u[kb], shub[h])
            lw = b.load(w_d[kb], shdb[h])
        else:
            b.load(w_g[kb], gwb[e])
            b.load(w_u[kb], uwb[e])
            lw = b.load(w_d[kb], dwb[e])
        return lw

    last_use_w = [0, 0]
    dl_w = [0, 0]
    dl_w[0] = load_expert(0, 0)
    for half in range(2):
        h0 = half * WH
        hw = min(WH, W - h0)
        for c in range(DC):
            b.i(nc.gpsimd.memset(racc_t[c][:, 0:hw], 0.0), "po")
        for e in range(NEX):
            kb = e % 2
            shared = (e >= NE)
            en = e + 1 if e + 1 < NEX else (0 if half == 0 else NEX)
            if en < NEX and (half == 0 or e + 1 < NEX):
                if last_use_w[1 - kb]:
                    b.w(nc.sync, "pe", last_use_w[1 - kb])
                dl_w[1 - kb] = load_expert(en if e + 1 < NEX else 0, 1 - kb)
            if not shared:
                b.w(nc.sync, "ve")
                lc = b.load(c_row[:, 0:hw], c_dram[e].unsqueeze(0)[:, h0:h0 + hw])
                b.w(nc.tensor, "dl", lc)
                b.w(nc.tensor, "ve")
                b.w(nc.tensor, "po")
                for (o, n) in _slices(hw, 512):
                    mm = nc.tensor.matmul(ps_y[0][:, 0:n], ones_row,
                                          c_row[:, o:o + n], start=True, stop=True)
                    pe_c = b.i(mm, "pe")
                    b.w(nc.vector, "pe", pe_c)
                    ve_cb = b.i(nc.vector.tensor_copy(c_b[:, o:o + n],
                                                      ps_y[0][:, 0:n]), "ve")
                    b.w(nc.tensor, "ve", ve_cb)
            b.w(nc.tensor, "dl", dl_w[kb])
            sc_par = {0: 0, 1: 0}
            ve_par = {0: 0, 1: 0}
            vey_par = {0: 0, 1: 0}
            for (o, n) in _slices(hw, 512):
                oo = h0 + o
                ve_g = {}
                for f in range(DC):
                    p = f % 2
                    if sc_par[p]:
                        b.w(nc.tensor, "sc", sc_par[p])
                    if ve_par[p]:
                        b.w(nc.tensor, "ve", ve_par[p])
                    for ic in range(DC):
                        nc.tensor.matmul(ps_h[p][:, 0:n],
                                         w_g[kb][:, ic, f * 128:(f + 1) * 128],
                                         xn[:, ic, oo:oo + n], start=(ic == 0),
                                         stop=(ic == DC - 1))
                    for ic in range(DC):
                        mm = nc.tensor.matmul(ps_u[p][:, 0:n],
                                              w_u[kb][:, ic, f * 128:(f + 1) * 128],
                                              xn[:, ic, oo:oo + n], start=(ic == 0),
                                              stop=(ic == DC - 1))
                    pe_gu = b.i(mm, "pe")
                    last_use_w[kb] = pe_gu
                    b.w(nc.scalar, "pe", pe_gu)
                    sc_par[p] = b.i(nc.scalar.activation(sh_b[:, 0:n],
                                                         ps_h[p][:, 0:n],
                                                         AF.Silu), "sc")
                    b.w(nc.vector, "sc", sc_par[p])
                    ve_par[p] = b.i(nc.vector.tensor_mul(gbuf[:, f, 0:n],
                                                         sh_b[:, 0:n],
                                                         ps_u[p][:, 0:n]), "ve")
                    ve_g[f] = ve_par[p]
                for oc in range(DC):
                    p = oc % 2
                    b.w(nc.tensor, "ve", max(ve_g.values()))
                    if vey_par[p]:
                        b.w(nc.tensor, "ve", vey_par[p])
                    for f in range(DC):
                        mm = nc.tensor.matmul(ps_y[p][:, 0:n],
                                              w_d[kb][:, f, oc * 128:(oc + 1) * 128],
                                              gbuf[:, f, 0:n], start=(f == 0),
                                              stop=(f == DC - 1),
                                              skip_group_check=True)
                    pe_y = b.i(mm, "pe")
                    last_use_w[kb] = pe_y
                    b.w(nc.vector, "pe", pe_y)
                    b.w(nc.vector, "po")
                    if shared:
                        vey_par[p] = b.i(nc.vector.tensor_copy(yc[p][:, 0:n],
                                                               ps_y[p][:, 0:n]),
                                         "ve")
                    else:
                        vey_par[p] = b.i(nc.vector.tensor_mul(yc[p][:, 0:n],
                                                              ps_y[p][:, 0:n],
                                                              c_b[:, o:o + n]),
                                         "ve")
                    b.w(nc.gpsimd, "ve", vey_par[p])
                    b.i(nc.gpsimd.tensor_add(racc_t[oc][:, o:o + n],
                                             racc_t[oc][:, o:o + n],
                                             yc[p][:, 0:n]), "po")
                pe_dn = b.n["pe"]
                ve_par = {0: b.n["ve"], 1: b.n["ve"]}
        b.w(nc.gpsimd, "ve")
        b.barrier()
        for (o, n) in _slices(hw, 512):
            for c in range(DC):
                b.w(nc.sync, "ve")
                ln = b.load(xs2[:, 0:n], x_in[:, c, h0 + o:h0 + o + n])
                b.w(nc.vector, "dl", ln)
                b.w(nc.vector, "po")
                ve_o = b.i(nc.vector.scalar_tensor_tensor(
                    xs2[:, 0:n], racc_t[c][:, o:o + n], 0.5, xs2[:, 0:n],
                    ALU.mult, ALU.add), "ve")
                b.w(nc.sync, "ve", ve_o)
                b.store(x_out[:, c, h0 + o:h0 + o + n], xs2[:, 0:n])
        b.barrier()


def run_l1ffn_stage(xparts, W, col_off, inputs):
    nc = build_l1ffn(W)
    wp = _prep_weights(inputs)
    im = []
    for c in range(NC):
        im.append(dict(x_in=np.ascontiguousarray(xparts[c][:, :, col_off:col_off + W]),
                       rw=wp["rw1"], rbias=wp["rbias"], gw=wp["gw"], uw=wp["uw"],
                       dw=wp["dw"], shg=wp["shg"], shu=wp["shu"], shd=wp["shd"]))
    res = run_bass_kernel_spmd(nc, im, core_ids=list(range(NC)))
    return [np.asarray(res.results[c]["x_out"]) for c in range(NC)]


W_M = 2051
NCI = D_INNER // 128
TB = 512


def build_mamba_a():
    b = B()
    W = W_M
    b.reg_fconst(LN_EPS)
    b.reg_fconst(RMS_EPS)
    b.reg_fconst(1.0)
    x_in = b.inp("x_in", [128, DC, W])
    g_in = b.inp("g_in", [128, DC])
    w_in = b.inp("w_in", [8, 128, DC * 4 * 128])
    conv_w = b.inp("conv_w", [128, NCI, 4])
    conv_b = b.inp("conv_b", [128, NCI])
    w_x = b.inp("w_x", [1, 128, NCI * 33])
    w_dt = b.inp("w_dt", [1, D_INNER])
    b_dt = b.inp("b_dt", [128, NCI])
    a_sb = b.inp("a_sb", [128, NCI * D_STATE])
    hm3 = b.inp("hm3", [128, 4])
    z_out = b.out("z_out", [128, NCI, T_OWN])
    xc_out = b.out("xc_out", [128, NCI, T_OWN])
    dt_out = b.out("dt_out", [128, NCI, T_OWN])
    smy_out = b.out("smy_out", [128, 512])
    par_out = b.out("par_out", [33, T_OWN])
    wib = b.dram("wib", [8, 128, DC * 4 * 128], BF16)
    wxb = b.dram("wxb", [1, 128, NCI * 33], BF16)
    emit_wcast(b, [(w_in, wib), (w_x, wxb)])
    with b.scope():
        emit_mamba_a_stage(b, x_in, z_out, xc_out, dt_out, smy_out, par_out,
                           wib, wxb, g_in, conv_w, conv_b, w_dt, b_dt, a_sb, hm3)
    return b.finish()


def emit_mamba_a_stage(b, x_in, z_out, xc_out, dt_out, smy_out, par_out,
                       wib, wxb, g_in, conv_w, conv_b, w_dt, b_dt, a_sb, hm3):
    nc = b.nc
    W = W_M
    import os as _os
    dbg = b.out("d_xs", [128, 2 * W_M]) if _os.environ.get("KDEBUG_DUMP") else None
    pdram = b.dram("pdram_ma", [33, T_OWN])
    xc_dram = b.dram("xc_dram_ma", [128, NCI, T_OWN])
    dt_dram = b.dram("dt_dram_ma", [128, NCI, T_OWN])
    ident = _ident128(b)
    ones_row = _ones_row(b)
    _stat_lhs(b)
    ps_a = b.psum("ps_a", [128, 512])
    ps_b = b.psum("ps_b", [128, 512])
    ps_bc = b.psum("ps_bc", [128, 1024])

    smy = b.sbuf("m_smy", [128, 512])
    a_s = b.sbuf("m_as", [128, NCI * D_STATE])
    ln_a = b.load(a_s, a_sb)
    b.barrier()
    b.w(nc.scalar, "dl", ln_a)
    b.i(nc.scalar.activation(a_s[:, :], a_s[:, :], AF.Exp), "sc")
    b.w(nc.vector, "sc")
    b.i(nc.vector.tensor_scalar(a_s[:, :], a_s[:, :], -1.0, None, ALU.mult), "ve")
    b.barrier()

    with ExitStack() as mctx:
        def sb(name, shape, dtype=F32):
            return mctx.enter_context(b.rawnc.sbuf_tensor(name, shape, dtype)).ap()

        xn = sb("m_xn", [128, DC, W], BF16)
        g_s = sb("m_g", [128, DC])
        b.load(g_s, g_in)
        emit_ln_rms_to_xn(b, x_in, W, xn, g_s, ps_a, ps_b)

        w_r = sb("m_wr", [128, DC, 4, 128], BF16)
        xs_c = sb("m_xsc", [128, W])
        xc_c = sb("m_xcc", [128, T_OWN])
        cw_s = sb("m_cw", [128, NCI, 4])
        cb_s = sb("m_cb", [128, NCI])
        bdt_s = sb("m_bdt", [128, NCI])
        hm_s = sb("m_hm", [128, 4])
        t1b = sb("m_t1", [128, W])
        t2b = sb("m_t2", [128, W])
        zstg = sb("m_zstg", [128, 512])
        b.load(cw_s, conv_w)
        b.load(cb_s, conv_b)
        b.load(bdt_s, b_dt)
        b.load(hm_s, hm3)
        b.barrier()

        for zg in range(4):
            b.w(nc.sync, "pe")
            lw = b.load(w_r, wib[4 + zg])
            b.w(nc.tensor, "dl", lw)
            for j in range(4):
                zc = zg * 4 + j
                for (o, n) in _slices(T_OWN, 512):
                    b.w(nc.tensor, "ve")
                    for ic in range(DC):
                        mm = nc.tensor.matmul(ps_a[:, 0:n], w_r[:, ic, j, :],
                                              xn[:, ic, 3 + o:3 + o + n],
                                              start=(ic == 0), stop=(ic == DC - 1))
                    pe_z = b.i(mm, "pe")
                    b.w(nc.vector, "pe", pe_z)
                    b.w(nc.vector, "st")
                    ve_z = b.i(nc.vector.tensor_copy(zstg[:, 0:n], ps_a[:, 0:n]),
                               "ve")
                    b.w(nc.sync, "ve", ve_z)
                    b.store(z_out[:, zc, o:o + n], zstg[:, 0:n])

        for ci in range(NCI):
            if ci % 4 == 0:
                b.w(nc.sync, "pe")
                lw = b.load(w_r, wib[ci // 4])
                b.w(nc.tensor, "dl", lw)
            for (o, n) in _slices(W, 512):
                b.w(nc.tensor, "ve")
                for ic in range(DC):
                    mm = nc.tensor.matmul(ps_a[:, 0:n], w_r[:, ic, ci % 4, :],
                                          xn[:, ic, o:o + n],
                                          start=(ic == 0), stop=(ic == DC - 1))
                pe_x = b.i(mm, "pe")
                b.w(nc.vector, "pe", pe_x)
                b.i(nc.vector.tensor_copy(xs_c[:, o:o + n], ps_a[:, 0:n]), "ve")
            nc.vector.tensor_mul(xs_c[:, 0:3], xs_c[:, 0:3], hm_s[:, 0:3])
            if dbg is not None and ci == 0:
                ve_dbg = b.i(nc.vector.tensor_copy(t1b[:, 0:W], xs_c[:, 0:W]), "ve")
                b.w(nc.sync, "ve", ve_dbg)
                b.store(dbg[:, 0:W], t1b[:, 0:W])
                b.w(nc.vector, "st")
            nc.vector.tensor_scalar(t1b[:, 0:T_OWN], xs_c[:, 0:T_OWN],
                                    cw_s[:, ci, 0:1], None, ALU.mult)
            nc.vector.scalar_tensor_tensor(t2b[:, 0:T_OWN], xs_c[:, 1:1 + T_OWN],
                                           cw_s[:, ci, 1:2], t1b[:, 0:T_OWN],
                                           ALU.mult, ALU.add)
            nc.vector.scalar_tensor_tensor(t1b[:, 0:T_OWN], xs_c[:, 2:2 + T_OWN],
                                           cw_s[:, ci, 2:3], t2b[:, 0:T_OWN],
                                           ALU.mult, ALU.add)
            ve_cv = b.i(nc.vector.scalar_tensor_tensor(
                t2b[:, 0:T_OWN], xs_c[:, 3:3 + T_OWN], cw_s[:, ci, 3:4],
                t1b[:, 0:T_OWN], ALU.mult, ALU.add), "ve")
            if dbg is not None and ci == 0:
                b.w(nc.sync, "ve", ve_cv)
                b.store(dbg[:, W:W + T_OWN], t2b[:, 0:T_OWN])
                b.w(nc.vector, "st")
            b.w(nc.scalar, "ve", ve_cv)
            b.w(nc.scalar, "st")
            sc_cv = b.i(nc.scalar.activation(xc_c[:, :], t2b[:, 0:T_OWN], AF.Silu,
                                             bias=cb_s[:, ci:ci + 1]), "sc")
            b.w(nc.sync, "sc", sc_cv)
            b.store(xc_out[:, ci, :], xc_c)
            b.store(xc_dram[:, ci, :], xc_c)

        wx_r = sb("m_wx", [128, NCI, 33], BF16)
        b.load(wx_r, wxb[0])
        xcb = sb("m_xcb", [128, 512], BF16)
        xcf = sb("m_xcf", [128, 512])
        params = sb("m_params", [33, T_OWN])
        b.barrier()
        for (o, n) in _slices(T_OWN, 512):
            for ci in range(NCI):
                b.w(nc.sync, "ve")
                lxx = b.load(xcf[:, 0:n], xc_dram[:, ci, o:o + n])
                b.w(nc.vector, "dl", lxx)
                ve_c = b.i(nc.vector.tensor_copy(xcb[:, 0:n], xcf[:, 0:n]), "ve")
                b.w(nc.tensor, "ve", ve_c)
                mm = nc.tensor.matmul(ps_b[0:33, 0:n], wx_r[:, ci, :], xcb[:, 0:n],
                                      start=(ci == 0), stop=(ci == NCI - 1))
            pe_p = b.i(mm, "pe")
            b.w(nc.vector, "pe", pe_p)
            ve_p = b.i(nc.vector.tensor_copy(params[:, o:o + n], ps_b[0:33, 0:n]),
                       "ve")
            b.w(nc.tensor, "ve", ve_p)
        b.w(nc.sync, "ve")
        st_p = b.store(pdram, params)
        b.store(par_out, params)
        nc.sync.wait_ge(b.sems["st"], st_p)
        b.barrier()

        wdt_s = sb("m_wdt", [1, D_INNER])
        b.load(wdt_s, w_dt)
        logdt_r = sb("m_ldt", [1, T_OWN])
        b.load(logdt_r, pdram[32].unsqueeze(0))
        dt_c = sb("m_dtc", [128, T_OWN])
        sp1 = sb("m_sp1", [128, T_OWN])
        sp2 = sb("m_sp2", [128, T_OWN])
        dtsum = sb("m_dtsum", [128, NCI])
        b.barrier()
        for ci in range(NCI):
            b.w(nc.tensor, "sc")
            for (o, n) in _slices(T_OWN, 512):
                mm = nc.tensor.matmul(ps_a[:, 0:n],
                                      wdt_s[:, ci * 128:(ci + 1) * 128],
                                      logdt_r[:, o:o + n], start=True, stop=True)
                pe_d = b.i(mm, "pe")
                b.w(nc.scalar, "pe", pe_d)
                nc.scalar.activation(sp1[:, o:o + n], ps_a[:, 0:n], AF.Abs,
                                     bias=bdt_s[:, ci:ci + 1])
                nc.scalar.activation(sp1[:, o:o + n], sp1[:, o:o + n], AF.Exp,
                                     scale=-1.0)
                nc.scalar.activation(sp1[:, o:o + n], sp1[:, o:o + n], AF.Ln,
                                     bias=1.0)
                sc_d = b.i(nc.scalar.activation(sp2[:, o:o + n], ps_a[:, 0:n],
                                                AF.Relu, bias=bdt_s[:, ci:ci + 1]),
                           "sc")
                b.w(nc.tensor, "sc", sc_d)
            b.w(nc.vector, "sc")
            b.w(nc.vector, "st")
            ve_d = b.i(nc.vector.tensor_add(dt_c[:, :], sp1[:, 0:T_OWN],
                                            sp2[:, 0:T_OWN]), "ve")
            nc.vector.reduce_sum(dtsum[:, ci:ci + 1], dt_c[:, :], axis=AX)
            b.w(nc.sync, "ve", ve_d)
            b.store(dt_out[:, ci, :], dt_c)
            b.store(dt_dram[:, ci, :], dt_c)
        b.barrier()
        for ci in range(NCI):
            for s in range(D_STATE):
                col = ci * D_STATE + s
                b.i(nc.scalar.activation(smy[:, 256 + col:257 + col],
                                         dtsum[:, ci:ci + 1], AF.Exp,
                                         scale=a_s[:, col:col + 1]), "sc")
        b.barrier()

    sst = scan_setup(b, init_hin=None)
    for tb in range(T_OWN // TB):
        scan_tblock(b, sst, tb, dt_dram, xc_dram, pdram, a_s, ps_bc, yacc=None)
    b.barrier()
    b.i(nc.vector.tensor_copy(smy[:, 0:256], sst["carry"][:, :]), "ve")
    b.w(nc.sync, "ve")
    b.store(smy_out, smy)
    if dbg is not None:
        b.store(dbg[:, 0:NCI * D_STATE], a_s)
        ve_dd = b.i(nc.vector.tensor_copy(sst["dt_tb"][:, 0, :],
                                          sst["dt_tb"][:, 0, :]), "ve")
        b.w(nc.sync, "ve", ve_dd)


def emit_ln_rms_to_xn(b, x_src, W, xn, g_s, ps_a, ps_b, tag="lr"):
    nc = b.nc
    with ExitStack() as lctx:
        def sb(name, shape, dtype=F32):
            return lctx.enter_context(
                b.rawnc.sbuf_tensor(f"{tag}{name}", shape, dtype)).ap()
        scr = dict(
            xcast=[sb(f"xc{i}", [128, 512], BF16) for i in range(2)],
            sq=[sb(f"sq{i}", [128, 512], BF16) for i in range(2)],
            xs=[sb(f"xs{i}", [128, 512], F32) for i in range(2)],
            alpha_r=sb("ar", [1, W]),
            beta_r=sb("br", [1, W]),
            alpha_b=sb("ab", [128, W]),
            beta_b=sb("bb", [128, W]),
        )
        emit_ln_stats(b, x_src, W, scr, ps_a, ps_b)
        stat_lhs = _stat_lhs(b)
        tmp = sb("tmp", [128, 512])
        xs2 = sb("xs2", [128, 512])
        t1c = sb("t1c", [128, 512], BF16)
        sqc = sb("sqc", [128, 512], BF16)
        rms_r = sb("rmsr", [1, W])
        for (o, n) in _slices(W, 512):
            for c in range(DC):
                ln = b.load(xs2[:, 0:n], x_src[:, c, o:o + n])
                b.w(nc.vector, "dl", ln)
                b.w(nc.vector, "sc")
                nc.vector.tensor_mul(tmp[:, 0:n], xs2[:, 0:n],
                                     scr["alpha_b"][:, o:o + n])
                ve_t = b.i(nc.vector.tensor_add(t1c[:, 0:n], tmp[:, 0:n],
                                                scr["beta_b"][:, o:o + n]), "ve")
                b.w(nc.scalar, "ve", ve_t)
                sc_q = b.i(nc.scalar.activation(sqc[:, 0:n], t1c[:, 0:n],
                                                AF.Square), "sc")
                b.w(nc.tensor, "sc", sc_q)
                b.w(nc.sync, "ve", ve_t)
                mm = nc.tensor.matmul(ps_a[0:1, 0:n], stat_lhs[:, 0:1],
                                      sqc[:, 0:n], start=(c == 0),
                                      stop=(c == DC - 1), skip_group_check=True)
            pe_m = b.i(mm, "pe")
            b.w(nc.vector, "pe", pe_m)
            ve_r = b.i(nc.vector.tensor_copy(rms_r[:, o:o + n], ps_a[0:1, 0:n]),
                       "ve")
            b.w(nc.tensor, "ve", ve_r)
        b.barrier()
        sc_rm = b.i(nc.scalar.activation(rms_r[:, 0:W], rms_r[:, 0:W], AF.Sqrt,
                                         bias=RMS_EPS), "sc")
        b.w(nc.vector, "sc", sc_rm)
        nc.vector.reciprocal(rms_r[:, 0:W], rms_r[:, 0:W])
        nc.vector.tensor_mul(scr["alpha_r"][:, 0:W], scr["alpha_r"][:, 0:W],
                             rms_r[:, 0:W])
        b.i(nc.vector.tensor_mul(scr["beta_r"][:, 0:W], scr["beta_r"][:, 0:W],
                                 rms_r[:, 0:W]), "ve")
        b.barrier()
        ones_row = _ones_row(b)
        for (o, n) in _slices(W, 512):
            mm = b.i(nc.tensor.matmul(ps_a[:, 0:n], ones_row,
                                      scr["alpha_r"][:, o:o + n], start=True,
                                      stop=True), "pe")
            b.w(nc.vector, "pe", mm)
            ve = b.i(nc.vector.tensor_copy(scr["alpha_b"][:, o:o + n],
                                           ps_a[:, 0:n]), "ve")
            b.w(nc.tensor, "ve", ve)
            mm = b.i(nc.tensor.matmul(ps_a[:, 0:n], ones_row,
                                      scr["beta_r"][:, o:o + n], start=True,
                                      stop=True), "pe")
            b.w(nc.vector, "pe", mm)
            ve = b.i(nc.vector.tensor_copy(scr["beta_b"][:, o:o + n],
                                           ps_a[:, 0:n]), "ve")
            b.w(nc.tensor, "ve", ve)
        b.barrier()
        for (o, n) in _slices(W, 512):
            for c in range(DC):
                ln = b.load(xs2[:, 0:n], x_src[:, c, o:o + n])
                b.w(nc.vector, "dl", ln)
                nc.vector.tensor_mul(tmp[:, 0:n], xs2[:, 0:n],
                                     scr["alpha_b"][:, o:o + n])
                nc.vector.tensor_add(tmp[:, 0:n], tmp[:, 0:n],
                                     scr["beta_b"][:, o:o + n])
                ve = b.i(nc.vector.tensor_scalar(xn[:, c, o:o + n], tmp[:, 0:n],
                                                 g_s[:, c:c + 1], None, ALU.mult),
                         "ve")
                b.w(nc.sync, "ve", ve)
        b.barrier()


def scan_setup(b, init_hin=None):
    nc = b.nc
    st = dict(
        dt_tb=b.sbuf("s_dt", [128, NCI, TB]),
        xc_tb=b.sbuf("s_xc", [128, NCI, TB]),
        ab=[b.sbuf(f"s_ab{i}", [128, TB]) for i in range(2)],
        bu=b.sbuf("s_bu", [128, TB]),
        h=b.sbuf("s_h", [128, TB]),
        hC=[b.sbuf(f"s_hc{i}", [128, TB]) for i in range(2)],
        carry=b.sbuf("s_carry", [128, NCI * D_STATE]),
        b_row=b.sbuf("s_brow", [1, TB]),
        c_row=b.sbuf("s_crow", [1, TB]),
        ve_scan_par={0: 0, 1: 0},
        po_par={0: 0, 1: 0},
    )
    if init_hin is None:
        b.i(nc.vector.memset(st["carry"][:, :], 0.0), "ve")
    else:
        b.i(nc.vector.tensor_copy(st["carry"][:, :], init_hin), "ve")
    b.barrier()
    return st


def scan_tblock(b, st, tb, dt_hbm, xc_hbm, pdram, a_s, ps_bc, yacc=None):
    nc = b.nc
    o = tb * TB
    b.w(nc.sync, "ve")
    b.w(nc.sync, "sc")
    b.w(nc.sync, "st")
    ld = b.load(st["dt_tb"][:, :, :], dt_hbm[:, :, o:o + TB])
    lx = b.load(st["xc_tb"][:, :, :], xc_hbm[:, :, o:o + TB])
    b.w(nc.vector, "dl", lx)
    b.w(nc.scalar, "dl", lx)
    if yacc is not None:
        b.w(nc.gpsimd, "ve")
        for ci in range(NCI):
            b.i(nc.gpsimd.memset(yacc[:, ci, :], 0.0), "po")
    for ci in range(NCI):
        b.i(nc.vector.tensor_mul(st["xc_tb"][:, ci, :], st["dt_tb"][:, ci, :],
                                 st["xc_tb"][:, ci, :]), "ve")
    ve_scan_par = st["ve_scan_par"]
    po_par = st["po_par"]
    for s in range(D_STATE):
        b.w(nc.sync, "pe")
        lb = b.load(st["b_row"], pdram[s].unsqueeze(0)[:, o:o + TB])
        lc2 = b.load(st["c_row"], pdram[D_STATE + s].unsqueeze(0)[:, o:o + TB])
        b.w(nc.tensor, "dl", lc2)
        b.w(nc.tensor, "ve")
        mmB = b.i(nc.tensor.matmul(ps_bc[:, 0:TB], _ones_row(b), st["b_row"],
                                   start=True, stop=True), "pe")
        if yacc is not None:
            mmB = b.i(nc.tensor.matmul(ps_bc[:, 512:512 + TB], _ones_row(b),
                                       st["c_row"], start=True, stop=True), "pe")
        for ci in range(NCI):
            col = ci * D_STATE + s
            p = ci % 2
            if ve_scan_par[p]:
                b.w(nc.scalar, "ve", ve_scan_par[p])
            sc_a = b.i(nc.scalar.activation(st["ab"][p][:, :],
                                            st["dt_tb"][:, ci, :],
                                            AF.Exp, scale=a_s[:, col:col + 1]),
                       "sc")
            b.w(nc.vector, "pe", mmB)
            b.w(nc.vector, "sc", sc_a)
            nc.vector.tensor_mul(st["bu"][:, :], st["xc_tb"][:, ci, :],
                                 ps_bc[:, 0:TB])
            ve_sc = b.i(nc.vector.tensor_tensor_scan(
                st["h"][:, :], st["ab"][p][:, :], st["bu"][:, :],
                st["carry"][:, col:col + 1], ALU.mult, ALU.add), "ve")
            ve_scan_par[p] = ve_sc
            nc.vector.tensor_copy(st["carry"][:, col:col + 1],
                                  st["h"][:, TB - 1:TB])
            if yacc is not None:
                if po_par[p]:
                    b.w(nc.vector, "po", po_par[p])
                ve_hc = b.i(nc.vector.tensor_mul(st["hC"][p][:, :], st["h"][:, :],
                                                 ps_bc[:, 512:512 + TB]), "ve")
                b.w(nc.gpsimd, "ve", ve_hc)
                po_par[p] = b.i(nc.gpsimd.tensor_add(yacc[:, ci, :],
                                                     yacc[:, ci, :],
                                                     st["hC"][p][:, :]), "po")


def build_mamba_b():
    b = B()
    W = W_M
    b.reg_fconst(LN_EPS)
    x_in = b.inp("x_in", [128, DC, W])
    z_in = b.inp("z_in", [128, NCI, T_OWN])
    xc_in = b.inp("xc_in", [128, NCI, T_OWN])
    dt_in = b.inp("dt_in", [128, NCI, T_OWN])
    par_in = b.inp("par_in", [33, T_OWN])
    smy_prev = b.inp("smy_prev", [128, 3 * 512])
    a_sb = b.inp("a_sb", [128, NCI * D_STATE])
    d_in = b.inp("d_in", [128, NCI])
    w_out = b.inp("w_out", [1, 128, NCI * D])
    x_out = b.out("x_out", [128, DC, T_OWN])
    wob2 = b.dram("wob2", [1, 128, NCI * D], BF16)
    emit_wcast(b, [(w_out, wob2)])
    with b.scope():
        emit_mamba_b_stage(b, x_in, z_in, xc_in, dt_in, par_in, smy_prev,
                           a_sb, d_in, wob2, x_out)
    return b.finish()


def emit_mamba_b_stage(b, x_in, z_in, xc_in, dt_in, par_in, smy_prev,
                       a_sb, d_in, wob2, x_out):
    nc = b.nc
    W = W_M
    _stat_lhs(b); _ones_row(b); _ident128(b)
    ps_a = b.psum("ps_a", [128, 512])
    ps_b = b.psum("ps_b", [128, 512])
    ps_bc = b.psum("ps_bc", [128, 1024])

    alpha_b = b.sbuf("mb_ab", [128, T_OWN])
    beta_b = b.sbuf("mb_bb", [128, T_OWN])
    scr = dict(alpha_b=alpha_b, beta_b=beta_b)
    x_owned = x_in[:, :, 3:3 + T_OWN]
    with ExitStack() as lctx:
        def _lsb(name, shape, dtype=F32):
            return lctx.enter_context(b.rawnc.sbuf_tensor(name, shape, dtype)).ap()
        scr_full = dict(
            xcast=[_lsb(f"mbxc{i}", [128, 512], BF16) for i in range(2)],
            sq=[_lsb(f"mbsq{i}", [128, 512], BF16) for i in range(2)],
            xs=[_lsb(f"mbxs{i}", [128, 512], F32) for i in range(2)],
            alpha_r=_lsb("mbar", [1, T_OWN]),
            beta_r=_lsb("mbbr", [1, T_OWN]),
            alpha_b=alpha_b, beta_b=beta_b)
        emit_ln_stats(b, x_owned, T_OWN, scr_full, ps_a, ps_b)

    a_s = b.sbuf("mb_as", [128, NCI * D_STATE])
    d_s = b.sbuf("mb_d", [128, NCI])
    hin = b.sbuf("mb_hin", [128, 256])
    b.load(a_s, a_sb)
    b.load(d_s, d_in)
    with ExitStack() as sctx:
        smyp = sctx.enter_context(b.rawnc.sbuf_tensor("mb_smyp", [128, 3 * 512], F32)).ap()
        tmp256 = sctx.enter_context(b.rawnc.sbuf_tensor("mb_t256", [128, 256], F32)).ap()
        b.load(smyp, smy_prev)
        b.barrier()
        b.i(nc.scalar.activation(a_s[:, :], a_s[:, :], AF.Exp), "sc")
        b.w(nc.vector, "sc")
        b.i(nc.vector.tensor_scalar(a_s[:, :], a_s[:, :], -1.0, None, ALU.mult), "ve")
        nc.vector.memset(hin[:, :], 0.0)
        for j in range(3):
            nc.vector.tensor_mul(tmp256, hin, smyp[:, j * 512 + 256:(j + 1) * 512])
            nc.vector.tensor_add(hin, tmp256, smyp[:, j * 512:j * 512 + 256])
        b.barrier()

    wo_r = b.sbuf("mb_wo", [128, NCI, D], BF16)
    b.load(wo_r, wob2[0])
    yacc = b.sbuf("mb_yacc", [128, NCI, TB])
    ybuf = b.sbuf("mb_ybuf", [128, NCI, TB], BF16)
    zbuf = b.sbuf("mb_z", [128, TB])
    xcg = b.sbuf("mb_xcg", [128, TB])
    zsil = b.sbuf("mb_zs", [128, TB])
    xstg = b.sbuf("mb_xs", [128, TB])
    r1 = b.sbuf("mb_r1", [128, TB])
    b.barrier()

    sst = scan_setup(b, init_hin=hin)
    for tb in range(T_OWN // TB):
        o = tb * TB
        scan_tblock(b, sst, tb, dt_in, xc_in, par_in, a_s, ps_bc, yacc=yacc)
        for ci in range(NCI):
            b.w(nc.sync, "sc")
            b.w(nc.sync, "ve")
            lz = b.load(zbuf, z_in[:, ci, o:o + TB])
            lxc = b.load(xcg, xc_in[:, ci, o:o + TB])
            b.w(nc.scalar, "dl", lz)
            sc_z = b.i(nc.scalar.activation(zsil, zbuf, AF.Silu), "sc")
            b.w(nc.vector, "dl", lxc)
            b.w(nc.vector, "sc", sc_z)
            b.w(nc.vector, "po")
            b.w(nc.vector, "pe")
            nc.vector.tensor_scalar(r1, xcg, d_s[:, ci:ci + 1], None, ALU.mult)
            nc.vector.tensor_add(r1, r1, yacc[:, ci, :])
            b.i(nc.vector.tensor_mul(ybuf[:, ci, :], r1, zsil), "ve")
        ve_y = b.n["ve"]
        for oc in range(DC):
            b.w(nc.tensor, "ve", ve_y)
            b.w(nc.tensor, "ve")
            for ci in range(NCI):
                mm = nc.tensor.matmul(ps_a[:, 0:TB], wo_r[:, ci, oc * 128:(oc + 1) * 128],
                                      ybuf[:, ci, :], start=(ci == 0),
                                      stop=(ci == NCI - 1))
            pe_o = b.i(mm, "pe")
            b.w(nc.sync, "st")
            lx = b.load(xstg, x_in[:, oc, 3 + o:3 + o + TB])
            b.w(nc.vector, "dl", lx)
            b.w(nc.vector, "pe", pe_o)
            nc.vector.tensor_mul(r1, xstg, scr["alpha_b"][:, o:o + TB])
            nc.vector.tensor_add(r1, r1, scr["beta_b"][:, o:o + TB])
            nc.vector.tensor_add(r1, r1, xstg)
            ve_o = b.i(nc.vector.tensor_add(r1, r1, ps_a[:, 0:TB]), "ve")
            b.w(nc.sync, "ve", ve_o)
            b.store(x_out[:, oc, o:o + TB], r1)


def _mamba_host_prep(inputs):
    pr = {}
    w_inT = np.asarray(inputs["L1_attn_W_in"], np.float32).T
    pr["w_inT"] = _pm(w_inT, (DC, 128, 8, 4, 128), (2, 1, 0, 3, 4),
                      (8, 128, DC * 4 * 128))
    w_xT = np.asarray(inputs["L1_attn_W_x"], np.float32).T
    pr["w_xT"] = _pm(w_xT, (NCI, 128, 33), (1, 0, 2), (1, 128, NCI * 33))
    w_outT = np.asarray(inputs["L1_attn_W_out"], np.float32).T
    pr["w_outT"] = _pm(w_outT, (NCI, 128, D), (1, 0, 2), (1, 128, NCI * D))
    pr["w_dt"] = np.ascontiguousarray(np.asarray(inputs["L1_attn_W_dt"], np.float32).reshape(1, D_INNER))
    pr["g"] = _fm_vec(inputs["L1_attn_g"], DC)
    pr["conv_b"] = _fm_vec(inputs["L1_attn_conv_b"], NCI)
    pr["b_dt"] = _fm_vec(inputs["L1_attn_b_dt"], NCI)
    pr["D"] = _fm_vec(inputs["L1_attn_D"], NCI)
    cw = np.asarray(inputs["L1_attn_conv_w"], np.float32)
    pr["conv_w"] = np.ascontiguousarray(cw.reshape(NCI, 128, 4).transpose(1, 0, 2))
    al = np.asarray(inputs["L1_attn_A_log"], np.float32)
    pr["a_sb"] = np.ascontiguousarray(
        al.reshape(NCI, 128, D_STATE).transpose(1, 0, 2).reshape(128, NCI * D_STATE))
    return pr


def run_mamba_stage(x4parts, inputs):
    pr = _mamba_host_prep(inputs)
    nc_a = build_mamba_a()
    im_a = []
    for c in range(NC):
        hm = np.ones((128, 4), np.float32)
        if c % 4 == 0:
            hm[:] = 0.0
        im_a.append(dict(x_in=np.ascontiguousarray(x4parts[c][:, :, 0:W_M]),
                         g_in=pr["g"], w_in=pr["w_inT"], conv_w=pr["conv_w"],
                         conv_b=pr["conv_b"], w_x=pr["w_xT"], w_dt=pr["w_dt"],
                         b_dt=pr["b_dt"], a_sb=pr["a_sb"], hm3=hm))
    res_a = run_bass_kernel_spmd(nc_a, im_a, core_ids=list(range(NC)))
    nc_b = build_mamba_b()
    im_b = []
    ident_smy = np.concatenate([np.zeros((128, 256), np.float32),
                                np.ones((128, 256), np.float32)], axis=1)
    for c in range(NC):
        bb, kk = c // 4, c % 4
        prevs = [ident_smy] * (3 - kk) + [
            np.asarray(res_a.results[bb * 4 + j]["smy_out"]) for j in range(kk)]
        smyp = np.concatenate(prevs, axis=1)
        ra = res_a.results[c]
        im_b.append(dict(x_in=np.ascontiguousarray(x4parts[c][:, :, 0:W_M]),
                         z_in=np.asarray(ra["z_out"]), xc_in=np.asarray(ra["xc_out"]),
                         dt_in=np.asarray(ra["dt_out"]), par_in=np.asarray(ra["par_out"]),
                         smy_prev=smyp, a_sb=pr["a_sb"], d_in=pr["D"],
                         w_out=pr["w_outT"]))
    res_b = run_bass_kernel_spmd(nc_b, im_b, core_ids=list(range(NC)))
    return [np.asarray(res_b.results[c]["x_out"]) for c in range(NC)]


def _prep_consts(b):
    _ident128(b)
    _ones_row(b)
    _stat_lhs(b)
    b.const_sbuf("ones4", np.ones((128, 4), np.float32))
    b.const_sbuf("ones16", np.ones((128, 16), np.float32))
    b.const_sbuf("ones8", np.ones((128, 8), np.float32))
    b.const_sbuf("zero1", np.zeros((128, 1), np.float32))
    b.reg_fconst(LN_EPS)
    b.reg_fconst(RMS_EPS)
    b.reg_fconst(1.0)


def build_launch1():
    b = B()
    nc = b.nc
    NT = W_ATT // 128
    x_in = b.inp("x_in", [128, DC, T_EXT])
    w1 = b.inp("w1", [8, 128, DC * 512])
    w2 = b.inp("w2", [8, 128, 4 * D])
    rw0 = b.inp("rw0", [D, 8])
    b1 = b.inp("b1", [128, F0 // 128])
    b2 = b.inp("b2", [128, DC])
    wq = b.inp("wq", [1, 128, DC * D]); wk = b.inp("wk", [1, 128, DC * D])
    wv = b.inp("wv", [1, 128, DC * D]); wo = b.inp("wo", [1, 128, DC * D])
    bq = b.inp("bq", [128, DC]); bk = b.inp("bk", [128, DC])
    bv = b.inp("bv", [1, D]); bo = b.inp("bo", [128, DC])
    cosq = b.inp("cosq", [128, 4, T_EXT]); sinq = b.inp("sinq", [128, 4, T_EXT])
    cosk = b.inp("cosk", [128, 4, T_EXT]); sink = b.inp("sink", [128, 4, T_EXT])
    maskA = b.inp("maskA", [NT, 128, HALO])
    maskT = b.inp("maskT", [NT, 128, HALO])
    rw1 = b.inp("rw1", [1, 128, DC * NE])
    rbias = b.inp("rbias", [128, NE])
    gw = b.inp("gw", [NE, 128, DC * D]); uw = b.inp("uw", [NE, 128, DC * D])
    dw = b.inp("dw", [NE, 128, DC * D])
    shg = b.inp("shg", [2, 128, DC * D]); shu = b.inp("shu", [2, 128, DC * D])
    shd = b.inp("shd", [2, 128, DC * D])
    g_in = b.inp("g_in", [128, DC])
    w_in = b.inp("w_in", [8, 128, DC * 4 * 128])
    conv_w = b.inp("conv_w", [128, NCI, 4])
    conv_b = b.inp("conv_b", [128, NCI])
    w_x = b.inp("w_x", [1, 128, NCI * 33])
    w_dt = b.inp("w_dt", [1, D_INNER])
    b_dt = b.inp("b_dt", [128, NCI])
    a_sb = b.inp("a_sb", [128, NCI * D_STATE])
    hm3 = b.inp("hm3", [128, 4])

    x4m_out = b.out("x4m_out", [128, DC, W_M])
    z_out = b.out("z_out", [128, NCI, T_OWN])
    xc_out = b.out("xc_out", [128, NCI, T_OWN])
    dt_out = b.out("dt_out", [128, NCI, T_OWN])
    smy_out = b.out("smy_out", [128, 512])
    par_out = b.out("par_out", [33, T_OWN])

    w1b = b.dram("w1b", [8, 128, DC * 512], BF16)
    w2b = b.dram("w2b", [8, 128, 4 * D], BF16)
    wqb = b.dram("wqb", [1, 128, DC * D], BF16)
    wkb = b.dram("wkb", [1, 128, DC * D], BF16)
    wvb = b.dram("wvb", [1, 128, DC * D], BF16)
    wob = b.dram("wob", [1, 128, DC * D], BF16)
    rwb = b.dram("rwb", [1, 128, DC * NE], BF16)
    gwb = b.dram("gwb", [NE, 128, DC * D], BF16)
    uwb = b.dram("uwb", [NE, 128, DC * D], BF16)
    dwb = b.dram("dwb", [NE, 128, DC * D], BF16)
    shgb = b.dram("shgb", [2, 128, DC * D], BF16)
    shub = b.dram("shub", [2, 128, DC * D], BF16)
    shdb = b.dram("shdb", [2, 128, DC * D], BF16)
    wib = b.dram("wib", [8, 128, DC * 4 * 128], BF16)
    wxb = b.dram("wxb", [1, 128, NCI * 33], BF16)

    x1d = b.dram("x1d", [128, DC, T_EXT])
    qd = b.dram("qd", [128, DC, W_ATT], BF16)
    kd = b.dram("kd", [128, DC, T_EXT], BF16)
    vd = b.dram("vd", [128, T_EXT // 128, D], BF16)
    x2d = b.dram("x2d", [128, DC, W_ATT])
    x3d = b.dram("x3d", [128, DC, W_ATT])
    x4d = b.dram("x4d", [128, DC, W_ATT])

    _prep_consts(b)
    emit_wcast(b, [(w1, w1b), (w2, w2b), (wq, wqb), (wk, wkb), (wv, wvb),
                   (wo, wob), (rw1, rwb), (gw, gwb), (uw, uwb), (dw, dwb),
                   (shg, shgb), (shu, shub), (shd, shdb), (w_in, wib),
                   (w_x, wxb)])

    with b.scope():
        emit_l0ffn_stage(b, x_in, x1d, T_EXT, w1b, w2b, rw0, b1, b2)
    b.flush_stores()
    with b.scope():
        emit_attn_a_stage(b, x1d, qd, kd, vd, wqb, wkb, wvb, bq, bk, bv,
                          cosq, sinq, cosk, sink)
    b.flush_stores()
    with b.scope():
        emit_attn_b_stage(b, x1d, qd, kd, vd, x2d, wob, bo, maskA, maskT)
    b.flush_stores()
    with b.scope():
        emit_l0ffn_stage(b, x2d, x3d, W_ATT, w1b, w2b, rw0, b1, b2)
    b.flush_stores()
    with b.scope():
        emit_l1ffn_stage(b, x3d, x4d, W_ATT, rwb, gwb, uwb, dwb,
                         shgb, shub, shdb, rbias, tag="f1a")
    b.flush_stores()
    b.store(x4m_out, x4d[:, :, W_ATT - W_M:])
    with b.scope():
        emit_mamba_a_stage(b, x4d[:, :, W_ATT - W_M:], z_out, xc_out, dt_out,
                           smy_out, par_out, wib, wxb, g_in, conv_w, conv_b,
                           w_dt, b_dt, a_sb, hm3)
    return b.finish()


def build_launch2():
    b = B()
    x_in = b.inp("x_in", [128, DC, W_M])
    z_in = b.inp("z_in", [128, NCI, T_OWN])
    xc_in = b.inp("xc_in", [128, NCI, T_OWN])
    dt_in = b.inp("dt_in", [128, NCI, T_OWN])
    par_in = b.inp("par_in", [33, T_OWN])
    smy_prev = b.inp("smy_prev", [128, 3 * 512])
    a_sb = b.inp("a_sb", [128, NCI * D_STATE])
    d_in = b.inp("d_in", [128, NCI])
    w_out = b.inp("w_out", [1, 128, NCI * D])
    rw1 = b.inp("rw1", [1, 128, DC * NE])
    rbias = b.inp("rbias", [128, NE])
    gw = b.inp("gw", [NE, 128, DC * D]); uw = b.inp("uw", [NE, 128, DC * D])
    dw = b.inp("dw", [NE, 128, DC * D])
    shg = b.inp("shg", [2, 128, DC * D]); shu = b.inp("shu", [2, 128, DC * D])
    shd = b.inp("shd", [2, 128, DC * D])
    x_out = b.out("x_out", [128, DC, T_OWN])

    wob2 = b.dram("wob2", [1, 128, NCI * D], BF16)
    rwb = b.dram("rwb", [1, 128, DC * NE], BF16)
    gwb = b.dram("gwb", [NE, 128, DC * D], BF16)
    uwb = b.dram("uwb", [NE, 128, DC * D], BF16)
    dwb = b.dram("dwb", [NE, 128, DC * D], BF16)
    shgb = b.dram("shgb", [2, 128, DC * D], BF16)
    shub = b.dram("shub", [2, 128, DC * D], BF16)
    shdb = b.dram("shdb", [2, 128, DC * D], BF16)
    x5d = b.dram("x5d", [128, DC, T_OWN])

    _prep_consts(b)
    emit_wcast(b, [(w_out, wob2), (rw1, rwb), (gw, gwb), (uw, uwb), (dw, dwb),
                   (shg, shgb), (shu, shub), (shd, shdb)])
    with b.scope():
        emit_mamba_b_stage(b, x_in, z_in, xc_in, dt_in, par_in, smy_prev,
                           a_sb, d_in, wob2, x5d)
    b.flush_stores()
    with b.scope():
        emit_l1ffn_stage(b, x5d, x_out, T_OWN, rwb, gwb, uwb, dwb,
                         shgb, shub, shdb, rbias, tag="f1b")
    return b.finish()


def kernel(**inputs):
    x0 = np.asarray(inputs["L0_attn_hidden_state"], np.float32)
    xp = _shard_x(x0)
    pr = _mamba_host_prep(inputs)
    wp = _prep_weights(inputs)
    nc1 = build_launch1()
    im1 = []
    for c in range(NC):
        kk = c % 4
        pos = np.maximum(kk * T_OWN - HALO + np.arange(T_EXT), 0)
        cq, sq = _rope_tables(pos, 0.125)
        ck, sk = _rope_tables(pos, 1.0)
        mA, mT = _attn_masks(c)
        hm = np.ones((128, 4), np.float32)
        if kk == 0:
            hm[:] = 0.0
        im1.append(dict(
            x_in=np.ascontiguousarray(xp[c]),
            cosq=cq, sinq=sq, cosk=ck, sink=sk, maskA=mA, maskT=mT,
            g_in=pr["g"], w_in=pr["w_inT"], conv_w=pr["conv_w"],
            conv_b=pr["conv_b"], w_x=pr["w_xT"], w_dt=pr["w_dt"],
            b_dt=pr["b_dt"], a_sb=pr["a_sb"], hm3=hm,
            **{k: wp[k] for k in ("w1", "w2", "rw0", "b1", "b2", "wq", "wk",
                                  "wv", "wo", "bq", "bk", "bv", "bo", "rw1",
                                  "rbias", "gw", "uw", "dw", "shg", "shu",
                                  "shd")}))
    res1 = run_bass_kernel_spmd(nc1, im1, core_ids=list(range(NC)))

    nc2 = build_launch2()
    ident_smy = np.concatenate([np.zeros((128, 256), np.float32),
                                np.ones((128, 256), np.float32)], axis=1)
    im2 = []
    for c in range(NC):
        bb, kk = c // 4, c % 4
        prevs = [ident_smy] * (3 - kk) + [
            np.asarray(res1.results[bb * 4 + j]["smy_out"]) for j in range(kk)]
        r1 = res1.results[c]
        im2.append(dict(
            x_in=np.asarray(r1["x4m_out"]), z_in=np.asarray(r1["z_out"]),
            xc_in=np.asarray(r1["xc_out"]), dt_in=np.asarray(r1["dt_out"]),
            par_in=np.asarray(r1["par_out"]),
            smy_prev=np.ascontiguousarray(np.concatenate(prevs, axis=1)),
            a_sb=pr["a_sb"], d_in=pr["D"], w_out=pr["w_outT"],
            **{k: wp[k] for k in ("rw1", "rbias", "gw", "uw", "dw",
                                  "shg", "shu", "shd")}))
    res2 = run_bass_kernel_spmd(nc2, im2, core_ids=list(range(NC)))
    outs = [np.asarray(res2.results[c]["x_out"]) for c in range(NC)]
    return _unshard_x(outs, T_OWN, 0)


# version 11 (final)
# speedup vs baseline: 1.4873x; 1.0177x over previous
import numpy as np
from contextlib import ExitStack

import concourse.bass as bass
import concourse.mybir as mybir
from concourse.bass_types import AP
from concourse.bass_utils import run_bass_kernel_spmd

F32 = mybir.dt.float32
BF16 = mybir.dt.bfloat16
AX = mybir.AxisListType.X
ALU = mybir.AluOpType
AF = mybir.ActivationFunctionType

NC = 8
D = 1024
DC = 8
T_OWN = 2048
HALO = 640
T_EXT = T_OWN + HALO
W_ATT = T_OWN + 128
NEG = -3.0e38
LN_EPS = 1e-5
RMS_EPS = 1e-6
D_INNER = 2048
D_STATE = 16
NE = 16
F0 = 4096
F_SH = 2048
GELU_C = 0.7978845608028654


def _slices(total, step, start=0):
    out = []
    o = start
    while o < total:
        out.append((o, min(step, total - o)))
        o += step
    return out


_GUARDED = {
    "tensor_tensor", "tensor_add", "tensor_sub", "tensor_mul", "tensor_max",
    "tensor_scalar", "tensor_single_scalar", "tensor_scalar_add",
    "tensor_scalar_sub", "tensor_scalar_mul", "tensor_scalar_max",
    "tensor_scalar_min", "tensor_relu", "tensor_reduce", "reduce_max",
    "reduce_sum", "scalar_tensor_tensor", "tensor_tensor_scan", "tensor_copy",
    "reciprocal", "select", "copy_predicated", "activation", "square", "sqrt",
    "mul", "copy", "add", "sign", "memset", "memzero", "bn_stats", "bn_aggr",
    "max", "max_index", "iota", "affine_select", "transpose", "stream_shuffle",
}


def _ap_name(a):
    try:
        return a.name
    except Exception:
        return None


_LAGGING = {
    "reduce_max", "reduce_sum", "tensor_reduce", "tensor_tensor_scan",
    "bn_stats", "bn_aggr", "max", "max_index",
}
_LAG_GAP = 2


class _EngProxy:

    def __init__(self, eng):
        self._eng = eng
        self._dirty = set()

    def _clear(self):
        self._dirty.clear()

    def __getattr__(self, name):
        real = getattr(self._eng, name)
        if name not in _GUARDED or not callable(real):
            return real

        def wrapped(*args, **kw):
            aps = [a for a in args if isinstance(a, AP)]
            aps += [v for v in kw.values() if isinstance(v, AP)]
            names = {_ap_name(a) for a in aps}
            if names & self._dirty:
                self._eng.drain()
                self._dirty.clear()
            r = real(*args, **kw)
            if aps:
                self._dirty.add(_ap_name(aps[0]))
            ao = kw.get("accum_out")
            if isinstance(ao, AP):
                self._dirty.add(_ap_name(ao))
            return r

        return wrapped


class _NCShim:

    def __init__(self, nc):
        self._nc = nc
        self.vector = _EngProxy(nc.vector)
        self.scalar = _EngProxy(nc.scalar)
        self.gpsimd = _EngProxy(nc.gpsimd)
        self.tensor = nc.tensor
        self.sync = nc.sync

    def __getattr__(self, name):
        return getattr(self._nc, name)

    def proxy_of(self, eng):
        for p in (self.vector, self.scalar, self.gpsimd):
            if p._eng is eng:
                return p
        return None


class B:
    def __init__(self, name="k"):
        self.rawnc = bass.Bass("TRN2", target_bir_lowering=False)
        self.nc = _NCShim(self.rawnc)
        self.ctx = ExitStack()
        self.n = {}
        self.sems = {}
        self._waited = {}
        self._inline = {}
        for key in ("pe", "sc", "ve", "po", "sy", "dl", "st"):
            self.sems[key] = self.ctx.enter_context(self.rawnc.semaphore(key))
            self.n[key] = 0

    def sbuf(self, name, shape, dtype=F32):
        if getattr(self, "_scopes", None):
            return self._scopes[-1].enter_context(
                self.rawnc.sbuf_tensor(name, shape, dtype,
                                       allow_name_mangling=True)).ap()
        return self.nc.alloc_sbuf_tensor(name, shape, dtype).ap()

    def psum(self, name, shape, dtype=F32):
        if getattr(self, "_scopes", None):
            return self._scopes[-1].enter_context(
                self.rawnc.psum_tensor(name, shape, dtype,
                                       allow_name_mangling=True)).ap()
        return self.nc.alloc_psum_tensor(name, shape, dtype).ap()

    from contextlib import contextmanager as _cm

    @_cm
    def scope(self):
        if not hasattr(self, "_scopes"):
            self._scopes = []
        es = ExitStack()
        self._scopes.append(es)
        try:
            with es:
                yield
        finally:
            self._scopes.pop()

    def flush_stores(self):
        self.barrier()
        for eng in self.rawnc.engines.values():
            self.w(eng, "st")
        self.barrier()

    def inline(self, name, arr):
        if name not in self._inline:
            self._inline[name] = self.nc.inline_tensor(
                np.ascontiguousarray(arr), name).ap()
        return self._inline[name]

    def reg_fconst(self, *vals):
        for v in vals:
            ap = self.const_sbuf(f"fc_{v}", np.full((128, 1), v, np.float32))
            self.nc.const_aps.register(F32, float(v), ap)

    def const_sbuf(self, name, arr, dtype=None):
        key = "cs_" + name
        if key not in self._inline:
            dram = self.nc.inline_tensor(np.ascontiguousarray(arr), name).ap()
            if dtype is None:
                dtype = dram.dtype
            sb = self.sbuf(key, list(arr.shape), dtype)
            self.load(sb, dram)
            self.barrier()
            for eng in self.rawnc.engines.values():
                self.w(eng, "dl")
            self.barrier()
            self._inline[key] = sb
        return self._inline[key]

    def inp(self, name, shape, dtype=F32):
        return self.nc.dram_tensor(name, list(shape), dtype, kind="ExternalInput").ap()

    def out(self, name, shape, dtype=F32):
        return self.nc.dram_tensor(name, list(shape), dtype, kind="ExternalOutput").ap()

    def dram(self, name, shape, dtype=F32):
        if not hasattr(self, "_dram_names"):
            self._dram_names = set()
        base, k = name, 1
        while name in self._dram_names:
            k += 1
            name = f"{base}_{k}"
        self._dram_names.add(name)
        return self.nc.dram_tensor(name, list(shape), dtype, kind="Internal").ap()

    def i(self, instr, key, drain=True):
        self.n[key] += 1
        if key == "pe" or not drain:
            instr.then_inc(self.sems[key], 1)
            return self.n[key]
        eng = self.rawnc.engines[instr.ins.engine]
        eng.drain().then_inc(self.sems[key], 1)
        p = self.nc.proxy_of(eng)
        if p is not None:
            p._clear()
        return self.n[key]

    def w(self, eng, key, val=None):
        if isinstance(eng, _EngProxy):
            eng = eng._eng
        if val is None:
            val = self.n[key]
        if val <= 0:
            return
        k = (self._ekey(eng), key)
        if self._waited.get(k, -1) >= val:
            return
        eng.wait_ge(self.sems[key], val)
        self._waited[k] = val

    def load(self, dst, src, q=None, **kw):
        eng = q if q is not None else self.nc.sync
        eng.dma_start(out=dst, in_=src, **kw).then_inc(self.sems["dl"], 16)
        self.n["dl"] += 16
        return self.n["dl"]

    def store(self, dst, src, q=None, **kw):
        eng = q if q is not None else self.nc.sync
        eng.dma_start(out=dst, in_=src, **kw).then_inc(self.sems["st"], 16)
        self.n["st"] += 16
        return self.n["st"]

    def barrier(self):
        self.rawnc.all_engine_barrier()
        for p in (self.nc.vector, self.nc.scalar, self.nc.gpsimd):
            p._clear()
        for key in ("pe", "sc", "ve", "po", "sy"):
            for eng in self.nc.engines.values():
                self._waited[(self._ekey(eng), key)] = self.n[key]

    def _ekey(self, eng):
        return eng.engine if hasattr(eng, "engine") else id(eng)

    def finish(self):
        self.rawnc.sync.wait_ge(self.sems["st"], self.n["st"])
        self.ctx.close()
        return self.rawnc


def _ident128(b):
    return b.const_sbuf("ident128", np.eye(128, dtype=np.float32))


def _ones_row(b):
    return b.const_sbuf("ones_row", np.ones((1, 128), np.float32))


def _stat_lhs(b):
    import ml_dtypes
    arr = np.zeros((128, 2), np.float32)
    arr[:, 0] = 1.0 / D
    arr[:, 1] = 1.0
    return b.const_sbuf("stat_lhs", arr.astype(ml_dtypes.bfloat16))


def emit_wcast(b, jobs):
    nc = b.nc
    CM = 4096
    with ExitStack() as wctx:
        stg = [wctx.enter_context(b.rawnc.sbuf_tensor(f"wcs{i}", [128, CM], F32)).ap()
               for i in range(2)]
        stgb = [wctx.enter_context(b.rawnc.sbuf_tensor(f"wcb{i}", [128, CM], BF16)).ap()
                for i in range(2)]
        last_cast = [0, 0]
        last_store = [0, 0]
        i = 0
        for src_ap, dst in jobs:
            G = src_ap.shape[0]
            C = src_ap.shape[2]
            for g in range(G):
                for (co, cn_w) in _slices(C, CM):
                    k = i % 2
                    if last_cast[k]:
                        b.w(nc.sync, "ve", last_cast[k])
                    ln = b.load(stg[k][:, 0:cn_w], src_ap[g][:, co:co + cn_w])
                    b.w(nc.vector, "dl", ln)
                    if last_store[k]:
                        b.w(nc.vector, "st")
                    cn = b.i(nc.vector.tensor_copy(stgb[k][:, 0:cn_w],
                                                   stg[k][:, 0:cn_w]), "ve")
                    last_cast[k] = cn
                    b.w(nc.sync, "ve", cn)
                    last_store[k] = b.store(dst[g][:, co:co + cn_w],
                                            stgb[k][:, 0:cn_w])
                    i += 1
        b.barrier()


def alloc_ln_scratch(b, W, tag=""):
    return dict(
        xcast=[b.sbuf(f"lnxc{tag}{i}", [128, 512], BF16) for i in range(2)],
        sq=[b.sbuf(f"lnsq{tag}{i}", [128, 512], BF16) for i in range(2)],
        xs=[b.sbuf(f"lnxs{tag}{i}", [128, 512], F32) for i in range(2)],
        alpha_r=b.sbuf(f"lnar{tag}", [1, W], F32),
        beta_r=b.sbuf(f"lnbr{tag}", [1, W], F32),
        alpha_b=b.sbuf(f"lnab{tag}", [128, W], F32),
        beta_b=b.sbuf(f"lnbb{tag}", [128, W], F32),
    )


def emit_ln_stats(b, x_src, W, scr, ps_a, ps_b):
    import os
    sub = int(os.environ.get("KDEBUG_SUB", "9"))
    nc = b.nc
    stat_lhs = _stat_lhs(b)
    ones_row = _ones_row(b)
    mean_r, sq_r = scr["alpha_r"], scr["beta_r"]
    last_mm = [0, 0]
    for (o, n) in _slices(W, 512):
        for c in range(DC):
            k = c % 2
            if last_mm[k]:
                b.w(nc.sync, "pe", last_mm[k])
            ln = b.load(scr["xs"][k][:, 0:n], x_src[:, c, o:o + n])
            b.w(nc.scalar, "dl", ln)
            if last_mm[k]:
                b.w(nc.scalar, "pe", last_mm[k])
            nc.scalar.copy(scr["xcast"][k][:, 0:n], scr["xs"][k][:, 0:n])
            sn = b.i(nc.scalar.activation(scr["sq"][k][:, 0:n],
                                          scr["xcast"][k][:, 0:n], AF.Square), "sc")
            b.w(nc.tensor, "sc", sn)
            nc.tensor.matmul(ps_a[0:1, 0:n], stat_lhs[:, 0:1], scr["xcast"][k][:, 0:n],
                             start=(c == 0), stop=(c == DC - 1), skip_group_check=True)
            mmn = b.i(
                nc.tensor.matmul(ps_b[0:1, 0:n], stat_lhs[:, 1:2], scr["sq"][k][:, 0:n],
                                 start=(c == 0), stop=(c == DC - 1),
                                 skip_group_check=True), "pe")
            last_mm[k] = mmn
        b.w(nc.vector, "pe", last_mm[(DC - 1) % 2])
        nc.vector.tensor_copy(mean_r[:, o:o + n], ps_a[0:1, 0:n])
        vn = b.i(nc.vector.tensor_copy(sq_r[:, o:o + n], ps_b[0:1, 0:n]), "ve")
        b.w(nc.tensor, "ve", vn)
    b.barrier()
    if sub == 0:
        return
    ar, br = scr["alpha_r"], scr["beta_r"]
    tmp_r = scr["alpha_b"][0:1, :]
    nc.scalar.mul(br[:, 0:W], br[:, 0:W], 1.0 / D)
    sn = b.i(nc.scalar.activation(tmp_r[:, 0:W], ar[:, 0:W], AF.Square), "sc")
    b.w(nc.vector, "sc", sn)
    vn = b.i(nc.vector.tensor_sub(br[:, 0:W], br[:, 0:W], tmp_r[:, 0:W]), "ve")
    b.w(nc.scalar, "ve", vn)
    sn = b.i(nc.scalar.activation(br[:, 0:W], br[:, 0:W], AF.Sqrt, bias=LN_EPS), "sc")
    b.w(nc.vector, "sc", sn)
    vn = b.i(nc.vector.reciprocal(br[:, 0:W], br[:, 0:W]), "ve")
    b.w(nc.vector, "ve", vn)
    nc.vector.tensor_mul(ar[:, 0:W], ar[:, 0:W], br[:, 0:W])
    nc.vector.tensor_scalar(ar[:, 0:W], ar[:, 0:W], -1.0, None, ALU.mult)
    nc.vector.tensor_copy(tmp_r[:, 0:W], ar[:, 0:W])
    nc.vector.tensor_copy(ar[:, 0:W], br[:, 0:W])
    vn = b.i(nc.vector.tensor_copy(br[:, 0:W], tmp_r[:, 0:W]), "ve")
    b.barrier()
    if sub == 1:
        return
    for (o, n) in _slices(W, 512):
        mm = b.i(nc.tensor.matmul(ps_a[:, 0:n], ones_row, ar[:, o:o + n],
                                  start=True, stop=True), "pe")
        b.w(nc.vector, "pe", mm)
        vn = b.i(nc.vector.tensor_copy(scr["alpha_b"][:, o:o + n], ps_a[:, 0:n]), "ve")
        b.w(nc.tensor, "ve", vn)
        mm = b.i(nc.tensor.matmul(ps_a[:, 0:n], ones_row, br[:, o:o + n],
                                  start=True, stop=True), "pe")
        b.w(nc.vector, "pe", mm)
        vn = b.i(nc.vector.tensor_copy(scr["beta_b"][:, o:o + n], ps_a[:, 0:n]), "ve")
        b.w(nc.tensor, "ve", vn)
    b.barrier()


def emit_ln_apply(b, x_sl, scr, o, n, out_bf, tmp):
    nc = b.nc
    nc.vector.tensor_mul(tmp[:, 0:n], x_sl[:, 0:n], scr["alpha_b"][:, o:o + n])
    return b.i(nc.vector.tensor_add(out_bf[:, 0:n], tmp[:, 0:n],
                                    scr["beta_b"][:, o:o + n]), "ve")


def build_l0ffn(W):
    b = B()
    x_in = b.inp("x_in", [128, DC, W])
    w1 = b.inp("w1", [8, 128, DC * 512])
    w2 = b.inp("w2", [8, 128, 4 * D])
    rw = b.inp("rw", [D, 8])
    b1 = b.inp("b1", [128, F0 // 128])
    b2 = b.inp("b2", [128, DC])
    x_out = b.out("x_out", [128, DC, W])
    w1b = b.dram("w1b", [8, 128, DC * 512], BF16)
    w2b = b.dram("w2b", [8, 128, 4 * D], BF16)
    b.reg_fconst(LN_EPS)
    emit_wcast(b, [(w1, w1b), (w2, w2b)])
    with b.scope():
        emit_l0ffn_stage(b, x_in, x_out, W, w1b, w2b, rw, b1, b2)
    return b.finish()


def emit_l0ffn_stage(b, x_in, x_out, W, w1b, w2b, rw, b1, b2):
    nc = b.nc
    import os
    phase = int(os.environ.get("KDEBUG_PHASE", "9"))
    ps_a = b.psum("ps_a", [128, 512])
    ps_b = b.psum("ps_b", [128, 512])
    ps_e = b.psum("ps_e", [128, 4, 512])

    scr = alloc_ln_scratch(b, W)
    emit_ln_stats(b, x_in, W, scr, ps_a, ps_b)
    if os.environ.get("KDEBUG_DUMP") and not getattr(b, "_dbg_used", False):
        b._dbg_used = True
        d_ab = b.out("d_ab", [2, W])
        b.store(d_ab[0:1, :], scr["alpha_r"][:, 0:W])
        b.store(d_ab[1:2, :], scr["beta_r"][:, 0:W])
        b.dbg_fac = b.out("d_fac", [8, 512])
        b.dbg_ftm = b.out("d_ftm", [128, 4])
        b.dbg_r = b.out("d_r", [128, 24])
        b.dbg_gh = b.out("d_gh", [128, 4, 512], BF16)
        b.dbg_eo = b.out("d_eo", [128, 2, 256])
        b.dbg_ft2 = b.out("d_ft2", [4, 128])
        b.dbg_xn = b.out("d_xn", [128, W], BF16)
    else:
        b.dbg_fac = None
        b.dbg_xn = None
    if phase == 1:
        return

    rws = b.sbuf("rws", [128, DC, 8], BF16)
    rwf = b.sbuf("rwf", [128, DC, 8], F32)
    for c in range(DC):
        b.load(rwf[:, c, :], rw[c * 128:(c + 1) * 128, :])
    b.w(nc.vector, "dl")
    b.i(nc.vector.tensor_copy(rws[:, :, :], rwf[:, :, :]), "ve")
    b1s = b.sbuf("b1s", [128, F0 // 128])
    b2s = b.sbuf("b2s", [128, DC])
    b.load(b1s, b1)
    b.load(b2s, b2)
    b.barrier()

    emit_l0ffn_slices(b, x_in, x_out, W, scr, w1b, w2b, rws, b1s, b2s,
                      ps_a, ps_b, ps_e)


def emit_l0ffn_slices(b, x_in, x_out, W, scr, w1b, w2b, rws, b1s, b2s,
                      ps_a, ps_b, ps_e):
    nc = b.nc
    ident = _ident128(b)
    ones_row = _ones_row(b)
    NF = F0 // 128

    x_s = [b.sbuf(f"l0x{i}", [128, DC, 512], F32) for i in range(2)]
    xn_s = b.sbuf("l0xn", [128, DC, 512], BF16)
    tmp = b.sbuf("l0tmp", [128, 512], F32)
    gh = b.sbuf("l0gh", [128, NF, 512], BF16)
    w1blk = [b.sbuf(f"l0w1{i}", [128, DC, 512], BF16) for i in range(2)]
    w2g = [b.sbuf(f"l0w2{i}", [128, 4, D], BF16) for i in range(2)]
    eo = b.sbuf("l0eo", [128, 512], F32)
    b2s_dbg = b.sbuf("l0eodbg", [128, 512], F32)
    t1 = b.sbuf("l0t1", [128, 512], F32)
    r_pr = b.sbuf("l0rpr", [128, 8], F32)
    r_m1 = b.sbuf("l0rm1", [128, 1], F32)
    r_m2 = b.sbuf("l0rm2", [128, 1], F32)
    r_s = b.sbuf("l0rs", [128, 1], F32)
    r_ge = b.sbuf("l0rge", [128, 8], F32)
    r_t1 = b.sbuf("l0rt1", [128, 8], F32)
    fac_tm = b.sbuf("l0ftm", [128, 4], F32)
    fac_t2 = b.sbuf("l0ft2", [4, 128], F32)
    fac_row = b.sbuf("l0frow", [1, 512], F32)
    fac_b = b.sbuf("l0fb", [128, 512], F32)
    fac_dram = b.dram("l0fdram", [4, 128], F32)

    import os
    nsl = int(os.environ.get("KDEBUG_NSLICE", "99"))
    parts = os.environ.get("KDEBUG_PARTS", "rab")
    for si, (o, n) in enumerate(_slices(W, 512)):
        if si >= nsl:
            break
        k = si % 2
        b.w(nc.sync, "st")
        ln_x = b.load(x_s[k][:, :, 0:n], x_in[:, :, o:o + n])
        b.w(nc.vector, "dl", ln_x)
        for c in range(DC):
            emit_ln_apply(b, x_s[k][:, c, 0:n], scr, o, n, xn_s[:, c, 0:n], tmp)
        vn_xn = b.n["ve"]
        if b.dbg_xn is not None:
            b.w(nc.sync, "ve", vn_xn)
            b.store(b.dbg_xn[:, o:o + n], xn_s[:, 0, 0:n])

        rstop = int(os.environ.get("KDEBUG_RSTOP", "9"))
        nt = (n + 127) // 128
        for t in range(nt if "r" in parts else 0):
            tn = min(128, n - t * 128)
            b.w(nc.tensor, "ve", vn_xn)
            b.w(nc.tensor, "sc")
            b.w(nc.tensor, "ve")
            for c in range(DC):
                mm = nc.tensor.matmul(ps_b[0:tn, 0:8],
                                      xn_s[:, c, t * 128:t * 128 + tn],
                                      rws[:, c, :], start=(c == 0), stop=(c == DC - 1))
            pe_r = b.i(mm, "pe")
            b.w(nc.vector, "pe", pe_r)
            ve1 = b.i(nc.vector.reduce_max(r_m1[0:tn, 0:1], ps_b[0:tn, 0:8], axis=AX,
                                           negate=True), "ve")
            b.w(nc.scalar, "ve", ve1)
            sc1 = b.i(nc.scalar.activation(r_pr[0:tn, :], ps_b[0:tn, 0:8], AF.Exp,
                                           bias=r_m1[0:tn, 0:1]), "sc")
            b.w(nc.vector, "sc", sc1)
            if b.dbg_fac is not None and si == 0 and t == 0:
                ve_d = b.i(nc.vector.tensor_copy(fac_b[:, 0:8], ps_b[0:tn, 0:8]), "ve")
                b.w(nc.sync, "ve", ve_d)
                b.store(b.dbg_r[:, 0:8], fac_b[:, 0:8])
                b.store(b.dbg_r[:, 8:16], r_pr[:, 0:8])
                b.store(b.dbg_r[:, 16:17], r_m1[:, 0:1], allow_slow_non_contiguous=True)
                b.w(nc.vector, "st")
            if rstop <= 0:
                continue
            ones8 = b.const_sbuf("ones8", np.ones((128, 8), np.float32))
            zero1 = b.const_sbuf("zero1", np.zeros((128, 1), np.float32))
            nc.vector.reduce_max(r_m1[0:tn, 0:1], r_pr[0:tn, :], axis=AX, negate=True)
            nc.vector.scalar_tensor_tensor(r_t1[0:tn, :], ones8[0:tn, :],
                                           r_m1[0:tn, 0:1], r_pr[0:tn, :],
                                           ALU.mult, ALU.add)
            nc.vector.tensor_scalar(r_ge[0:tn, :], r_t1[0:tn, :], 0.0,
                                    None, ALU.is_ge)
            nc.vector.scalar_tensor_tensor(r_t1[0:tn, :], r_ge[0:tn, :], -1e30,
                                           r_pr[0:tn, :], ALU.mult, ALU.add)
            nc.vector.reduce_max(r_m2[0:tn, 0:1], r_t1[0:tn, :], axis=AX)
            nc.vector.tensor_sub(r_s[0:tn, 0:1], r_m2[0:tn, 0:1], r_m1[0:tn, 0:1])
            nc.vector.tensor_sub(r_m1[0:tn, 0:1], zero1[0:tn, 0:1], r_m1[0:tn, 0:1])
            if b.dbg_fac is not None and si == 0 and t == 0:
                ved = b.i(nc.vector.tensor_copy(fac_b[:, 8:16], r_t1[:, 0:8]), "ve")
                b.w(nc.sync, "ve", ved)
                b.store(b.dbg_r[:, 17:18], r_m1[:, 0:1], allow_slow_non_contiguous=True)
                b.store(b.dbg_r[:, 18:19], r_m2[:, 0:1], allow_slow_non_contiguous=True)
                b.store(b.dbg_r[:, 19:20], r_s[:, 0:1], allow_slow_non_contiguous=True)
                b.w(nc.sync, "ve", ved)
                b.store(b.dbg_r[:, 8:16], fac_b[:, 8:16])
                b.w(nc.vector, "st")
            nc.vector.reciprocal(r_s[0:tn, 0:1], r_s[0:tn, 0:1])
            nc.vector.tensor_mul(r_m1[0:tn, 0:1], r_m1[0:tn, 0:1], r_s[0:tn, 0:1])
            nc.vector.tensor_mul(r_m2[0:tn, 0:1], r_m2[0:tn, 0:1], r_s[0:tn, 0:1])
            ve2 = b.i(nc.vector.tensor_add(fac_tm[0:tn, t:t + 1], r_m1[0:tn, 0:1],
                                           r_m2[0:tn, 0:1]), "ve")
            if tn < 128 and si == 0 and t == 0:
                pass
        if "r" not in parts or rstop <= 1:
            nc.vector.memset(fac_tm[:, 0:4], 1.0)
        b.w(nc.tensor, "ve")
        pe_t = b.i(nc.tensor.transpose(ps_a[0:4, 0:128], fac_tm[:, 0:4], ident), "pe")
        b.w(nc.vector, "pe", pe_t)
        ve3 = b.i(nc.vector.tensor_copy(fac_t2[:, :], ps_a[0:4, 0:128]), "ve")
        b.w(nc.sync, "ve", ve3)
        if b.dbg_fac is not None and si == 0:
            b.store(b.dbg_ftm, fac_tm[:, 0:4])
            b.store(b.dbg_ft2, fac_t2)
        if rstop <= 2:
            nc.vector.memset(fac_row[0:1, 0:n], 1.0)
        st1 = b.store(fac_dram, fac_t2)
        nc.sync.wait_ge(b.sems["st"], st1)
        if rstop > 2:
            ln_f = b.load(fac_row[0:1, 0:n],
                          fac_dram.rearrange("a b -> (a b)").unsqueeze(0)[:, 0:n])
            b.w(nc.tensor, "dl", ln_f)
        b.w(nc.vector, "dl")
        b.w(nc.tensor, "ve")
        b.w(nc.tensor, "sc")
        mm = nc.tensor.matmul(ps_b[:, 0:n], ones_row, fac_row[0:1, 0:n],
                              start=True, stop=True)
        pe_f = b.i(mm, "pe")
        b.w(nc.vector, "pe", pe_f)
        ve_f = b.i(nc.vector.tensor_copy(fac_b[:, 0:n], ps_b[:, 0:n]), "ve")
        if b.dbg_fac is not None:
            b.w(nc.sync, "dl")
            b.store(b.dbg_fac[si:si + 1, 0:n], fac_row[0:1, 0:n])

        last_gelu = {0: 0, 1: 0}
        last_w1pe = [0, 0]
        for fgi in (range(F0 // 512) if "a" in parts else []):
            kk = fgi % 2
            if last_w1pe[kk]:
                b.w(nc.sync, "pe", last_w1pe[kk])
            ln_w = b.load(w1blk[kk], w1b[fgi])
            for f4 in range(4):
                f = fgi * 4 + f4
                pp = f % 2
                ps_h = ps_a if pp == 0 else ps_b
                b.w(nc.tensor, "dl", ln_w)
                b.w(nc.tensor, "ve", max(vn_xn, ve_f))
                if last_gelu[pp]:
                    b.w(nc.tensor, "sc", last_gelu[pp])
                for c in range(DC):
                    mm = nc.tensor.matmul(ps_h[:, 0:n],
                                          w1blk[kk][:, c, f4 * 128:(f4 + 1) * 128],
                                          xn_s[:, c, 0:n],
                                          start=(c == 0), stop=(c == DC - 1))
                pe_h = b.i(mm, "pe")
                last_w1pe[kk] = pe_h
                b.w(nc.scalar, "pe", pe_h)
                last_gelu[pp] = b.i(
                    nc.scalar.activation(gh[:, f, 0:n], ps_h[:, 0:n],
                                         AF.Gelu_apprx_tanh, bias=b1s[:, f:f + 1]),
                    "sc")
        sc_gh = b.n["sc"]
        if b.dbg_fac is not None and si == 0:
            b.w(nc.sync, "sc", sc_gh)
            for ff in range(4):
                b.store(b.dbg_gh[:, ff, :], gh[:, ff * 8, 0:n])

        last_w2pe = [0, 0]
        for half in (range(2) if "b" in parts else []):
            ve_before = b.n["ve"]
            for fg in range(NF // 4):
                kk = fg % 2
                if last_w2pe[kk]:
                    b.w(nc.sync, "pe", last_w2pe[kk])
                ln2 = b.load(w2g[kk], w2b[fg])
                b.w(nc.tensor, "dl", ln2)
                b.w(nc.tensor, "ve", ve_before)
                b.w(nc.tensor, "sc", sc_gh)
                for fr in range(4):
                    f = fg * 4 + fr
                    for ci in range(4):
                        c = half * 4 + ci
                        mm = nc.tensor.matmul(ps_e[:, ci, 0:n],
                                              w2g[kk][:, fr, c * 128:(c + 1) * 128],
                                              gh[:, f, 0:n],
                                              start=(f == 0), stop=(f == NF - 1),
                                              skip_group_check=True)
                last_w2pe[kk] = b.i(mm, "pe")
            pe_eo = b.n["pe"]
            for ci in range(4):
                c = half * 4 + ci
                b.w(nc.vector, "pe", pe_eo)
                nc.vector.tensor_scalar(eo[:, 0:n], ps_e[:, ci, 0:n],
                                        b2s[:, c:c + 1], None, ALU.add)
                nc.vector.tensor_mul(t1[:, 0:n], eo[:, 0:n], fac_b[:, 0:n])
                if b.dbg_fac is not None and si == 0 and c == 0:
                    ved2 = b.i(nc.vector.tensor_copy(b2s_dbg[:, 0:n], eo[:, 0:n]), "ve")
                    b.w(nc.sync, "ve", ved2)
                    b.store(b.dbg_eo[:, 0, 0:256], b2s_dbg[:, 0:256])
                    b.store(b.dbg_eo[:, 1, 0:256], b2s_dbg[:, 256:512])
                    b.w(nc.vector, "st")
                ve_o = b.i(
                    nc.vector.scalar_tensor_tensor(x_s[k][:, c, 0:n],
                                                   t1[:, 0:n], 0.5,
                                                   x_s[k][:, c, 0:n],
                                                   ALU.mult, ALU.add), "ve")
                b.w(nc.sync, "ve", ve_o)
                b.store(x_out[:, c, o:o + n], x_s[k][:, c, 0:n])
        b.barrier()


def _pm(w, *resh_perm):
    resh, perm, out = resh_perm
    a = np.ascontiguousarray(np.asarray(w, np.float32).reshape(resh).transpose(perm))
    return np.ascontiguousarray(a.reshape(out))


def _fm_vec(vec, nchunk):
    return np.ascontiguousarray(
        np.asarray(vec, np.float32).reshape(nchunk, 128).T)


def _shard_x(x0):
    x0T = np.ascontiguousarray(np.asarray(x0, np.float32).transpose(0, 2, 1))
    out = []
    for c in range(NC):
        bb, kk = c // 4, c % 4
        lo = kk * T_OWN - HALO
        sl = np.zeros((D, T_EXT), np.float32)
        src_lo = max(lo, 0)
        sl[:, src_lo - lo:] = x0T[bb, :, src_lo:(kk + 1) * T_OWN]
        out.append(np.ascontiguousarray(
            sl.reshape(DC, 128, T_EXT).transpose(1, 0, 2)))
    return out


def _unshard_x(parts, width, col0):
    x = np.zeros((2, 8192, D), np.float32)
    for c in range(NC):
        bb, kk = c // 4, c % 4
        sl = np.asarray(parts[c], np.float32)[:, :, col0:col0 + T_OWN]
        x[bb, kk * T_OWN:(kk + 1) * T_OWN, :] = (
            sl.transpose(1, 0, 2).reshape(D, T_OWN).T)
    return x


def run_l0ffn_stage(xparts, W, col_off, inputs, counter=[0]):
    nc = build_l0ffn(W)
    wp = _prep_weights(inputs)
    in_maps = []
    for c in range(NC):
        in_maps.append(dict(
            x_in=np.ascontiguousarray(xparts[c][:, :, col_off:col_off + W]),
            w1=wp["w1"], w2=wp["w2"], rw=wp["rw0"], b1=wp["b1"], b2=wp["b2"]))
    res = run_bass_kernel_spmd(nc, in_maps, core_ids=list(range(NC)))
    return [np.asarray(res.results[c]["x_out"]) for c in range(NC)]


def _prep_weights(inputs):
    wp = {}
    wp["w1"] = _pm(inputs["L0_ffn_w1"], (DC, 128, 8, 512), (2, 1, 0, 3),
                   (8, 128, DC * 512))
    wp["w2"] = _pm(inputs["L0_ffn_w2"], (8, 4, 128, D), (0, 2, 1, 3),
                   (8, 128, 4 * D))
    wp["rw0"] = np.ascontiguousarray(np.asarray(inputs["L0_ffn_router_w"], np.float32))
    wp["b1"] = _fm_vec(inputs["L0_ffn_b1"], F0 // 128)
    wp["b2"] = _fm_vec(inputs["L0_ffn_b2"], DC)
    for nm, key in (("wq", "L0_attn_Wq"), ("wk", "L0_attn_Wk"),
                    ("wv", "L0_attn_Wv"), ("wo", "L0_attn_Wo")):
        wp[nm] = _pm(np.asarray(inputs[key], np.float32).T,
                     (DC, 128, D), (1, 0, 2), (1, 128, DC * D))
    wp["bq"] = _fm_vec(inputs["L0_attn_bq"], DC)
    wp["bk"] = _fm_vec(inputs["L0_attn_bk"], DC)
    wp["bv"] = np.asarray(inputs["L0_attn_bv"], np.float32).reshape(1, D)
    wp["bo"] = _fm_vec(inputs["L0_attn_bo"], DC)
    wp["rw1"] = _pm(np.asarray(inputs["L1_ffn_router_w"], np.float32).T,
                    (DC, 128, NE), (1, 0, 2), (1, 128, DC * NE))
    wp["rbias"] = np.ascontiguousarray(np.broadcast_to(
        np.asarray(inputs["L1_ffn_bias"], np.float32).reshape(1, NE), (128, NE)))
    wp["gw"] = _pm(inputs["L1_ffn_gate_w"], (NE, DC, 128, D), (0, 2, 1, 3),
                   (NE, 128, DC * D))
    wp["uw"] = _pm(inputs["L1_ffn_up_w"], (NE, DC, 128, D), (0, 2, 1, 3),
                   (NE, 128, DC * D))
    wp["dw"] = _pm(inputs["L1_ffn_down_w"], (NE, DC, 128, D), (0, 2, 1, 3),
                   (NE, 128, DC * D))
    wp["shg"] = _pm(inputs["L1_ffn_sh_gate_w"], (DC, 128, 2, D), (2, 1, 0, 3),
                    (2, 128, DC * D))
    wp["shu"] = _pm(inputs["L1_ffn_sh_up_w"], (DC, 128, 2, D), (2, 1, 0, 3),
                    (2, 128, DC * D))
    wp["shd"] = _pm(inputs["L1_ffn_sh_down_w"], (2, DC, 128, D), (0, 2, 1, 3),
                    (2, 128, DC * D))
    return wp


def kernel_multilaunch(inputs):
    x0 = np.asarray(inputs["L0_attn_hidden_state"], np.float32)
    xp = _shard_x(x0)
    x1 = run_l0ffn_stage(xp, T_EXT, 0, inputs)
    x2 = run_attn_stage(x1, inputs)
    x3 = run_l0ffn_stage(x2, W_ATT, 0, inputs)
    x4 = run_l1ffn_stage(x3, W_ATT, 0, inputs)
    x4m = [p[:, :, W_ATT - W_M:] for p in x4]
    x5 = run_mamba_stage(x4m, inputs)
    x6 = run_l1ffn_stage(x5, T_OWN, 0, inputs)
    return _unshard_x(x6, T_OWN, 0)


def build_attn_a():
    b = B()
    W = T_EXT
    b.reg_fconst(LN_EPS)
    x_in = b.inp("x_in", [128, DC, W])
    wq = b.inp("wq", [1, 128, DC * D])
    wk = b.inp("wk", [1, 128, DC * D])
    wv = b.inp("wv", [1, 128, DC * D])
    bq = b.inp("bq", [128, DC])
    bk = b.inp("bk", [128, DC])
    bv = b.inp("bv", [1, D])
    cosq = b.inp("cosq", [128, 4, W])
    sinq = b.inp("sinq", [128, 4, W])
    cosk = b.inp("cosk", [128, 4, W])
    sink = b.inp("sink", [128, 4, W])
    q_out = b.out("q_out", [128, DC, W_ATT], BF16)
    k_out = b.out("k_out", [128, DC, W], BF16)
    v_out = b.out("v_out", [128, W // 128, D], BF16)
    wqb = b.dram("wqb", [1, 128, DC * D], BF16)
    wkb = b.dram("wkb", [1, 128, DC * D], BF16)
    wvb = b.dram("wvb", [1, 128, DC * D], BF16)
    emit_wcast(b, [(wq, wqb), (wk, wkb), (wv, wvb)])
    with b.scope():
        emit_attn_a_stage(b, x_in, q_out, k_out, v_out, wqb, wkb, wvb,
                          bq, bk, bv, cosq, sinq, cosk, sink)
    return b.finish()


def emit_attn_a_stage(b, x_in, q_out, k_out, v_out, wqb, wkb, wvb,
                      bq, bk, bv, cosq, sinq, cosk, sink):
    nc = b.nc
    W = T_EXT
    QOFF = HALO - 128
    ones_row = _ones_row(b)
    ps_a = b.psum("ps_a", [128, 512])
    ps_b = b.psum("ps_b", [128, 512])
    ps_v = b.psum("ps_v", [128, 1024])

    scr = alloc_ln_scratch(b, W)
    emit_ln_stats(b, x_in, W, scr, ps_a, ps_b)

    bqs = b.sbuf("bqs", [128, DC]); b.load(bqs, bq)
    bks = b.sbuf("bks", [128, DC]); b.load(bks, bk)
    bvs = b.sbuf("bvs", [1, D]);    b.load(bvs, bv)
    b.barrier()

    x_s = b.sbuf("a_x", [128, DC, 512])
    xn_s = b.sbuf("a_xn", [128, DC, 512], BF16)
    tmp = b.sbuf("a_tmp", [128, 512])
    tmp2 = b.sbuf("a_tmp2", [128, 512])
    kraw = b.sbuf("a_kraw", [128, DC, 512])
    rope_o = b.sbuf("a_ro", [128, DC, 512], BF16)
    trig = [b.sbuf(f"a_trig{i}", [128, 512]) for i in range(2)]
    wcur = b.sbuf("a_wcur", [128, DC, D], BF16)
    v_bf = b.sbuf("a_vbf", [128, 1024], BF16)

    for (o, n) in _slices(W, 512):
        st_before = b.n["st"]
        b.w(nc.sync, "ve")
        b.w(nc.sync, "pe")
        ln_x = b.load(x_s[:, :, 0:n], x_in[:, :, o:o + n])
        b.w(nc.vector, "dl", ln_x)
        b.w(nc.vector, "pe")
        for c in range(DC):
            emit_ln_apply(b, x_s[:, c, 0:n], scr, o, n, xn_s[:, c, 0:n], tmp)
        ve_xn = b.n["ve"]

        for proj, wb, bias_s, cos_in, sin_in, out_hbm, q_off in (
                ("k", wkb, bks, cosk, sink, k_out, 0),
                ("q", wqb, bqs, cosq, sinq, q_out, QOFF)):
            if o + n <= q_off:
                continue
            b.w(nc.sync, "pe")
            lw = b.load(wcur, wb[0])
            b.w(nc.tensor, "dl", lw)
            for oc in range(DC):
                kk = oc % 2
                ps = ps_a if kk == 0 else ps_b
                b.w(nc.tensor, "ve", ve_xn)
                b.w(nc.tensor, "ve")
                for ic in range(DC):
                    mm = nc.tensor.matmul(ps[:, 0:n],
                                          wcur[:, ic, oc * 128:(oc + 1) * 128],
                                          xn_s[:, ic, 0:n],
                                          start=(ic == 0), stop=(ic == DC - 1))
                pe_k = b.i(mm, "pe")
                b.w(nc.vector, "pe", pe_k)
                b.i(nc.vector.tensor_scalar(kraw[:, oc, 0:n], ps[:, 0:n],
                                            bias_s[:, oc:oc + 1], None, ALU.add),
                    "ve")
            oo = o - q_off
            for c in range(4):
                b.w(nc.sync, "ve")
                lc = b.load(trig[0][:, 0:n], cos_in[:, c, o:o + n])
                ls = b.load(trig[1][:, 0:n], sin_in[:, c, o:o + n])
                b.w(nc.vector, "dl", ls)
                b.w(nc.vector, "st", st_before)
                nc.vector.tensor_mul(tmp[:, 0:n], kraw[:, c, 0:n], trig[0][:, 0:n])
                nc.vector.tensor_mul(tmp2[:, 0:n], kraw[:, c + 4, 0:n], trig[1][:, 0:n])
                b.i(nc.vector.tensor_sub(rope_o[:, c, 0:n], tmp[:, 0:n],
                                         tmp2[:, 0:n]), "ve")
                nc.vector.tensor_mul(tmp[:, 0:n], kraw[:, c, 0:n], trig[1][:, 0:n])
                nc.vector.tensor_mul(tmp2[:, 0:n], kraw[:, c + 4, 0:n], trig[0][:, 0:n])
                b.i(nc.vector.tensor_add(rope_o[:, c + 4, 0:n], tmp[:, 0:n],
                                         tmp2[:, 0:n]), "ve")
            b.w(nc.sync, "ve")
            b.store(out_hbm[:, :, oo:oo + n], rope_o[:, :, 0:n])

        b.w(nc.sync, "pe")
        lw = b.load(wcur, wvb[0])
        b.w(nc.tensor, "dl", lw)
        nt = (n + 127) // 128
        for t in range(nt):
            tn = min(128, n - t * 128)
            vt = (o + t * 128) // 128
            for ov in range(2):
                b.w(nc.tensor, "ve", ve_xn)
                b.w(nc.tensor, "ve")
                for ic in range(DC):
                    mm = nc.tensor.matmul(ps_v[0:tn, ov * 512:(ov + 1) * 512],
                                          xn_s[:, ic, t * 128:t * 128 + tn],
                                          wcur[:, ic, ov * 512:(ov + 1) * 512],
                                          start=(ic == 0), stop=False,
                                          skip_group_check=True)
                mm = nc.tensor.matmul(ps_v[0:tn, ov * 512:(ov + 1) * 512],
                                      ones_row[:, 0:tn],
                                      bvs[:, ov * 512:(ov + 1) * 512],
                                      start=False, stop=True, skip_group_check=True)
                pe_v = b.i(mm, "pe")
                b.w(nc.vector, "pe", pe_v)
                ve_v = b.i(nc.vector.tensor_copy(v_bf[0:tn, ov * 512:(ov + 1) * 512],
                                                 ps_v[0:tn, ov * 512:(ov + 1) * 512]),
                           "ve")
                b.w(nc.sync, "ve", ve_v)
                b.store(v_out[0:tn, vt, ov * 512:(ov + 1) * 512],
                        v_bf[0:tn, ov * 512:(ov + 1) * 512])


def build_attn_b():
    b = B()
    W = T_EXT
    NT = W_ATT // 128
    b.reg_fconst(1.0)
    x_in = b.inp("x_in", [128, DC, W])
    q_in = b.inp("q_in", [128, DC, W_ATT], BF16)
    k_in = b.inp("k_in", [128, DC, W], BF16)
    v_in = b.inp("v_in", [128, W // 128, D], BF16)
    wo = b.inp("wo", [1, 128, DC * D])
    bo = b.inp("bo", [128, DC])
    maskA = b.inp("maskA", [NT, 128, HALO])
    maskT = b.inp("maskT", [NT, 128, HALO])
    x_out = b.out("x_out", [128, DC, W_ATT])
    wob = b.dram("wob", [1, 128, DC * D], BF16)
    emit_wcast(b, [(wo, wob)])
    with b.scope():
        emit_attn_b_stage(b, x_in, q_in, k_in, v_in, x_out, wob, bo,
                          maskA, maskT)
    return b.finish()


def emit_attn_b_stage(b, x_in, q_in, k_in, v_in, x_out, wob, bo, maskA, maskT):
    nc = b.nc
    W = T_EXT
    NT = W_ATT // 128
    ident = _ident128(b)
    ones_row = _ones_row(b)
    ps_a = b.psum("ps_a", [128, 512])
    ps_A = b.psum("ps_A", [128, 1024])
    ps_T = b.psum("ps_T", [128, 1024])
    ps_o = b.psum("ps_o", [128, 256])

    bos = b.sbuf("bos", [128, DC]); b.load(bos, bo)
    wo_r = b.sbuf("b_wo", [128, DC, D], BF16)
    b.load(wo_r, wob[0])

    k_t = [b.sbuf(f"b_k{i}", [128, DC, HALO], BF16) for i in range(2)]
    v_t = [b.sbuf(f"b_v{i}", [128, 5, D], BF16) for i in range(2)]
    q_t = [b.sbuf(f"b_q{i}", [128, DC, 128], BF16) for i in range(2)]
    mA_t = [b.sbuf(f"b_mA{i}", [128, HALO]) for i in range(2)]
    mT_t = [b.sbuf(f"b_mT{i}", [128, HALO]) for i in range(2)]
    x_t = [b.sbuf(f"b_x{i}", [128, DC, 128]) for i in range(2)]
    pT = b.sbuf("b_pT", [128, HALO], BF16)
    pA_junk = b.sbuf("b_pA", [128, HALO], BF16)
    av = b.sbuf("b_av", [128, DC, 128], BF16)
    Zr = b.sbuf("b_Z", [128, 1])
    mneg = b.sbuf("b_mneg", [128, 1])
    cneg = b.sbuf("b_cneg", [128, 1])
    crow = b.sbuf("b_crow", [1, 128])
    t1 = b.sbuf("b_t1", [128, 128])
    b.barrier()

    last_use = [0, 0]
    for ti in range(NT):
        kk = ti % 2
        if last_use[kk]:
            b.w(nc.sync, "pe", last_use[kk])
            b.w(nc.sync, "ve")
            b.w(nc.sync, "st")
        kv0 = ti * 128
        ln1 = b.load(k_t[kk][:, :, :], k_in[:, :, kv0:kv0 + HALO])
        b.load(v_t[kk].rearrange("p f d -> p (f d)"),
               v_in[:, kv0 // 128:kv0 // 128 + 5, :].rearrange("p f d -> p (f d)"))
        b.load(q_t[kk][:, :, :], q_in[:, :, ti * 128:(ti + 1) * 128])
        b.load(mA_t[kk], maskA[ti])
        b.load(mT_t[kk], maskT[ti])
        ln2 = b.load(x_t[kk][:, :, :], x_in[:, :, HALO - 128 + ti * 128:
                                            HALO - 128 + (ti + 1) * 128])
        b.w(nc.tensor, "dl", ln2)
        for h in range(16):
            hc, hr = h // 2, (h % 2) * 64
            b.w(nc.tensor, "ve")
            b.w(nc.tensor, "sc")
            for j, (ko, kn) in enumerate(_slices(HALO, 512)):
                mm = nc.tensor.matmul(ps_A[:, ko:ko + kn],
                                      q_t[kk][hr:hr + 64, hc, :],
                                      k_t[kk][hr:hr + 64, hc, ko:ko + kn],
                                      start=True, stop=True)
            pe_s = b.i(mm, "pe")
            b.w(nc.vector, "pe", pe_s)
            nc.vector.tensor_add(ps_A[:, 0:HALO], ps_A[:, 0:HALO], mA_t[kk])
            b.i(nc.vector.reduce_max(mneg, ps_A[:, 0:HALO], axis=AX, negate=True),
                "ve")
            b.w(nc.scalar, "ve")
            sc_e = b.i(nc.scalar.activation(pA_junk, ps_A[:, 0:HALO], AF.Exp,
                                            bias=mneg, accum_out=Zr), "sc")
            b.w(nc.scalar, "sc", sc_e)
            sc_l = b.i(nc.scalar.activation(Zr, Zr, AF.Ln), "sc")
            b.w(nc.vector, "sc", sc_l)
            nc.vector.tensor_sub(cneg, mneg, Zr)
            ve_c = b.i(nc.vector.tensor_scalar(cneg, cneg, 1e37, None, ALU.min),
                       "ve")
            b.w(nc.tensor, "ve", ve_c)
            pe_t = b.i(nc.tensor.transpose(ps_o[0:1, 128:256], cneg, ident), "pe")
            b.w(nc.vector, "pe", pe_t)
            ve_cr = b.i(nc.vector.tensor_copy(crow, ps_o[0:1, 128:256]), "ve")
            b.w(nc.tensor, "ve", ve_cr)
            b.w(nc.tensor, "sc")
            for j in range(5):
                nc.tensor.matmul(ps_T[:, j * 128:(j + 1) * 128],
                                 k_t[kk][hr:hr + 64, hc, j * 128:(j + 1) * 128],
                                 q_t[kk][hr:hr + 64, hc, :],
                                 start=True, stop=False, skip_group_check=True)
                mm = nc.tensor.matmul(ps_T[:, j * 128:(j + 1) * 128],
                                      ones_row, crow, start=False, stop=True,
                                      skip_group_check=True)
            pe_st = b.i(mm, "pe")
            b.w(nc.vector, "pe", pe_st)
            ve_m = b.i(nc.vector.tensor_add(ps_T[:, 0:HALO], ps_T[:, 0:HALO],
                                            mT_t[kk]), "ve")
            b.w(nc.scalar, "ve", ve_m)
            sc_p = b.i(nc.scalar.activation(pT, ps_T[:, 0:HALO], AF.Exp), "sc")
            b.w(nc.tensor, "sc", sc_p)
            if h % 2 == 0:
                b.w(nc.tensor, "ve")
            for j in range(5):
                mm = nc.tensor.matmul(ps_o[hr:hr + 64, 0:128],
                                      v_t[kk][:, j, h * 64:(h + 1) * 64],
                                      pT[:, j * 128:(j + 1) * 128],
                                      start=(j == 0), stop=(j == 4),
                                      skip_group_check=True)
            pe_pv = b.i(mm, "pe")
            if h % 2 == 1:
                b.w(nc.vector, "pe", pe_pv)
                b.i(nc.vector.tensor_copy(av[:, hc, :], ps_o[:, 0:128]), "ve")
        ve_av = b.n["ve"]
        for oc in range(DC):
            b.w(nc.tensor, "ve", ve_av)
            b.w(nc.tensor, "ve")
            for fc in range(DC):
                mm = nc.tensor.matmul(ps_a[:, 0:128], wo_r[:, fc, oc * 128:(oc + 1) * 128],
                                      av[:, fc, :], start=(fc == 0), stop=(fc == DC - 1))
            pe_o2 = b.i(mm, "pe")
            b.w(nc.vector, "pe", pe_o2)
            nc.vector.tensor_scalar(t1, ps_a[:, 0:128], bos[:, oc:oc + 1], None,
                                    ALU.add)
            ve_o = b.i(nc.vector.tensor_add(x_t[kk][:, oc, :], x_t[kk][:, oc, :],
                                            t1), "ve")
            b.w(nc.sync, "ve", ve_o)
            b.store(x_out[:, oc, ti * 128:(ti + 1) * 128], x_t[kk][:, oc, :])
        last_use[kk] = b.n["pe"]


def _rope_tables(positions, scale_q):
    half = 512
    inv = 1.0 / (10000.0 ** (np.arange(half, dtype=np.float64) / half))
    ang = positions.astype(np.float64)[None, :] * inv[:, None]
    cos = np.cos(ang).astype(np.float32) * scale_q
    sin = np.sin(ang).astype(np.float32) * scale_q
    W = positions.shape[0]
    return (np.ascontiguousarray(cos.reshape(4, 128, W).transpose(1, 0, 2)),
            np.ascontiguousarray(sin.reshape(4, 128, W).transpose(1, 0, 2)))


def _attn_masks(core):
    NT = W_ATT // 128
    bb, kk = core // 4, core % 4
    t0g = kk * T_OWN
    mA = np.zeros((NT, 128, HALO), np.float32)
    mT = np.zeros((NT, 128, HALO), np.float32)
    for ti in range(NT):
        qg = t0g - 128 + ti * 128 + np.arange(128)
        kg = t0g - HALO + ti * 128 + np.arange(HALO)
        valid = ((kg[None, :] <= qg[:, None])
                 & (kg[None, :] >= qg[:, None] - 512)
                 & (kg[None, :] >= 0) & (qg[:, None] >= 0))
        mA[ti] = np.where(valid, 0.0, NEG)
        vT = valid.reshape(128, 5, 128)
        mT[ti] = np.where(vT.transpose(2, 1, 0).reshape(128, HALO)
                          [: , :], 0.0, NEG)
    return mA, mT


def run_attn_stage(xparts, inputs):
    nc_a = build_attn_a()
    nc_b = build_attn_b()
    wp = _prep_weights(inputs)
    im_a, masks = [], []
    for c in range(NC):
        kk = c % 4
        pos = np.maximum(kk * T_OWN - HALO + np.arange(T_EXT), 0)
        cq, sq = _rope_tables(pos, 0.125)
        ck, sk = _rope_tables(pos, 1.0)
        mA, mT = _attn_masks(c)
        masks.append((mA, mT))
        im_a.append(dict(x_in=np.ascontiguousarray(xparts[c]), wq=wp["wq"],
                         wk=wp["wk"], wv=wp["wv"], bq=wp["bq"], bk=wp["bk"],
                         bv=wp["bv"], cosq=cq, sinq=sq, cosk=ck, sink=sk))
    res_a = run_bass_kernel_spmd(nc_a, im_a, core_ids=list(range(NC)))
    im_b = []
    for c in range(NC):
        ra = res_a.results[c]
        mA, mT = masks[c]
        im_b.append(dict(x_in=np.ascontiguousarray(xparts[c]),
                         q_in=np.asarray(ra["q_out"]), k_in=np.asarray(ra["k_out"]),
                         v_in=np.asarray(ra["v_out"]), wo=wp["wo"], bo=wp["bo"],
                         maskA=mA, maskT=mT))
    res_b = run_bass_kernel_spmd(nc_b, im_b, core_ids=list(range(NC)))
    return [np.asarray(res_b.results[c]["x_out"]) for c in range(NC)]


def emit_ln_to_xn(b, x_src, W, xn, ps_a, ps_b, tag="lnx"):
    nc = b.nc
    with ExitStack() as lctx:
        def sb(name, shape, dtype=F32):
            return lctx.enter_context(
                b.rawnc.sbuf_tensor(f"{tag}{name}", shape, dtype)).ap()
        scr = dict(
            xcast=[sb(f"xc{i}", [128, 512], BF16) for i in range(2)],
            sq=[sb(f"sq{i}", [128, 512], BF16) for i in range(2)],
            xs=[sb(f"xs{i}", [128, 512], F32) for i in range(2)],
            alpha_r=sb("ar", [1, W]),
            beta_r=sb("br", [1, W]),
            alpha_b=sb("ab", [128, W]),
            beta_b=sb("bb", [128, W]),
        )
        emit_ln_stats(b, x_src, W, scr, ps_a, ps_b)
        tmp = sb("tmp", [128, 512])
        xs2 = sb("xs2", [128, 512])
        for (o, n) in _slices(W, 512):
            for c in range(DC):
                ln = b.load(xs2[:, 0:n], x_src[:, c, o:o + n])
                b.w(nc.vector, "dl", ln)
                ve = emit_ln_apply(b, xs2[:, 0:n], scr, o, n, xn[:, c, o:o + n], tmp)
                b.w(nc.sync, "ve", ve)
        b.barrier()


def build_l1ffn(W):
    b = B()
    b.reg_fconst(LN_EPS)
    x_in = b.inp("x_in", [128, DC, W])
    rw = b.inp("rw", [1, 128, DC * NE])
    rbias = b.inp("rbias", [128, NE])
    gw = b.inp("gw", [NE, 128, DC * D])
    uw = b.inp("uw", [NE, 128, DC * D])
    dw = b.inp("dw", [NE, 128, DC * D])
    shg = b.inp("shg", [2, 128, DC * D])
    shu = b.inp("shu", [2, 128, DC * D])
    shd = b.inp("shd", [2, 128, DC * D])
    x_out = b.out("x_out", [128, DC, W])
    rwb = b.dram("rwb", [1, 128, DC * NE], BF16)
    gwb = b.dram("gwb", [NE, 128, DC * D], BF16)
    uwb = b.dram("uwb", [NE, 128, DC * D], BF16)
    dwb = b.dram("dwb", [NE, 128, DC * D], BF16)
    shgb = b.dram("shgb", [2, 128, DC * D], BF16)
    shub = b.dram("shub", [2, 128, DC * D], BF16)
    shdb = b.dram("shdb", [2, 128, DC * D], BF16)
    emit_wcast(b, [(rw, rwb), (gw, gwb), (uw, uwb), (dw, dwb),
                   (shg, shgb), (shu, shub), (shd, shdb)])
    with b.scope():
        emit_l1ffn_stage(b, x_in, x_out, W, rwb, gwb, uwb, dwb,
                         shgb, shub, shdb, rbias, tag="l1f")
    return b.finish()


def emit_l1ffn_stage(b, x_in, x_out, W, rwb, gwb, uwb, dwb, shgb, shub, shdb,
                     rbias, tag="l1f"):
    nc = b.nc
    NT = W // 128
    ident = _ident128(b)
    ones_row = _ones_row(b)
    ps_h = [b.psum(f"ps_h{i}", [128, 512]) for i in range(2)]
    ps_u = [b.psum(f"ps_u{i}", [128, 512]) for i in range(2)]
    ps_y = [b.psum(f"ps_y{i}", [128, 512]) for i in range(2)]

    _stat_lhs(b)
    ones4 = b.const_sbuf("ones4", np.ones((128, 4), np.float32))
    ones16 = b.const_sbuf("ones16", np.ones((128, 16), np.float32))
    xn = b.sbuf("f_xn", [128, DC, W], BF16)
    emit_ln_to_xn(b, x_in, W, xn, ps_h[0], ps_u[0])

    rws = b.sbuf("f_rws", [128, DC, NE], BF16)
    b.load(rws, rwb[0])
    bias_b = b.sbuf("f_biasb", [128, NE])
    ln_b = b.load(bias_b, rbias)
    sc_s = b.sbuf("f_sc", [128, NE])
    s_s = b.sbuf("f_s", [128, NE])
    ms_s = b.sbuf("f_ms", [128, NE])
    t16 = b.sbuf("f_t16", [128, NE])
    ge16 = b.sbuf("f_ge16", [128, NE])
    cur16 = b.sbuf("f_cur", [128, NE])
    gsc = b.sbuf("f_gsc", [128, 4])
    t4 = b.sbuf("f_t4", [128, 4])
    ge4 = b.sbuf("f_ge4", [128, 4])
    gmask = b.sbuf("f_gm", [128, 4])
    m1n = b.sbuf("f_m1n", [128, 1])
    m2r = b.sbuf("f_m2", [128, 1])
    tsum = b.sbuf("f_tsum", [128, 1])
    comb = b.sbuf("f_comb", [128, NE])
    c_T = b.sbuf("f_cT", [16, W])
    c_dram = b.dram(f"f_cdram_{tag}", [NE, W])
    b.w(nc.vector, "dl")
    b.barrier()

    for ti in range(NT):
        t0 = ti * 128
        b.w(nc.tensor, "ve")
        b.w(nc.tensor, "sc")
        for ic in range(DC):
            mm = nc.tensor.matmul(ps_h[0][:, 0:NE], xn[:, ic, t0:t0 + 128],
                                  rws[:, ic, :], start=(ic == 0),
                                  stop=(ic == DC - 1))
        pe_r = b.i(mm, "pe")
        b.w(nc.scalar, "pe", pe_r)
        sc1 = b.i(nc.scalar.activation(sc_s, ps_h[0][:, 0:NE], AF.Sigmoid), "sc")
        b.w(nc.vector, "sc", sc1)
        nc.vector.tensor_add(s_s, sc_s, bias_b)
        for g in range(4):
            sg = s_s[:, g * 4:(g + 1) * 4]
            nc.vector.reduce_max(m1n, sg, axis=AX, negate=True)
            nc.vector.scalar_tensor_tensor(t4, ones4[:, 0:4], m1n, sg,
                                           ALU.mult, ALU.add)
            nc.vector.tensor_scalar(ge4, t4, 0.0, None, ALU.is_ge)
            nc.vector.scalar_tensor_tensor(t4, ge4, -1e30, sg, ALU.mult, ALU.add)
            nc.vector.reduce_max(m2r, t4, axis=AX)
            nc.vector.tensor_sub(gsc[:, g:g + 1], m2r, m1n)
        nc.vector.reduce_max(m1n, gsc, axis=AX, negate=True)
        nc.vector.scalar_tensor_tensor(t4, ones4[:, 0:4], m1n, gsc,
                                       ALU.mult, ALU.add)
        nc.vector.tensor_scalar(ge4, t4, 0.0, None, ALU.is_ge)
        nc.vector.scalar_tensor_tensor(t4, ge4, -1e30, gsc, ALU.mult, ALU.add)
        nc.vector.reduce_max(m2r, t4, axis=AX, negate=True)
        nc.vector.scalar_tensor_tensor(t4, ones4[:, 0:4], m2r, gsc,
                                       ALU.mult, ALU.add)
        nc.vector.tensor_scalar(gmask, t4, 0.0, None, ALU.is_ge)
        nc.vector.tensor_scalar(t16, s_s, 1.0, None, ALU.add)
        for g in range(4):
            nc.vector.tensor_scalar(ge16[:, g * 4:(g + 1) * 4],
                                    t16[:, g * 4:(g + 1) * 4],
                                    gmask[:, g:g + 1], None, ALU.mult)
        nc.vector.tensor_scalar(ms_s, ge16, -1.0, None, ALU.add)
        nc.vector.tensor_copy(cur16, ms_s)
        for it in range(3):
            nc.vector.reduce_max(m1n, cur16, axis=AX, negate=True)
            nc.vector.scalar_tensor_tensor(t16, ones16, m1n, cur16,
                                           ALU.mult, ALU.add)
            nc.vector.tensor_scalar(ge16, t16, 0.0, None, ALU.is_ge)
            nc.vector.scalar_tensor_tensor(cur16, ge16, -1e30, cur16,
                                           ALU.mult, ALU.add)
        nc.vector.reduce_max(m1n, cur16, axis=AX, negate=True)
        nc.vector.scalar_tensor_tensor(t16, ones16, m1n, ms_s, ALU.mult, ALU.add)
        nc.vector.tensor_scalar(ge16, t16, 0.0, None, ALU.is_ge)
        nc.vector.tensor_mul(t16, sc_s, ge16)
        nc.vector.reduce_sum(tsum, t16, axis=AX)
        nc.vector.tensor_scalar(tsum, tsum, 1e-20, None, ALU.add)
        nc.vector.reciprocal(tsum, tsum)
        ve_c = b.i(nc.vector.tensor_scalar(comb, t16, tsum, 2.5,
                                           ALU.mult, ALU.mult), "ve")
        b.w(nc.tensor, "ve", ve_c)
        pe_t = b.i(nc.tensor.transpose(ps_u[0][0:16, 0:128], comb, ident), "pe")
        b.w(nc.vector, "pe", pe_t)
        b.i(nc.vector.tensor_copy(c_T[:, t0:t0 + 128], ps_u[0][0:16, 0:128]), "ve")
    b.w(nc.sync, "ve")
    st_c = b.store(c_dram, c_T)
    nc.sync.wait_ge(b.sems["st"], st_c)
    b.barrier()

    NEX = NE + 2
    w_g = [b.sbuf(f"f_wg{i}", [128, DC, D], BF16) for i in range(2)]
    w_u = [b.sbuf(f"f_wu{i}", [128, DC, D], BF16) for i in range(2)]
    w_d = [b.sbuf(f"f_wd{i}", [128, DC, D], BF16) for i in range(2)]
    gbuf = b.sbuf("f_gbuf", [128, DC, 512], BF16)
    sh_b = b.sbuf("f_shb", [128, 512], BF16)
    yc = [b.sbuf(f"f_yc{i}", [128, 512]) for i in range(2)]
    c_row = b.sbuf("f_crow", [1, W])
    WH = (W + 1023) // 1024 * 512 if False else (W + 1) // 2
    c_b = b.sbuf("f_cb", [128, WH + 8])
    racc_t = [b.sbuf(f"f_racc{c}", [128, WH + 8]) for c in range(DC)]
    xs2 = b.sbuf("f_xs2", [128, 512])

    def load_expert(e, kb):
        if e >= NEX:
            return 0
        if e >= NE:
            h = e - NE
            b.load(w_g[kb], shgb[h])
            b.load(w_u[kb], shub[h])
            lw = b.load(w_d[kb], shdb[h])
        else:
            b.load(w_g[kb], gwb[e])
            b.load(w_u[kb], uwb[e])
            lw = b.load(w_d[kb], dwb[e])
        return lw

    last_use_w = [0, 0]
    dl_w = [0, 0]
    dl_w[0] = load_expert(0, 0)
    for half in range(2):
        h0 = half * WH
        hw = min(WH, W - h0)
        for c in range(DC):
            b.i(nc.gpsimd.memset(racc_t[c][:, 0:hw], 0.0), "po")
        for e in range(NEX):
            kb = e % 2
            shared = (e >= NE)
            en = e + 1 if e + 1 < NEX else (0 if half == 0 else NEX)
            if en < NEX and (half == 0 or e + 1 < NEX):
                if last_use_w[1 - kb]:
                    b.w(nc.sync, "pe", last_use_w[1 - kb])
                dl_w[1 - kb] = load_expert(en if e + 1 < NEX else 0, 1 - kb)
            if not shared:
                b.w(nc.sync, "ve")
                lc = b.load(c_row[:, 0:hw], c_dram[e].unsqueeze(0)[:, h0:h0 + hw])
                b.w(nc.tensor, "dl", lc)
                b.w(nc.tensor, "ve")
                b.w(nc.tensor, "po")
                for (o, n) in _slices(hw, 512):
                    mm = nc.tensor.matmul(ps_y[0][:, 0:n], ones_row,
                                          c_row[:, o:o + n], start=True, stop=True)
                    pe_c = b.i(mm, "pe")
                    b.w(nc.vector, "pe", pe_c)
                    ve_cb = b.i(nc.vector.tensor_copy(c_b[:, o:o + n],
                                                      ps_y[0][:, 0:n]), "ve")
                    b.w(nc.tensor, "ve", ve_cb)
            b.w(nc.tensor, "dl", dl_w[kb])
            sc_par = {0: 0, 1: 0}
            ve_par = {0: 0, 1: 0}
            vey_par = {0: 0, 1: 0}
            for (o, n) in _slices(hw, 512):
                oo = h0 + o
                ve_g = {}
                for f in range(DC):
                    p = f % 2
                    if sc_par[p]:
                        b.w(nc.tensor, "sc", sc_par[p])
                    if ve_par[p]:
                        b.w(nc.tensor, "ve", ve_par[p])
                    for ic in range(DC):
                        nc.tensor.matmul(ps_h[p][:, 0:n],
                                         w_g[kb][:, ic, f * 128:(f + 1) * 128],
                                         xn[:, ic, oo:oo + n], start=(ic == 0),
                                         stop=(ic == DC - 1))
                    for ic in range(DC):
                        mm = nc.tensor.matmul(ps_u[p][:, 0:n],
                                              w_u[kb][:, ic, f * 128:(f + 1) * 128],
                                              xn[:, ic, oo:oo + n], start=(ic == 0),
                                              stop=(ic == DC - 1))
                    pe_gu = b.i(mm, "pe")
                    last_use_w[kb] = pe_gu
                    b.w(nc.scalar, "pe", pe_gu)
                    sc_par[p] = b.i(nc.scalar.activation(sh_b[:, 0:n],
                                                         ps_h[p][:, 0:n],
                                                         AF.Silu), "sc")
                    b.w(nc.vector, "sc", sc_par[p])
                    ve_par[p] = b.i(nc.vector.tensor_mul(gbuf[:, f, 0:n],
                                                         sh_b[:, 0:n],
                                                         ps_u[p][:, 0:n]), "ve")
                    ve_g[f] = ve_par[p]
                for oc in range(DC):
                    p = oc % 2
                    b.w(nc.tensor, "ve", max(ve_g.values()))
                    if vey_par[p]:
                        b.w(nc.tensor, "ve", vey_par[p])
                    for f in range(DC):
                        mm = nc.tensor.matmul(ps_y[p][:, 0:n],
                                              w_d[kb][:, f, oc * 128:(oc + 1) * 128],
                                              gbuf[:, f, 0:n], start=(f == 0),
                                              stop=(f == DC - 1),
                                              skip_group_check=True)
                    pe_y = b.i(mm, "pe")
                    last_use_w[kb] = pe_y
                    b.w(nc.vector, "pe", pe_y)
                    b.w(nc.vector, "po")
                    if shared:
                        vey_par[p] = b.i(nc.vector.tensor_copy(yc[p][:, 0:n],
                                                               ps_y[p][:, 0:n]),
                                         "ve")
                    else:
                        vey_par[p] = b.i(nc.vector.tensor_mul(yc[p][:, 0:n],
                                                              ps_y[p][:, 0:n],
                                                              c_b[:, o:o + n]),
                                         "ve")
                    b.w(nc.gpsimd, "ve", vey_par[p])
                    b.i(nc.gpsimd.tensor_add(racc_t[oc][:, o:o + n],
                                             racc_t[oc][:, o:o + n],
                                             yc[p][:, 0:n]), "po")
                pe_dn = b.n["pe"]
                ve_par = {0: b.n["ve"], 1: b.n["ve"]}
        b.w(nc.gpsimd, "ve")
        b.barrier()
        for (o, n) in _slices(hw, 512):
            for c in range(DC):
                b.w(nc.sync, "ve")
                ln = b.load(xs2[:, 0:n], x_in[:, c, h0 + o:h0 + o + n])
                b.w(nc.vector, "dl", ln)
                b.w(nc.vector, "po")
                ve_o = b.i(nc.vector.scalar_tensor_tensor(
                    xs2[:, 0:n], racc_t[c][:, o:o + n], 0.5, xs2[:, 0:n],
                    ALU.mult, ALU.add), "ve")
                b.w(nc.sync, "ve", ve_o)
                b.store(x_out[:, c, h0 + o:h0 + o + n], xs2[:, 0:n])
        b.barrier()


def run_l1ffn_stage(xparts, W, col_off, inputs):
    nc = build_l1ffn(W)
    wp = _prep_weights(inputs)
    im = []
    for c in range(NC):
        im.append(dict(x_in=np.ascontiguousarray(xparts[c][:, :, col_off:col_off + W]),
                       rw=wp["rw1"], rbias=wp["rbias"], gw=wp["gw"], uw=wp["uw"],
                       dw=wp["dw"], shg=wp["shg"], shu=wp["shu"], shd=wp["shd"]))
    res = run_bass_kernel_spmd(nc, im, core_ids=list(range(NC)))
    return [np.asarray(res.results[c]["x_out"]) for c in range(NC)]


W_M = 2051
NCI = D_INNER // 128
TB = 512


def build_mamba_a():
    b = B()
    W = W_M
    b.reg_fconst(LN_EPS)
    b.reg_fconst(RMS_EPS)
    b.reg_fconst(1.0)
    x_in = b.inp("x_in", [128, DC, W])
    g_in = b.inp("g_in", [128, DC])
    w_in = b.inp("w_in", [8, 128, DC * 4 * 128])
    conv_w = b.inp("conv_w", [128, NCI, 4])
    conv_b = b.inp("conv_b", [128, NCI])
    w_x = b.inp("w_x", [1, 128, NCI * 33])
    w_dt = b.inp("w_dt", [1, D_INNER])
    b_dt = b.inp("b_dt", [128, NCI])
    a_sb = b.inp("a_sb", [128, NCI * D_STATE])
    hm3 = b.inp("hm3", [128, 4])
    z_out = b.out("z_out", [128, NCI, T_OWN])
    xc_out = b.out("xc_out", [128, NCI, T_OWN])
    dt_out = b.out("dt_out", [128, NCI, T_OWN])
    smy_out = b.out("smy_out", [128, 512])
    par_out = b.out("par_out", [33, T_OWN])
    wib = b.dram("wib", [8, 128, DC * 4 * 128], BF16)
    wxb = b.dram("wxb", [1, 128, NCI * 33], BF16)
    emit_wcast(b, [(w_in, wib), (w_x, wxb)])
    with b.scope():
        emit_mamba_a_stage(b, x_in, z_out, xc_out, dt_out, smy_out, par_out,
                           wib, wxb, g_in, conv_w, conv_b, w_dt, b_dt, a_sb, hm3)
    return b.finish()


def emit_mamba_a_stage(b, x_in, z_out, xc_out, dt_out, smy_out, par_out,
                       wib, wxb, g_in, conv_w, conv_b, w_dt, b_dt, a_sb, hm3):
    nc = b.nc
    W = W_M
    import os as _os
    dbg = b.out("d_xs", [128, 2 * W_M]) if _os.environ.get("KDEBUG_DUMP") else None
    pdram = b.dram("pdram_ma", [33, T_OWN])
    xc_dram = b.dram("xc_dram_ma", [128, NCI, T_OWN])
    dt_dram = b.dram("dt_dram_ma", [128, NCI, T_OWN])
    ident = _ident128(b)
    ones_row = _ones_row(b)
    _stat_lhs(b)
    ps_a = b.psum("ps_a", [128, 512])
    ps_b = b.psum("ps_b", [128, 512])
    ps_bc = [b.psum("ps_bc0", [128, 1024]),
             b.psum("ps_bc1", [128, 1024])]

    smy = b.sbuf("m_smy", [128, 512])
    a_s = b.sbuf("m_as", [128, NCI * D_STATE])
    ln_a = b.load(a_s, a_sb)
    b.barrier()
    b.w(nc.scalar, "dl", ln_a)
    b.i(nc.scalar.activation(a_s[:, :], a_s[:, :], AF.Exp), "sc")
    b.w(nc.vector, "sc")
    b.i(nc.vector.tensor_scalar(a_s[:, :], a_s[:, :], -1.0, None, ALU.mult), "ve")
    b.barrier()

    with ExitStack() as mctx:
        def sb(name, shape, dtype=F32):
            return mctx.enter_context(b.rawnc.sbuf_tensor(name, shape, dtype)).ap()

        xn = sb("m_xn", [128, DC, W], BF16)
        g_s = sb("m_g", [128, DC])
        b.load(g_s, g_in)
        emit_ln_rms_to_xn(b, x_in, W, xn, g_s, ps_a, ps_b)

        w_r = sb("m_wr", [128, DC, 4, 128], BF16)
        xs_c = sb("m_xsc", [128, W])
        xc_c = sb("m_xcc", [128, T_OWN])
        cw_s = sb("m_cw", [128, NCI, 4])
        cb_s = sb("m_cb", [128, NCI])
        bdt_s = sb("m_bdt", [128, NCI])
        hm_s = sb("m_hm", [128, 4])
        t1b = sb("m_t1", [128, W])
        t2b = sb("m_t2", [128, W])
        zstg = sb("m_zstg", [128, 512])
        b.load(cw_s, conv_w)
        b.load(cb_s, conv_b)
        b.load(bdt_s, b_dt)
        b.load(hm_s, hm3)
        b.barrier()

        for zg in range(4):
            b.w(nc.sync, "pe")
            lw = b.load(w_r, wib[4 + zg])
            b.w(nc.tensor, "dl", lw)
            for j in range(4):
                zc = zg * 4 + j
                for (o, n) in _slices(T_OWN, 512):
                    b.w(nc.tensor, "ve")
                    for ic in range(DC):
                        mm = nc.tensor.matmul(ps_a[:, 0:n], w_r[:, ic, j, :],
                                              xn[:, ic, 3 + o:3 + o + n],
                                              start=(ic == 0), stop=(ic == DC - 1))
                    pe_z = b.i(mm, "pe")
                    b.w(nc.vector, "pe", pe_z)
                    b.w(nc.vector, "st")
                    ve_z = b.i(nc.vector.tensor_copy(zstg[:, 0:n], ps_a[:, 0:n]),
                               "ve")
                    b.w(nc.sync, "ve", ve_z)
                    b.store(z_out[:, zc, o:o + n], zstg[:, 0:n])

        sc_cv = 0
        for ci in range(NCI):
            if ci % 4 == 0:
                b.w(nc.sync, "pe")
                lw = b.load(w_r, wib[ci // 4])
                b.w(nc.tensor, "dl", lw)
            for (o, n) in _slices(W, 512):
                b.w(nc.tensor, "ve")
                for ic in range(DC):
                    mm = nc.tensor.matmul(ps_a[:, 0:n], w_r[:, ic, ci % 4, :],
                                          xn[:, ic, o:o + n],
                                          start=(ic == 0), stop=(ic == DC - 1))
                pe_x = b.i(mm, "pe")
                b.w(nc.vector, "pe", pe_x)
                b.i(nc.vector.tensor_copy(xs_c[:, o:o + n], ps_a[:, 0:n]), "ve")
            nc.vector.tensor_mul(xs_c[:, 0:3], xs_c[:, 0:3], hm_s[:, 0:3])
            if dbg is not None and ci == 0:
                ve_dbg = b.i(nc.vector.tensor_copy(t1b[:, 0:W], xs_c[:, 0:W]), "ve")
                b.w(nc.sync, "ve", ve_dbg)
                b.store(dbg[:, 0:W], t1b[:, 0:W])
                b.w(nc.vector, "st")
            nc.vector.tensor_scalar(t1b[:, 0:T_OWN], xs_c[:, 0:T_OWN],
                                    cw_s[:, ci, 0:1], None, ALU.mult)
            if sc_cv:
                b.w(nc.vector, "sc", sc_cv)
            nc.vector.scalar_tensor_tensor(t2b[:, 0:T_OWN], xs_c[:, 1:1 + T_OWN],
                                           cw_s[:, ci, 1:2], t1b[:, 0:T_OWN],
                                           ALU.mult, ALU.add)
            nc.vector.scalar_tensor_tensor(t1b[:, 0:T_OWN], xs_c[:, 2:2 + T_OWN],
                                           cw_s[:, ci, 2:3], t2b[:, 0:T_OWN],
                                           ALU.mult, ALU.add)
            ve_cv = b.i(nc.vector.scalar_tensor_tensor(
                t2b[:, 0:T_OWN], xs_c[:, 3:3 + T_OWN], cw_s[:, ci, 3:4],
                t1b[:, 0:T_OWN], ALU.mult, ALU.add), "ve")
            if dbg is not None and ci == 0:
                b.w(nc.sync, "ve", ve_cv)
                b.store(dbg[:, W:W + T_OWN], t2b[:, 0:T_OWN])
                b.w(nc.vector, "st")
            b.w(nc.scalar, "ve", ve_cv)
            b.w(nc.scalar, "st")
            sc_cv = b.i(nc.scalar.activation(xc_c[:, :], t2b[:, 0:T_OWN], AF.Silu,
                                             bias=cb_s[:, ci:ci + 1]), "sc")
            b.w(nc.sync, "sc", sc_cv)
            b.store(xc_out[:, ci, :], xc_c)
            b.store(xc_dram[:, ci, :], xc_c)

        wx_r = sb("m_wx", [128, NCI, 33], BF16)
        b.load(wx_r, wxb[0])
        xcb = sb("m_xcb", [128, 512], BF16)
        xcf = sb("m_xcf", [128, 512])
        params = sb("m_params", [33, T_OWN])
        b.barrier()
        for (o, n) in _slices(T_OWN, 512):
            for ci in range(NCI):
                b.w(nc.sync, "ve")
                lxx = b.load(xcf[:, 0:n], xc_dram[:, ci, o:o + n])
                b.w(nc.vector, "dl", lxx)
                ve_c = b.i(nc.vector.tensor_copy(xcb[:, 0:n], xcf[:, 0:n]), "ve")
                b.w(nc.tensor, "ve", ve_c)
                mm = nc.tensor.matmul(ps_b[0:33, 0:n], wx_r[:, ci, :], xcb[:, 0:n],
                                      start=(ci == 0), stop=(ci == NCI - 1))
            pe_p = b.i(mm, "pe")
            b.w(nc.vector, "pe", pe_p)
            ve_p = b.i(nc.vector.tensor_copy(params[:, o:o + n], ps_b[0:33, 0:n]),
                       "ve")
            b.w(nc.tensor, "ve", ve_p)
        b.w(nc.sync, "ve")
        st_p = b.store(pdram, params)
        b.store(par_out, params)
        nc.sync.wait_ge(b.sems["st"], st_p)
        b.barrier()

        wdt_s = sb("m_wdt", [1, D_INNER])
        b.load(wdt_s, w_dt)
        logdt_r = sb("m_ldt", [1, T_OWN])
        b.load(logdt_r, pdram[32].unsqueeze(0))
        dt_c = sb("m_dtc", [128, T_OWN])
        sp1 = sb("m_sp1", [128, T_OWN])
        sp2 = sb("m_sp2", [128, T_OWN])
        dtsum = sb("m_dtsum", [128, NCI])
        b.barrier()
        ve_d = 0
        for ci in range(NCI):
            b.w(nc.tensor, "sc")
            if ve_d:
                b.w(nc.scalar, "ve", ve_d)
            for (o, n) in _slices(T_OWN, 512):
                mm = nc.tensor.matmul(ps_a[:, 0:n],
                                      wdt_s[:, ci * 128:(ci + 1) * 128],
                                      logdt_r[:, o:o + n], start=True, stop=True)
                pe_d = b.i(mm, "pe")
                b.w(nc.scalar, "pe", pe_d)
                nc.scalar.activation(sp1[:, o:o + n], ps_a[:, 0:n], AF.Abs,
                                     bias=bdt_s[:, ci:ci + 1])
                nc.scalar.activation(sp1[:, o:o + n], sp1[:, o:o + n], AF.Exp,
                                     scale=-1.0)
                nc.scalar.activation(sp1[:, o:o + n], sp1[:, o:o + n], AF.Ln,
                                     bias=1.0)
                sc_d = b.i(nc.scalar.activation(sp2[:, o:o + n], ps_a[:, 0:n],
                                                AF.Relu, bias=bdt_s[:, ci:ci + 1]),
                           "sc")
                b.w(nc.tensor, "sc", sc_d)
            b.w(nc.vector, "sc")
            b.w(nc.vector, "st")
            ve_d = b.i(nc.vector.tensor_add(dt_c[:, :], sp1[:, 0:T_OWN],
                                            sp2[:, 0:T_OWN]), "ve")
            nc.vector.reduce_sum(dtsum[:, ci:ci + 1], dt_c[:, :], axis=AX)
            b.w(nc.sync, "ve", ve_d)
            b.store(dt_out[:, ci, :], dt_c)
            b.store(dt_dram[:, ci, :], dt_c)
        b.barrier()
        for ci in range(NCI):
            for s in range(D_STATE):
                col = ci * D_STATE + s
                b.i(nc.scalar.activation(smy[:, 256 + col:257 + col],
                                         dtsum[:, ci:ci + 1], AF.Exp,
                                         scale=a_s[:, col:col + 1]), "sc")
        b.barrier()

    sst = scan_setup(b, init_hin=None)
    for tb in range(T_OWN // TB):
        scan_tblock(b, sst, tb, dt_dram, xc_dram, pdram, a_s, ps_bc, yacc=None)
    b.barrier()
    b.i(nc.vector.tensor_copy(smy[:, 0:256], sst["carry"][:, :]), "ve")
    b.w(nc.sync, "ve")
    b.store(smy_out, smy)
    if dbg is not None:
        b.store(dbg[:, 0:NCI * D_STATE], a_s)
        ve_dd = b.i(nc.vector.tensor_copy(sst["dt_tb"][:, 0, :],
                                          sst["dt_tb"][:, 0, :]), "ve")
        b.w(nc.sync, "ve", ve_dd)


def emit_ln_rms_to_xn(b, x_src, W, xn, g_s, ps_a, ps_b, tag="lr"):
    nc = b.nc
    with ExitStack() as lctx:
        def sb(name, shape, dtype=F32):
            return lctx.enter_context(
                b.rawnc.sbuf_tensor(f"{tag}{name}", shape, dtype)).ap()
        scr = dict(
            xcast=[sb(f"xc{i}", [128, 512], BF16) for i in range(2)],
            sq=[sb(f"sq{i}", [128, 512], BF16) for i in range(2)],
            xs=[sb(f"xs{i}", [128, 512], F32) for i in range(2)],
            alpha_r=sb("ar", [1, W]),
            beta_r=sb("br", [1, W]),
            alpha_b=sb("ab", [128, W]),
            beta_b=sb("bb", [128, W]),
        )
        emit_ln_stats(b, x_src, W, scr, ps_a, ps_b)
        stat_lhs = _stat_lhs(b)
        tmp = sb("tmp", [128, 512])
        xs2 = sb("xs2", [128, 512])
        t1c = sb("t1c", [128, 512], BF16)
        sqc = sb("sqc", [128, 512], BF16)
        rms_r = sb("rmsr", [1, W])
        for (o, n) in _slices(W, 512):
            for c in range(DC):
                ln = b.load(xs2[:, 0:n], x_src[:, c, o:o + n])
                b.w(nc.vector, "dl", ln)
                b.w(nc.vector, "sc")
                nc.vector.tensor_mul(tmp[:, 0:n], xs2[:, 0:n],
                                     scr["alpha_b"][:, o:o + n])
                ve_t = b.i(nc.vector.tensor_add(t1c[:, 0:n], tmp[:, 0:n],
                                                scr["beta_b"][:, o:o + n]), "ve")
                b.w(nc.scalar, "ve", ve_t)
                sc_q = b.i(nc.scalar.activation(sqc[:, 0:n], t1c[:, 0:n],
                                                AF.Square), "sc")
                b.w(nc.tensor, "sc", sc_q)
                b.w(nc.sync, "ve", ve_t)
                mm = nc.tensor.matmul(ps_a[0:1, 0:n], stat_lhs[:, 0:1],
                                      sqc[:, 0:n], start=(c == 0),
                                      stop=(c == DC - 1), skip_group_check=True)
            pe_m = b.i(mm, "pe")
            b.w(nc.vector, "pe", pe_m)
            ve_r = b.i(nc.vector.tensor_copy(rms_r[:, o:o + n], ps_a[0:1, 0:n]),
                       "ve")
            b.w(nc.tensor, "ve", ve_r)
        b.barrier()
        sc_rm = b.i(nc.scalar.activation(rms_r[:, 0:W], rms_r[:, 0:W], AF.Sqrt,
                                         bias=RMS_EPS), "sc")
        b.w(nc.vector, "sc", sc_rm)
        nc.vector.reciprocal(rms_r[:, 0:W], rms_r[:, 0:W])
        nc.vector.tensor_mul(scr["alpha_r"][:, 0:W], scr["alpha_r"][:, 0:W],
                             rms_r[:, 0:W])
        b.i(nc.vector.tensor_mul(scr["beta_r"][:, 0:W], scr["beta_r"][:, 0:W],
                                 rms_r[:, 0:W]), "ve")
        b.barrier()
        ones_row = _ones_row(b)
        for (o, n) in _slices(W, 512):
            mm = b.i(nc.tensor.matmul(ps_a[:, 0:n], ones_row,
                                      scr["alpha_r"][:, o:o + n], start=True,
                                      stop=True), "pe")
            b.w(nc.vector, "pe", mm)
            ve = b.i(nc.vector.tensor_copy(scr["alpha_b"][:, o:o + n],
                                           ps_a[:, 0:n]), "ve")
            b.w(nc.tensor, "ve", ve)
            mm = b.i(nc.tensor.matmul(ps_a[:, 0:n], ones_row,
                                      scr["beta_r"][:, o:o + n], start=True,
                                      stop=True), "pe")
            b.w(nc.vector, "pe", mm)
            ve = b.i(nc.vector.tensor_copy(scr["beta_b"][:, o:o + n],
                                           ps_a[:, 0:n]), "ve")
            b.w(nc.tensor, "ve", ve)
        b.barrier()
        for (o, n) in _slices(W, 512):
            for c in range(DC):
                ln = b.load(xs2[:, 0:n], x_src[:, c, o:o + n])
                b.w(nc.vector, "dl", ln)
                nc.vector.tensor_mul(tmp[:, 0:n], xs2[:, 0:n],
                                     scr["alpha_b"][:, o:o + n])
                nc.vector.tensor_add(tmp[:, 0:n], tmp[:, 0:n],
                                     scr["beta_b"][:, o:o + n])
                ve = b.i(nc.vector.tensor_scalar(xn[:, c, o:o + n], tmp[:, 0:n],
                                                 g_s[:, c:c + 1], None, ALU.mult),
                         "ve")
                b.w(nc.sync, "ve", ve)
        b.barrier()


def scan_setup(b, init_hin=None):
    nc = b.nc
    st = dict(
        dt_tb=b.sbuf("s_dt", [128, NCI, TB]),
        xc_tb=b.sbuf("s_xc", [128, NCI, TB]),
        ab=[b.sbuf(f"s_ab{i}", [128, TB]) for i in range(2)],
        bu=b.sbuf("s_bu", [128, TB]),
        h=b.sbuf("s_h", [128, TB]),
        hC=[b.sbuf(f"s_hc{i}", [128, TB]) for i in range(2)],
        carry=b.sbuf("s_carry", [128, NCI * D_STATE]),
        b_row=[b.sbuf(f"s_brow{i}", [1, TB]) for i in range(2)],
        c_row=[b.sbuf(f"s_crow{i}", [1, TB]) for i in range(2)],
        ve_scan_par={0: 0, 1: 0},
        po_par={0: 0, 1: 0},
        mm_par={0: 0, 1: 0},
    )
    if init_hin is None:
        b.i(nc.vector.memset(st["carry"][:, :], 0.0), "ve")
    else:
        b.i(nc.vector.tensor_copy(st["carry"][:, :], init_hin), "ve")
    b.barrier()
    return st


def scan_tblock(b, st, tb, dt_hbm, xc_hbm, pdram, a_s, ps_bcs, yacc=None):
    nc = b.nc
    o = tb * TB

    def issue_s(s, par):
        b.w(nc.sync, "pe")
        lb = b.load(st["b_row"][par], pdram[s].unsqueeze(0)[:, o:o + TB])
        lc2 = b.load(st["c_row"][par],
                     pdram[D_STATE + s].unsqueeze(0)[:, o:o + TB])
        b.w(nc.tensor, "dl", lc2)
        b.w(nc.tensor, "ve")
        mm = b.i(nc.tensor.matmul(ps_bcs[par][:, 0:TB], _ones_row(b),
                                  st["b_row"][par], start=True, stop=True), "pe")
        if yacc is not None:
            mm = b.i(nc.tensor.matmul(ps_bcs[par][:, 512:512 + TB],
                                      _ones_row(b), st["c_row"][par],
                                      start=True, stop=True), "pe")
        st["mm_par"][par] = mm

    b.w(nc.sync, "ve")
    b.w(nc.sync, "sc")
    b.w(nc.sync, "st")
    ld = b.load(st["dt_tb"][:, :, :], dt_hbm[:, :, o:o + TB])
    lx = b.load(st["xc_tb"][:, :, :], xc_hbm[:, :, o:o + TB])
    b.w(nc.vector, "dl", lx)
    b.w(nc.scalar, "dl", lx)
    issue_s(0, 0)
    if yacc is not None:
        b.w(nc.gpsimd, "ve")
        for ci in range(NCI):
            b.i(nc.gpsimd.memset(yacc[:, ci, :], 0.0), "po")
    for ci in range(NCI):
        b.i(nc.vector.tensor_mul(st["xc_tb"][:, ci, :], st["dt_tb"][:, ci, :],
                                 st["xc_tb"][:, ci, :]), "ve")
    ve_scan_par = st["ve_scan_par"]
    po_par = st["po_par"]
    for s in range(D_STATE):
        par = s % 2
        if s + 1 < D_STATE:
            issue_s(s + 1, 1 - par)
        mmB = st["mm_par"][par]
        for ci in range(NCI):
            col = ci * D_STATE + s
            p = ci % 2
            if ve_scan_par[p]:
                b.w(nc.scalar, "ve", ve_scan_par[p])
            sc_a = b.i(nc.scalar.activation(st["ab"][p][:, :],
                                            st["dt_tb"][:, ci, :],
                                            AF.Exp, scale=a_s[:, col:col + 1]),
                       "sc")
            b.w(nc.vector, "pe", mmB)
            b.w(nc.vector, "sc", sc_a)
            nc.vector.tensor_mul(st["bu"][:, :], st["xc_tb"][:, ci, :],
                                 ps_bcs[par][:, 0:TB])
            ve_sc = b.i(nc.vector.tensor_tensor_scan(
                st["h"][:, :], st["ab"][p][:, :], st["bu"][:, :],
                st["carry"][:, col:col + 1], ALU.mult, ALU.add), "ve")
            ve_scan_par[p] = ve_sc
            nc.vector.tensor_copy(st["carry"][:, col:col + 1],
                                  st["h"][:, TB - 1:TB])
            if yacc is not None:
                if po_par[p]:
                    b.w(nc.vector, "po", po_par[p])
                ve_hc = b.i(nc.vector.tensor_mul(st["hC"][p][:, :], st["h"][:, :],
                                                 ps_bcs[par][:, 512:512 + TB]),
                            "ve")
                b.w(nc.gpsimd, "ve", ve_hc)
                po_par[p] = b.i(nc.gpsimd.tensor_add(yacc[:, ci, :],
                                                     yacc[:, ci, :],
                                                     st["hC"][p][:, :]), "po")


def build_mamba_b():
    b = B()
    W = W_M
    b.reg_fconst(LN_EPS)
    x_in = b.inp("x_in", [128, DC, W])
    z_in = b.inp("z_in", [128, NCI, T_OWN])
    xc_in = b.inp("xc_in", [128, NCI, T_OWN])
    dt_in = b.inp("dt_in", [128, NCI, T_OWN])
    par_in = b.inp("par_in", [33, T_OWN])
    smy_prev = b.inp("smy_prev", [128, 3 * 512])
    a_sb = b.inp("a_sb", [128, NCI * D_STATE])
    d_in = b.inp("d_in", [128, NCI])
    w_out = b.inp("w_out", [1, 128, NCI * D])
    x_out = b.out("x_out", [128, DC, T_OWN])
    import os as _os
    dbg = (b.out("d_mb", [128, NCI * TB + 512])
           if _os.environ.get("KDEBUG_MB") else None)
    wob2 = b.dram("wob2", [1, 128, NCI * D], BF16)
    emit_wcast(b, [(w_out, wob2)])
    with b.scope():
        emit_mamba_b_stage(b, x_in, z_in, xc_in, dt_in, par_in, smy_prev,
                           a_sb, d_in, wob2, x_out, dbg=dbg)
    return b.finish()


def emit_mamba_b_stage(b, x_in, z_in, xc_in, dt_in, par_in, smy_prev,
                       a_sb, d_in, wob2, x_out, dbg=None):
    nc = b.nc
    W = W_M
    _stat_lhs(b); _ones_row(b); _ident128(b)
    ps_a = b.psum("ps_a", [128, 512])
    ps_b = b.psum("ps_b", [128, 512])
    ps_bc = [b.psum("ps_bc0", [128, 1024]),
             b.psum("ps_bc1", [128, 1024])]

    alpha_b = b.sbuf("mb_ab", [128, T_OWN])
    beta_b = b.sbuf("mb_bb", [128, T_OWN])
    scr = dict(alpha_b=alpha_b, beta_b=beta_b)
    x_owned = x_in[:, :, 3:3 + T_OWN]
    with ExitStack() as lctx:
        def _lsb(name, shape, dtype=F32):
            return lctx.enter_context(b.rawnc.sbuf_tensor(name, shape, dtype)).ap()
        scr_full = dict(
            xcast=[_lsb(f"mbxc{i}", [128, 512], BF16) for i in range(2)],
            sq=[_lsb(f"mbsq{i}", [128, 512], BF16) for i in range(2)],
            xs=[_lsb(f"mbxs{i}", [128, 512], F32) for i in range(2)],
            alpha_r=_lsb("mbar", [1, T_OWN]),
            beta_r=_lsb("mbbr", [1, T_OWN]),
            alpha_b=alpha_b, beta_b=beta_b)
        emit_ln_stats(b, x_owned, T_OWN, scr_full, ps_a, ps_b)

    a_s = b.sbuf("mb_as", [128, NCI * D_STATE])
    d_s = b.sbuf("mb_d", [128, NCI])
    hin = b.sbuf("mb_hin", [128, 256])
    b.load(a_s, a_sb)
    b.load(d_s, d_in)
    with ExitStack() as sctx:
        smyp = sctx.enter_context(b.rawnc.sbuf_tensor("mb_smyp", [128, 3 * 512], F32)).ap()
        tmp256 = sctx.enter_context(b.rawnc.sbuf_tensor("mb_t256", [128, 256], F32)).ap()
        b.load(smyp, smy_prev)
        b.barrier()
        b.i(nc.scalar.activation(a_s[:, :], a_s[:, :], AF.Exp), "sc")
        b.w(nc.vector, "sc")
        b.i(nc.vector.tensor_scalar(a_s[:, :], a_s[:, :], -1.0, None, ALU.mult), "ve")
        nc.vector.memset(hin[:, :], 0.0)
        for j in range(3):
            nc.vector.tensor_mul(tmp256, hin, smyp[:, j * 512 + 256:(j + 1) * 512])
            nc.vector.tensor_add(hin, tmp256, smyp[:, j * 512:j * 512 + 256])
        b.barrier()

    wo_r = b.sbuf("mb_wo", [128, NCI, D], BF16)
    b.load(wo_r, wob2[0])
    yacc = b.sbuf("mb_yacc", [128, NCI, TB])
    ybuf = b.sbuf("mb_ybuf", [128, NCI, TB], BF16)
    zbuf = b.sbuf("mb_z", [128, TB])
    xcg = b.sbuf("mb_xcg", [128, TB])
    zsil = b.sbuf("mb_zs", [128, TB])
    xstg = b.sbuf("mb_xs", [128, TB])
    r1 = b.sbuf("mb_r1", [128, TB])
    b.barrier()

    sst = scan_setup(b, init_hin=hin)
    for tb in range(T_OWN // TB):
        o = tb * TB
        scan_tblock(b, sst, tb, dt_in, xc_in, par_in, a_s, ps_bc, yacc=yacc)
        ve_g = 0
        for ci in range(NCI):
            b.w(nc.sync, "sc")
            b.w(nc.sync, "ve")
            lz = b.load(zbuf, z_in[:, ci, o:o + TB])
            lxc = b.load(xcg, xc_in[:, ci, o:o + TB])
            b.w(nc.scalar, "dl", lz)
            if ve_g:
                b.w(nc.scalar, "ve", ve_g)
            sc_z = b.i(nc.scalar.activation(zsil, zbuf, AF.Silu), "sc")
            b.w(nc.vector, "dl", lxc)
            b.w(nc.vector, "sc", sc_z)
            b.w(nc.vector, "po")
            b.w(nc.vector, "pe")
            nc.vector.tensor_scalar(r1, xcg, d_s[:, ci:ci + 1], None, ALU.mult)
            nc.vector.tensor_add(r1, r1, yacc[:, ci, :])
            ve_g = b.i(nc.vector.tensor_mul(ybuf[:, ci, :], r1, zsil), "ve")
        ve_y = b.n["ve"]
        if dbg is not None and tb == 0:
            b.w(nc.sync, "po")
            b.w(nc.sync, "ve")
            b.store(dbg[:, 0:NCI * TB],
                    yacc.rearrange("p a b -> p (a b)"))
            b.store(dbg[:, NCI * TB:NCI * TB + 256], a_s)
            b.store(dbg[:, NCI * TB + 256:NCI * TB + 512], sst["carry"])
            b.w(nc.vector, "st")
            b.w(nc.gpsimd, "st")
        for oc in range(DC):
            b.w(nc.tensor, "ve", ve_y)
            b.w(nc.tensor, "ve")
            for ci in range(NCI):
                mm = nc.tensor.matmul(ps_a[:, 0:TB], wo_r[:, ci, oc * 128:(oc + 1) * 128],
                                      ybuf[:, ci, :], start=(ci == 0),
                                      stop=(ci == NCI - 1))
            pe_o = b.i(mm, "pe")
            b.w(nc.sync, "st")
            lx = b.load(xstg, x_in[:, oc, 3 + o:3 + o + TB])
            b.w(nc.vector, "dl", lx)
            b.w(nc.vector, "pe", pe_o)
            nc.vector.tensor_mul(r1, xstg, scr["alpha_b"][:, o:o + TB])
            nc.vector.tensor_add(r1, r1, scr["beta_b"][:, o:o + TB])
            nc.vector.tensor_add(r1, r1, xstg)
            ve_o = b.i(nc.vector.tensor_add(r1, r1, ps_a[:, 0:TB]), "ve")
            b.w(nc.sync, "ve", ve_o)
            b.store(x_out[:, oc, o:o + TB], r1)


def _mamba_host_prep(inputs):
    pr = {}
    w_inT = np.asarray(inputs["L1_attn_W_in"], np.float32).T
    pr["w_inT"] = _pm(w_inT, (DC, 128, 8, 4, 128), (2, 1, 0, 3, 4),
                      (8, 128, DC * 4 * 128))
    w_xT = np.asarray(inputs["L1_attn_W_x"], np.float32).T
    pr["w_xT"] = _pm(w_xT, (NCI, 128, 33), (1, 0, 2), (1, 128, NCI * 33))
    w_outT = np.asarray(inputs["L1_attn_W_out"], np.float32).T
    pr["w_outT"] = _pm(w_outT, (NCI, 128, D), (1, 0, 2), (1, 128, NCI * D))
    pr["w_dt"] = np.ascontiguousarray(np.asarray(inputs["L1_attn_W_dt"], np.float32).reshape(1, D_INNER))
    pr["g"] = _fm_vec(inputs["L1_attn_g"], DC)
    pr["conv_b"] = _fm_vec(inputs["L1_attn_conv_b"], NCI)
    pr["b_dt"] = _fm_vec(inputs["L1_attn_b_dt"], NCI)
    pr["D"] = _fm_vec(inputs["L1_attn_D"], NCI)
    cw = np.asarray(inputs["L1_attn_conv_w"], np.float32)
    pr["conv_w"] = np.ascontiguousarray(cw.reshape(NCI, 128, 4).transpose(1, 0, 2))
    al = np.asarray(inputs["L1_attn_A_log"], np.float32)
    pr["a_sb"] = np.ascontiguousarray(
        al.reshape(NCI, 128, D_STATE).transpose(1, 0, 2).reshape(128, NCI * D_STATE))
    return pr


def run_mamba_stage(x4parts, inputs):
    pr = _mamba_host_prep(inputs)
    nc_a = build_mamba_a()
    im_a = []
    for c in range(NC):
        hm = np.ones((128, 4), np.float32)
        if c % 4 == 0:
            hm[:] = 0.0
        im_a.append(dict(x_in=np.ascontiguousarray(x4parts[c][:, :, 0:W_M]),
                         g_in=pr["g"], w_in=pr["w_inT"], conv_w=pr["conv_w"],
                         conv_b=pr["conv_b"], w_x=pr["w_xT"], w_dt=pr["w_dt"],
                         b_dt=pr["b_dt"], a_sb=pr["a_sb"], hm3=hm))
    res_a = run_bass_kernel_spmd(nc_a, im_a, core_ids=list(range(NC)))
    nc_b = build_mamba_b()
    im_b = []
    ident_smy = np.concatenate([np.zeros((128, 256), np.float32),
                                np.ones((128, 256), np.float32)], axis=1)
    for c in range(NC):
        bb, kk = c // 4, c % 4
        prevs = [ident_smy] * (3 - kk) + [
            np.asarray(res_a.results[bb * 4 + j]["smy_out"]) for j in range(kk)]
        smyp = np.concatenate(prevs, axis=1)
        ra = res_a.results[c]
        im_b.append(dict(x_in=np.ascontiguousarray(x4parts[c][:, :, 0:W_M]),
                         z_in=np.asarray(ra["z_out"]), xc_in=np.asarray(ra["xc_out"]),
                         dt_in=np.asarray(ra["dt_out"]), par_in=np.asarray(ra["par_out"]),
                         smy_prev=smyp, a_sb=pr["a_sb"], d_in=pr["D"],
                         w_out=pr["w_outT"]))
    res_b = run_bass_kernel_spmd(nc_b, im_b, core_ids=list(range(NC)))
    return [np.asarray(res_b.results[c]["x_out"]) for c in range(NC)]


def _prep_consts(b):
    _ident128(b)
    _ones_row(b)
    _stat_lhs(b)
    b.const_sbuf("ones4", np.ones((128, 4), np.float32))
    b.const_sbuf("ones16", np.ones((128, 16), np.float32))
    b.const_sbuf("ones8", np.ones((128, 8), np.float32))
    b.const_sbuf("zero1", np.zeros((128, 1), np.float32))
    b.reg_fconst(LN_EPS)
    b.reg_fconst(RMS_EPS)
    b.reg_fconst(1.0)


def build_launch1():
    b = B()
    nc = b.nc
    NT = W_ATT // 128
    x_in = b.inp("x_in", [128, DC, T_EXT])
    w1 = b.inp("w1", [8, 128, DC * 512])
    w2 = b.inp("w2", [8, 128, 4 * D])
    rw0 = b.inp("rw0", [D, 8])
    b1 = b.inp("b1", [128, F0 // 128])
    b2 = b.inp("b2", [128, DC])
    wq = b.inp("wq", [1, 128, DC * D]); wk = b.inp("wk", [1, 128, DC * D])
    wv = b.inp("wv", [1, 128, DC * D]); wo = b.inp("wo", [1, 128, DC * D])
    bq = b.inp("bq", [128, DC]); bk = b.inp("bk", [128, DC])
    bv = b.inp("bv", [1, D]); bo = b.inp("bo", [128, DC])
    cosq = b.inp("cosq", [128, 4, T_EXT]); sinq = b.inp("sinq", [128, 4, T_EXT])
    cosk = b.inp("cosk", [128, 4, T_EXT]); sink = b.inp("sink", [128, 4, T_EXT])
    maskA = b.inp("maskA", [NT, 128, HALO])
    maskT = b.inp("maskT", [NT, 128, HALO])
    rw1 = b.inp("rw1", [1, 128, DC * NE])
    rbias = b.inp("rbias", [128, NE])
    gw = b.inp("gw", [NE, 128, DC * D]); uw = b.inp("uw", [NE, 128, DC * D])
    dw = b.inp("dw", [NE, 128, DC * D])
    shg = b.inp("shg", [2, 128, DC * D]); shu = b.inp("shu", [2, 128, DC * D])
    shd = b.inp("shd", [2, 128, DC * D])
    g_in = b.inp("g_in", [128, DC])
    w_in = b.inp("w_in", [8, 128, DC * 4 * 128])
    conv_w = b.inp("conv_w", [128, NCI, 4])
    conv_b = b.inp("conv_b", [128, NCI])
    w_x = b.inp("w_x", [1, 128, NCI * 33])
    w_dt = b.inp("w_dt", [1, D_INNER])
    b_dt = b.inp("b_dt", [128, NCI])
    a_sb = b.inp("a_sb", [128, NCI * D_STATE])
    hm3 = b.inp("hm3", [128, 4])

    x4m_out = b.out("x4m_out", [128, DC, W_M])
    z_out = b.out("z_out", [128, NCI, T_OWN])
    xc_out = b.out("xc_out", [128, NCI, T_OWN])
    dt_out = b.out("dt_out", [128, NCI, T_OWN])
    smy_out = b.out("smy_out", [128, 512])
    par_out = b.out("par_out", [33, T_OWN])

    w1b = b.dram("w1b", [8, 128, DC * 512], BF16)
    w2b = b.dram("w2b", [8, 128, 4 * D], BF16)
    wqb = b.dram("wqb", [1, 128, DC * D], BF16)
    wkb = b.dram("wkb", [1, 128, DC * D], BF16)
    wvb = b.dram("wvb", [1, 128, DC * D], BF16)
    wob = b.dram("wob", [1, 128, DC * D], BF16)
    rwb = b.dram("rwb", [1, 128, DC * NE], BF16)
    gwb = b.dram("gwb", [NE, 128, DC * D], BF16)
    uwb = b.dram("uwb", [NE, 128, DC * D], BF16)
    dwb = b.dram("dwb", [NE, 128, DC * D], BF16)
    shgb = b.dram("shgb", [2, 128, DC * D], BF16)
    shub = b.dram("shub", [2, 128, DC * D], BF16)
    shdb = b.dram("shdb", [2, 128, DC * D], BF16)
    wib = b.dram("wib", [8, 128, DC * 4 * 128], BF16)
    wxb = b.dram("wxb", [1, 128, NCI * 33], BF16)

    x1d = b.dram("x1d", [128, DC, T_EXT])
    qd = b.dram("qd", [128, DC, W_ATT], BF16)
    kd = b.dram("kd", [128, DC, T_EXT], BF16)
    vd = b.dram("vd", [128, T_EXT // 128, D], BF16)
    x2d = b.dram("x2d", [128, DC, W_ATT])
    x3d = b.dram("x3d", [128, DC, W_ATT])
    x4d = b.dram("x4d", [128, DC, W_ATT])

    _prep_consts(b)
    emit_wcast(b, [(w1, w1b), (w2, w2b), (wq, wqb), (wk, wkb), (wv, wvb),
                   (wo, wob), (rw1, rwb), (gw, gwb), (uw, uwb), (dw, dwb),
                   (shg, shgb), (shu, shub), (shd, shdb), (w_in, wib),
                   (w_x, wxb)])

    with b.scope():
        emit_l0ffn_stage(b, x_in, x1d, T_EXT, w1b, w2b, rw0, b1, b2)
    b.flush_stores()
    with b.scope():
        emit_attn_a_stage(b, x1d, qd, kd, vd, wqb, wkb, wvb, bq, bk, bv,
                          cosq, sinq, cosk, sink)
    b.flush_stores()
    with b.scope():
        emit_attn_b_stage(b, x1d, qd, kd, vd, x2d, wob, bo, maskA, maskT)
    b.flush_stores()
    with b.scope():
        emit_l0ffn_stage(b, x2d, x3d, W_ATT, w1b, w2b, rw0, b1, b2)
    b.flush_stores()
    with b.scope():
        emit_l1ffn_stage(b, x3d, x4d, W_ATT, rwb, gwb, uwb, dwb,
                         shgb, shub, shdb, rbias, tag="f1a")
    b.flush_stores()
    b.store(x4m_out, x4d[:, :, W_ATT - W_M:])
    with b.scope():
        emit_mamba_a_stage(b, x4d[:, :, W_ATT - W_M:], z_out, xc_out, dt_out,
                           smy_out, par_out, wib, wxb, g_in, conv_w, conv_b,
                           w_dt, b_dt, a_sb, hm3)
    return b.finish()


def build_launch2():
    b = B()
    x_in = b.inp("x_in", [128, DC, W_M])
    z_in = b.inp("z_in", [128, NCI, T_OWN])
    xc_in = b.inp("xc_in", [128, NCI, T_OWN])
    dt_in = b.inp("dt_in", [128, NCI, T_OWN])
    par_in = b.inp("par_in", [33, T_OWN])
    smy_prev = b.inp("smy_prev", [128, 3 * 512])
    a_sb = b.inp("a_sb", [128, NCI * D_STATE])
    d_in = b.inp("d_in", [128, NCI])
    w_out = b.inp("w_out", [1, 128, NCI * D])
    rw1 = b.inp("rw1", [1, 128, DC * NE])
    rbias = b.inp("rbias", [128, NE])
    gw = b.inp("gw", [NE, 128, DC * D]); uw = b.inp("uw", [NE, 128, DC * D])
    dw = b.inp("dw", [NE, 128, DC * D])
    shg = b.inp("shg", [2, 128, DC * D]); shu = b.inp("shu", [2, 128, DC * D])
    shd = b.inp("shd", [2, 128, DC * D])
    x_out = b.out("x_out", [128, DC, T_OWN])

    wob2 = b.dram("wob2", [1, 128, NCI * D], BF16)
    rwb = b.dram("rwb", [1, 128, DC * NE], BF16)
    gwb = b.dram("gwb", [NE, 128, DC * D], BF16)
    uwb = b.dram("uwb", [NE, 128, DC * D], BF16)
    dwb = b.dram("dwb", [NE, 128, DC * D], BF16)
    shgb = b.dram("shgb", [2, 128, DC * D], BF16)
    shub = b.dram("shub", [2, 128, DC * D], BF16)
    shdb = b.dram("shdb", [2, 128, DC * D], BF16)
    x5d = b.dram("x5d", [128, DC, T_OWN])

    _prep_consts(b)
    emit_wcast(b, [(w_out, wob2), (rw1, rwb), (gw, gwb), (uw, uwb), (dw, dwb),
                   (shg, shgb), (shu, shub), (shd, shdb)])
    with b.scope():
        emit_mamba_b_stage(b, x_in, z_in, xc_in, dt_in, par_in, smy_prev,
                           a_sb, d_in, wob2, x5d)
    b.flush_stores()
    with b.scope():
        emit_l1ffn_stage(b, x5d, x_out, T_OWN, rwb, gwb, uwb, dwb,
                         shgb, shub, shdb, rbias, tag="f1b")
    return b.finish()


def kernel(**inputs):
    x0 = np.asarray(inputs["L0_attn_hidden_state"], np.float32)
    xp = _shard_x(x0)
    pr = _mamba_host_prep(inputs)
    wp = _prep_weights(inputs)
    nc1 = build_launch1()
    im1 = []
    for c in range(NC):
        kk = c % 4
        pos = np.maximum(kk * T_OWN - HALO + np.arange(T_EXT), 0)
        cq, sq = _rope_tables(pos, 0.125)
        ck, sk = _rope_tables(pos, 1.0)
        mA, mT = _attn_masks(c)
        hm = np.ones((128, 4), np.float32)
        if kk == 0:
            hm[:] = 0.0
        im1.append(dict(
            x_in=np.ascontiguousarray(xp[c]),
            cosq=cq, sinq=sq, cosk=ck, sink=sk, maskA=mA, maskT=mT,
            g_in=pr["g"], w_in=pr["w_inT"], conv_w=pr["conv_w"],
            conv_b=pr["conv_b"], w_x=pr["w_xT"], w_dt=pr["w_dt"],
            b_dt=pr["b_dt"], a_sb=pr["a_sb"], hm3=hm,
            **{k: wp[k] for k in ("w1", "w2", "rw0", "b1", "b2", "wq", "wk",
                                  "wv", "wo", "bq", "bk", "bv", "bo", "rw1",
                                  "rbias", "gw", "uw", "dw", "shg", "shu",
                                  "shd")}))
    res1 = run_bass_kernel_spmd(nc1, im1, core_ids=list(range(NC)))

    nc2 = build_launch2()
    ident_smy = np.concatenate([np.zeros((128, 256), np.float32),
                                np.ones((128, 256), np.float32)], axis=1)
    im2 = []
    for c in range(NC):
        bb, kk = c // 4, c % 4
        prevs = [ident_smy] * (3 - kk) + [
            np.asarray(res1.results[bb * 4 + j]["smy_out"]) for j in range(kk)]
        r1 = res1.results[c]
        im2.append(dict(
            x_in=np.asarray(r1["x4m_out"]), z_in=np.asarray(r1["z_out"]),
            xc_in=np.asarray(r1["xc_out"]), dt_in=np.asarray(r1["dt_out"]),
            par_in=np.asarray(r1["par_out"]),
            smy_prev=np.ascontiguousarray(np.concatenate(prevs, axis=1)),
            a_sb=pr["a_sb"], d_in=pr["D"], w_out=pr["w_outT"],
            **{k: wp[k] for k in ("rw1", "rbias", "gw", "uw", "dw",
                                  "shg", "shu", "shd")}))
    res2 = run_bass_kernel_spmd(nc2, im2, core_ids=list(range(NC)))
    outs = [np.asarray(res2.results[c]["x_out"]) for c in range(NC)]
    return _unshard_x(outs, T_OWN, 0)


# version 12
# speedup vs baseline: 1.4959x; 1.0057x over previous
import numpy as np
from contextlib import ExitStack

import concourse.bass as bass
import concourse.mybir as mybir
from concourse.bass_types import AP
from concourse.bass_utils import run_bass_kernel_spmd

F32 = mybir.dt.float32
BF16 = mybir.dt.bfloat16
AX = mybir.AxisListType.X
ALU = mybir.AluOpType
AF = mybir.ActivationFunctionType

NC = 8
D = 1024
DC = 8
T_OWN = 2048
HALO = 640
T_EXT = T_OWN + HALO
W_ATT = T_OWN + 128
NEG = -3.0e38
LN_EPS = 1e-5
RMS_EPS = 1e-6
D_INNER = 2048
D_STATE = 16
NE = 16
F0 = 4096
F_SH = 2048
GELU_C = 0.7978845608028654


def _slices(total, step, start=0):
    out = []
    o = start
    while o < total:
        out.append((o, min(step, total - o)))
        o += step
    return out


_GUARDED = {
    "tensor_tensor", "tensor_add", "tensor_sub", "tensor_mul", "tensor_max",
    "tensor_scalar", "tensor_single_scalar", "tensor_scalar_add",
    "tensor_scalar_sub", "tensor_scalar_mul", "tensor_scalar_max",
    "tensor_scalar_min", "tensor_relu", "tensor_reduce", "reduce_max",
    "reduce_sum", "scalar_tensor_tensor", "tensor_tensor_scan", "tensor_copy",
    "reciprocal", "select", "copy_predicated", "activation", "square", "sqrt",
    "mul", "copy", "add", "sign", "memset", "memzero", "bn_stats", "bn_aggr",
    "max", "max_index", "iota", "affine_select", "transpose", "stream_shuffle",
}


def _ap_name(a):
    try:
        return a.name
    except Exception:
        return None


_LAGGING = {
    "reduce_max", "reduce_sum", "tensor_reduce", "tensor_tensor_scan",
    "bn_stats", "bn_aggr", "max", "max_index",
}
_LAG_GAP = 2


class _EngProxy:

    def __init__(self, eng):
        self._eng = eng
        self._dirty = set()

    def _clear(self):
        self._dirty.clear()

    def __getattr__(self, name):
        real = getattr(self._eng, name)
        if name not in _GUARDED or not callable(real):
            return real

        def wrapped(*args, **kw):
            aps = [a for a in args if isinstance(a, AP)]
            aps += [v for v in kw.values() if isinstance(v, AP)]
            names = {_ap_name(a) for a in aps}
            if names & self._dirty:
                self._eng.drain()
                self._dirty.clear()
            r = real(*args, **kw)
            if aps:
                self._dirty.add(_ap_name(aps[0]))
            ao = kw.get("accum_out")
            if isinstance(ao, AP):
                self._dirty.add(_ap_name(ao))
            return r

        return wrapped


class _NCShim:

    def __init__(self, nc):
        self._nc = nc
        self.vector = _EngProxy(nc.vector)
        self.scalar = _EngProxy(nc.scalar)
        self.gpsimd = _EngProxy(nc.gpsimd)
        self.tensor = nc.tensor
        self.sync = nc.sync

    def __getattr__(self, name):
        return getattr(self._nc, name)

    def proxy_of(self, eng):
        for p in (self.vector, self.scalar, self.gpsimd):
            if p._eng is eng:
                return p
        return None


class B:
    def __init__(self, name="k"):
        self.rawnc = bass.Bass("TRN2", target_bir_lowering=False)
        self.nc = _NCShim(self.rawnc)
        self.ctx = ExitStack()
        self.n = {}
        self.sems = {}
        self._waited = {}
        self._inline = {}
        for key in ("pe", "sc", "ve", "po", "sy", "dl", "st"):
            self.sems[key] = self.ctx.enter_context(self.rawnc.semaphore(key))
            self.n[key] = 0

    def sbuf(self, name, shape, dtype=F32):
        if getattr(self, "_scopes", None):
            return self._scopes[-1].enter_context(
                self.rawnc.sbuf_tensor(name, shape, dtype,
                                       allow_name_mangling=True)).ap()
        return self.nc.alloc_sbuf_tensor(name, shape, dtype).ap()

    def psum(self, name, shape, dtype=F32):
        if getattr(self, "_scopes", None):
            return self._scopes[-1].enter_context(
                self.rawnc.psum_tensor(name, shape, dtype,
                                       allow_name_mangling=True)).ap()
        return self.nc.alloc_psum_tensor(name, shape, dtype).ap()

    from contextlib import contextmanager as _cm

    @_cm
    def scope(self):
        if not hasattr(self, "_scopes"):
            self._scopes = []
        es = ExitStack()
        self._scopes.append(es)
        try:
            with es:
                yield
        finally:
            self._scopes.pop()

    def flush_stores(self):
        self.barrier()
        for eng in self.rawnc.engines.values():
            self.w(eng, "st")
        self.barrier()

    def inline(self, name, arr):
        if name not in self._inline:
            self._inline[name] = self.nc.inline_tensor(
                np.ascontiguousarray(arr), name).ap()
        return self._inline[name]

    def reg_fconst(self, *vals):
        for v in vals:
            ap = self.const_sbuf(f"fc_{v}", np.full((128, 1), v, np.float32))
            self.nc.const_aps.register(F32, float(v), ap)

    def const_sbuf(self, name, arr, dtype=None):
        key = "cs_" + name
        if key not in self._inline:
            dram = self.nc.inline_tensor(np.ascontiguousarray(arr), name).ap()
            if dtype is None:
                dtype = dram.dtype
            sb = self.sbuf(key, list(arr.shape), dtype)
            self.load(sb, dram)
            self.barrier()
            for eng in self.rawnc.engines.values():
                self.w(eng, "dl")
            self.barrier()
            self._inline[key] = sb
        return self._inline[key]

    def inp(self, name, shape, dtype=F32):
        return self.nc.dram_tensor(name, list(shape), dtype, kind="ExternalInput").ap()

    def out(self, name, shape, dtype=F32):
        return self.nc.dram_tensor(name, list(shape), dtype, kind="ExternalOutput").ap()

    def dram(self, name, shape, dtype=F32):
        if not hasattr(self, "_dram_names"):
            self._dram_names = set()
        base, k = name, 1
        while name in self._dram_names:
            k += 1
            name = f"{base}_{k}"
        self._dram_names.add(name)
        return self.nc.dram_tensor(name, list(shape), dtype, kind="Internal").ap()

    def i(self, instr, key, drain=True):
        self.n[key] += 1
        if key == "pe" or not drain:
            instr.then_inc(self.sems[key], 1)
            return self.n[key]
        eng = self.rawnc.engines[instr.ins.engine]
        eng.drain().then_inc(self.sems[key], 1)
        p = self.nc.proxy_of(eng)
        if p is not None:
            p._clear()
        return self.n[key]

    def w(self, eng, key, val=None):
        if isinstance(eng, _EngProxy):
            eng = eng._eng
        if val is None:
            val = self.n[key]
        if val <= 0:
            return
        k = (self._ekey(eng), key)
        if self._waited.get(k, -1) >= val:
            return
        eng.wait_ge(self.sems[key], val)
        self._waited[k] = val

    def load(self, dst, src, q=None, **kw):
        eng = q if q is not None else self.nc.sync
        eng.dma_start(out=dst, in_=src, **kw).then_inc(self.sems["dl"], 16)
        self.n["dl"] += 16
        return self.n["dl"]

    def store(self, dst, src, q=None, **kw):
        eng = q if q is not None else self.nc.sync
        eng.dma_start(out=dst, in_=src, **kw).then_inc(self.sems["st"], 16)
        self.n["st"] += 16
        return self.n["st"]

    def barrier(self):
        self.rawnc.all_engine_barrier()
        for p in (self.nc.vector, self.nc.scalar, self.nc.gpsimd):
            p._clear()
        for key in ("pe", "sc", "ve", "po", "sy"):
            for eng in self.nc.engines.values():
                self._waited[(self._ekey(eng), key)] = self.n[key]

    def _ekey(self, eng):
        return eng.engine if hasattr(eng, "engine") else id(eng)

    def finish(self):
        self.rawnc.sync.wait_ge(self.sems["st"], self.n["st"])
        self.ctx.close()
        return self.rawnc


def _ident128(b):
    return b.const_sbuf("ident128", np.eye(128, dtype=np.float32))


def _ones_row(b):
    return b.const_sbuf("ones_row", np.ones((1, 128), np.float32))


def _stat_lhs(b):
    import ml_dtypes
    arr = np.zeros((128, 2), np.float32)
    arr[:, 0] = 1.0 / D
    arr[:, 1] = 1.0
    return b.const_sbuf("stat_lhs", arr.astype(ml_dtypes.bfloat16))


def emit_wcast(b, jobs):
    nc = b.nc
    CM = 4096
    with ExitStack() as wctx:
        stg = [wctx.enter_context(b.rawnc.sbuf_tensor(f"wcs{i}", [128, CM], F32)).ap()
               for i in range(2)]
        stgb = [wctx.enter_context(b.rawnc.sbuf_tensor(f"wcb{i}", [128, CM], BF16)).ap()
                for i in range(2)]
        last_cast = [0, 0]
        last_store = [0, 0]
        i = 0
        for src_ap, dst in jobs:
            G = src_ap.shape[0]
            C = src_ap.shape[2]
            for g in range(G):
                for (co, cn_w) in _slices(C, CM):
                    k = i % 2
                    if last_cast[k]:
                        b.w(nc.sync, "ve", last_cast[k])
                    ln = b.load(stg[k][:, 0:cn_w], src_ap[g][:, co:co + cn_w])
                    b.w(nc.vector, "dl", ln)
                    if last_store[k]:
                        b.w(nc.vector, "st")
                    cn = b.i(nc.vector.tensor_copy(stgb[k][:, 0:cn_w],
                                                   stg[k][:, 0:cn_w]), "ve")
                    last_cast[k] = cn
                    b.w(nc.sync, "ve", cn)
                    last_store[k] = b.store(dst[g][:, co:co + cn_w],
                                            stgb[k][:, 0:cn_w])
                    i += 1
        b.barrier()


def alloc_ln_scratch(b, W, tag=""):
    return dict(
        xcast=[b.sbuf(f"lnxc{tag}{i}", [128, 512], BF16) for i in range(2)],
        sq=[b.sbuf(f"lnsq{tag}{i}", [128, 512], BF16) for i in range(2)],
        xs=[b.sbuf(f"lnxs{tag}{i}", [128, 512], F32) for i in range(2)],
        alpha_r=b.sbuf(f"lnar{tag}", [1, W], F32),
        beta_r=b.sbuf(f"lnbr{tag}", [1, W], F32),
        alpha_b=b.sbuf(f"lnab{tag}", [128, W], F32),
        beta_b=b.sbuf(f"lnbb{tag}", [128, W], F32),
    )


def emit_ln_stats(b, x_src, W, scr, ps_a, ps_b):
    import os
    sub = int(os.environ.get("KDEBUG_SUB", "9"))
    nc = b.nc
    stat_lhs = _stat_lhs(b)
    ones_row = _ones_row(b)
    mean_r, sq_r = scr["alpha_r"], scr["beta_r"]
    last_mm = [0, 0]
    for (o, n) in _slices(W, 512):
        for c in range(DC):
            k = c % 2
            if last_mm[k]:
                b.w(nc.sync, "pe", last_mm[k])
            ln = b.load(scr["xs"][k][:, 0:n], x_src[:, c, o:o + n])
            b.w(nc.scalar, "dl", ln)
            if last_mm[k]:
                b.w(nc.scalar, "pe", last_mm[k])
            nc.scalar.copy(scr["xcast"][k][:, 0:n], scr["xs"][k][:, 0:n])
            sn = b.i(nc.scalar.activation(scr["sq"][k][:, 0:n],
                                          scr["xcast"][k][:, 0:n], AF.Square), "sc")
            b.w(nc.tensor, "sc", sn)
            nc.tensor.matmul(ps_a[0:1, 0:n], stat_lhs[:, 0:1], scr["xcast"][k][:, 0:n],
                             start=(c == 0), stop=(c == DC - 1), skip_group_check=True)
            mmn = b.i(
                nc.tensor.matmul(ps_b[0:1, 0:n], stat_lhs[:, 1:2], scr["sq"][k][:, 0:n],
                                 start=(c == 0), stop=(c == DC - 1),
                                 skip_group_check=True), "pe")
            last_mm[k] = mmn
        b.w(nc.vector, "pe", last_mm[(DC - 1) % 2])
        nc.vector.tensor_copy(mean_r[:, o:o + n], ps_a[0:1, 0:n])
        vn = b.i(nc.vector.tensor_copy(sq_r[:, o:o + n], ps_b[0:1, 0:n]), "ve")
        b.w(nc.tensor, "ve", vn)
    b.barrier()
    if sub == 0:
        return
    ar, br = scr["alpha_r"], scr["beta_r"]
    tmp_r = scr["alpha_b"][0:1, :]
    nc.scalar.mul(br[:, 0:W], br[:, 0:W], 1.0 / D)
    sn = b.i(nc.scalar.activation(tmp_r[:, 0:W], ar[:, 0:W], AF.Square), "sc")
    b.w(nc.vector, "sc", sn)
    vn = b.i(nc.vector.tensor_sub(br[:, 0:W], br[:, 0:W], tmp_r[:, 0:W]), "ve")
    b.w(nc.scalar, "ve", vn)
    sn = b.i(nc.scalar.activation(br[:, 0:W], br[:, 0:W], AF.Sqrt, bias=LN_EPS), "sc")
    b.w(nc.vector, "sc", sn)
    vn = b.i(nc.vector.reciprocal(br[:, 0:W], br[:, 0:W]), "ve")
    b.w(nc.vector, "ve", vn)
    nc.vector.tensor_mul(ar[:, 0:W], ar[:, 0:W], br[:, 0:W])
    nc.vector.tensor_scalar(ar[:, 0:W], ar[:, 0:W], -1.0, None, ALU.mult)
    nc.vector.tensor_copy(tmp_r[:, 0:W], ar[:, 0:W])
    nc.vector.tensor_copy(ar[:, 0:W], br[:, 0:W])
    vn = b.i(nc.vector.tensor_copy(br[:, 0:W], tmp_r[:, 0:W]), "ve")
    b.barrier()
    if sub == 1:
        return
    for (o, n) in _slices(W, 512):
        mm = b.i(nc.tensor.matmul(ps_a[:, 0:n], ones_row, ar[:, o:o + n],
                                  start=True, stop=True), "pe")
        b.w(nc.vector, "pe", mm)
        vn = b.i(nc.vector.tensor_copy(scr["alpha_b"][:, o:o + n], ps_a[:, 0:n]), "ve")
        b.w(nc.tensor, "ve", vn)
        mm = b.i(nc.tensor.matmul(ps_a[:, 0:n], ones_row, br[:, o:o + n],
                                  start=True, stop=True), "pe")
        b.w(nc.vector, "pe", mm)
        vn = b.i(nc.vector.tensor_copy(scr["beta_b"][:, o:o + n], ps_a[:, 0:n]), "ve")
        b.w(nc.tensor, "ve", vn)
    b.barrier()


def emit_ln_apply(b, x_sl, scr, o, n, out_bf, tmp):
    nc = b.nc
    nc.vector.tensor_mul(tmp[:, 0:n], x_sl[:, 0:n], scr["alpha_b"][:, o:o + n])
    return b.i(nc.vector.tensor_add(out_bf[:, 0:n], tmp[:, 0:n],
                                    scr["beta_b"][:, o:o + n]), "ve")


def build_l0ffn(W):
    b = B()
    x_in = b.inp("x_in", [128, DC, W])
    w1 = b.inp("w1", [8, 128, DC * 512])
    w2 = b.inp("w2", [8, 128, 4 * D])
    rw = b.inp("rw", [D, 8])
    b1 = b.inp("b1", [128, F0 // 128])
    b2 = b.inp("b2", [128, DC])
    x_out = b.out("x_out", [128, DC, W])
    w1b = b.dram("w1b", [8, 128, DC * 512], BF16)
    w2b = b.dram("w2b", [8, 128, 4 * D], BF16)
    b.reg_fconst(LN_EPS)
    emit_wcast(b, [(w1, w1b), (w2, w2b)])
    with b.scope():
        emit_l0ffn_stage(b, x_in, x_out, W, w1b, w2b, rw, b1, b2)
    return b.finish()


def emit_l0ffn_stage(b, x_in, x_out, W, w1b, w2b, rw, b1, b2):
    nc = b.nc
    import os
    phase = int(os.environ.get("KDEBUG_PHASE", "9"))
    ps_a = b.psum("ps_a", [128, 512])
    ps_b = b.psum("ps_b", [128, 512])
    ps_e = b.psum("ps_e", [128, 4, 512])

    scr = alloc_ln_scratch(b, W)
    emit_ln_stats(b, x_in, W, scr, ps_a, ps_b)
    if os.environ.get("KDEBUG_DUMP") and not getattr(b, "_dbg_used", False):
        b._dbg_used = True
        d_ab = b.out("d_ab", [2, W])
        b.store(d_ab[0:1, :], scr["alpha_r"][:, 0:W])
        b.store(d_ab[1:2, :], scr["beta_r"][:, 0:W])
        b.dbg_fac = b.out("d_fac", [8, 512])
        b.dbg_ftm = b.out("d_ftm", [128, 4])
        b.dbg_r = b.out("d_r", [128, 24])
        b.dbg_gh = b.out("d_gh", [128, 4, 512], BF16)
        b.dbg_eo = b.out("d_eo", [128, 2, 256])
        b.dbg_ft2 = b.out("d_ft2", [4, 128])
        b.dbg_xn = b.out("d_xn", [128, W], BF16)
    else:
        b.dbg_fac = None
        b.dbg_xn = None
    if phase == 1:
        return

    rws = b.sbuf("rws", [128, DC, 8], BF16)
    rwf = b.sbuf("rwf", [128, DC, 8], F32)
    for c in range(DC):
        b.load(rwf[:, c, :], rw[c * 128:(c + 1) * 128, :])
    b.w(nc.vector, "dl")
    b.i(nc.vector.tensor_copy(rws[:, :, :], rwf[:, :, :]), "ve")
    b1s = b.sbuf("b1s", [128, F0 // 128])
    b2s = b.sbuf("b2s", [128, DC])
    b.load(b1s, b1)
    b.load(b2s, b2)
    b.barrier()

    emit_l0ffn_slices(b, x_in, x_out, W, scr, w1b, w2b, rws, b1s, b2s,
                      ps_a, ps_b, ps_e)


def emit_l0ffn_slices(b, x_in, x_out, W, scr, w1b, w2b, rws, b1s, b2s,
                      ps_a, ps_b, ps_e):
    nc = b.nc
    ident = _ident128(b)
    ones_row = _ones_row(b)
    NF = F0 // 128

    x_s = [b.sbuf(f"l0x{i}", [128, DC, 512], F32) for i in range(2)]
    xn_s = b.sbuf("l0xn", [128, DC, 512], BF16)
    tmp = b.sbuf("l0tmp", [128, 512], F32)
    gh = b.sbuf("l0gh", [128, NF, 512], BF16)
    w1blk = [b.sbuf(f"l0w1{i}", [128, DC, 512], BF16) for i in range(2)]
    w2g = [b.sbuf(f"l0w2{i}", [128, 4, D], BF16) for i in range(2)]
    eo = b.sbuf("l0eo", [128, 512], F32)
    b2s_dbg = b.sbuf("l0eodbg", [128, 512], F32)
    t1 = b.sbuf("l0t1", [128, 512], F32)
    r_pr = b.sbuf("l0rpr", [128, 8], F32)
    r_m1 = b.sbuf("l0rm1", [128, 1], F32)
    r_m2 = b.sbuf("l0rm2", [128, 1], F32)
    r_s = b.sbuf("l0rs", [128, 1], F32)
    r_ge = b.sbuf("l0rge", [128, 8], F32)
    r_t1 = b.sbuf("l0rt1", [128, 8], F32)
    fac_tm = b.sbuf("l0ftm", [128, 4], F32)
    fac_t2 = b.sbuf("l0ft2", [4, 128], F32)
    fac_row = b.sbuf("l0frow", [1, 512], F32)
    fac_b = b.sbuf("l0fb", [128, 512], F32)
    fac_dram = b.dram("l0fdram", [4, 128], F32)

    import os
    nsl = int(os.environ.get("KDEBUG_NSLICE", "99"))
    parts = os.environ.get("KDEBUG_PARTS", "rab")
    for si, (o, n) in enumerate(_slices(W, 512)):
        if si >= nsl:
            break
        k = si % 2
        b.w(nc.sync, "st")
        ln_x = b.load(x_s[k][:, :, 0:n], x_in[:, :, o:o + n])
        b.w(nc.vector, "dl", ln_x)
        for c in range(DC):
            emit_ln_apply(b, x_s[k][:, c, 0:n], scr, o, n, xn_s[:, c, 0:n], tmp)
        vn_xn = b.n["ve"]
        if b.dbg_xn is not None:
            b.w(nc.sync, "ve", vn_xn)
            b.store(b.dbg_xn[:, o:o + n], xn_s[:, 0, 0:n])

        rstop = int(os.environ.get("KDEBUG_RSTOP", "9"))
        nt = (n + 127) // 128
        for t in range(nt if "r" in parts else 0):
            tn = min(128, n - t * 128)
            b.w(nc.tensor, "ve", vn_xn)
            b.w(nc.tensor, "sc")
            b.w(nc.tensor, "ve")
            for c in range(DC):
                mm = nc.tensor.matmul(ps_b[0:tn, 0:8],
                                      xn_s[:, c, t * 128:t * 128 + tn],
                                      rws[:, c, :], start=(c == 0), stop=(c == DC - 1))
            pe_r = b.i(mm, "pe")
            b.w(nc.vector, "pe", pe_r)
            ve1 = b.i(nc.vector.reduce_max(r_m1[0:tn, 0:1], ps_b[0:tn, 0:8], axis=AX,
                                           negate=True), "ve")
            b.w(nc.scalar, "ve", ve1)
            sc1 = b.i(nc.scalar.activation(r_pr[0:tn, :], ps_b[0:tn, 0:8], AF.Exp,
                                           bias=r_m1[0:tn, 0:1]), "sc")
            b.w(nc.vector, "sc", sc1)
            if b.dbg_fac is not None and si == 0 and t == 0:
                ve_d = b.i(nc.vector.tensor_copy(fac_b[:, 0:8], ps_b[0:tn, 0:8]), "ve")
                b.w(nc.sync, "ve", ve_d)
                b.store(b.dbg_r[:, 0:8], fac_b[:, 0:8])
                b.store(b.dbg_r[:, 8:16], r_pr[:, 0:8])
                b.store(b.dbg_r[:, 16:17], r_m1[:, 0:1], allow_slow_non_contiguous=True)
                b.w(nc.vector, "st")
            if rstop <= 0:
                continue
            ones8 = b.const_sbuf("ones8", np.ones((128, 8), np.float32))
            zero1 = b.const_sbuf("zero1", np.zeros((128, 1), np.float32))
            nc.vector.reduce_max(r_m1[0:tn, 0:1], r_pr[0:tn, :], axis=AX, negate=True)
            nc.vector.scalar_tensor_tensor(r_t1[0:tn, :], ones8[0:tn, :],
                                           r_m1[0:tn, 0:1], r_pr[0:tn, :],
                                           ALU.mult, ALU.add)
            nc.vector.tensor_scalar(r_ge[0:tn, :], r_t1[0:tn, :], 0.0,
                                    None, ALU.is_ge)
            nc.vector.scalar_tensor_tensor(r_t1[0:tn, :], r_ge[0:tn, :], -1e30,
                                           r_pr[0:tn, :], ALU.mult, ALU.add)
            nc.vector.reduce_max(r_m2[0:tn, 0:1], r_t1[0:tn, :], axis=AX)
            nc.vector.tensor_sub(r_s[0:tn, 0:1], r_m2[0:tn, 0:1], r_m1[0:tn, 0:1])
            nc.vector.tensor_sub(r_m1[0:tn, 0:1], zero1[0:tn, 0:1], r_m1[0:tn, 0:1])
            if b.dbg_fac is not None and si == 0 and t == 0:
                ved = b.i(nc.vector.tensor_copy(fac_b[:, 8:16], r_t1[:, 0:8]), "ve")
                b.w(nc.sync, "ve", ved)
                b.store(b.dbg_r[:, 17:18], r_m1[:, 0:1], allow_slow_non_contiguous=True)
                b.store(b.dbg_r[:, 18:19], r_m2[:, 0:1], allow_slow_non_contiguous=True)
                b.store(b.dbg_r[:, 19:20], r_s[:, 0:1], allow_slow_non_contiguous=True)
                b.w(nc.sync, "ve", ved)
                b.store(b.dbg_r[:, 8:16], fac_b[:, 8:16])
                b.w(nc.vector, "st")
            nc.vector.reciprocal(r_s[0:tn, 0:1], r_s[0:tn, 0:1])
            nc.vector.tensor_mul(r_m1[0:tn, 0:1], r_m1[0:tn, 0:1], r_s[0:tn, 0:1])
            nc.vector.tensor_mul(r_m2[0:tn, 0:1], r_m2[0:tn, 0:1], r_s[0:tn, 0:1])
            ve2 = b.i(nc.vector.tensor_add(fac_tm[0:tn, t:t + 1], r_m1[0:tn, 0:1],
                                           r_m2[0:tn, 0:1]), "ve")
            if tn < 128 and si == 0 and t == 0:
                pass
        if "r" not in parts or rstop <= 1:
            nc.vector.memset(fac_tm[:, 0:4], 1.0)
        b.w(nc.tensor, "ve")
        pe_t = b.i(nc.tensor.transpose(ps_a[0:4, 0:128], fac_tm[:, 0:4], ident), "pe")
        b.w(nc.vector, "pe", pe_t)
        ve3 = b.i(nc.vector.tensor_copy(fac_t2[:, :], ps_a[0:4, 0:128]), "ve")
        b.w(nc.sync, "ve", ve3)
        if b.dbg_fac is not None and si == 0:
            b.store(b.dbg_ftm, fac_tm[:, 0:4])
            b.store(b.dbg_ft2, fac_t2)
        if rstop <= 2:
            nc.vector.memset(fac_row[0:1, 0:n], 1.0)
        st1 = b.store(fac_dram, fac_t2)
        nc.sync.wait_ge(b.sems["st"], st1)
        if rstop > 2:
            ln_f = b.load(fac_row[0:1, 0:n],
                          fac_dram.rearrange("a b -> (a b)").unsqueeze(0)[:, 0:n])
            b.w(nc.tensor, "dl", ln_f)
        b.w(nc.vector, "dl")
        b.w(nc.tensor, "ve")
        b.w(nc.tensor, "sc")
        mm = nc.tensor.matmul(ps_b[:, 0:n], ones_row, fac_row[0:1, 0:n],
                              start=True, stop=True)
        pe_f = b.i(mm, "pe")
        b.w(nc.vector, "pe", pe_f)
        ve_f = b.i(nc.vector.tensor_copy(fac_b[:, 0:n], ps_b[:, 0:n]), "ve")
        if b.dbg_fac is not None:
            b.w(nc.sync, "dl")
            b.store(b.dbg_fac[si:si + 1, 0:n], fac_row[0:1, 0:n])

        last_gelu = {0: 0, 1: 0}
        last_w1pe = [0, 0]
        for fgi in (range(F0 // 512) if "a" in parts else []):
            kk = fgi % 2
            if last_w1pe[kk]:
                b.w(nc.sync, "pe", last_w1pe[kk])
            ln_w = b.load(w1blk[kk], w1b[fgi])
            for f4 in range(4):
                f = fgi * 4 + f4
                pp = f % 2
                ps_h = ps_a if pp == 0 else ps_b
                b.w(nc.tensor, "dl", ln_w)
                b.w(nc.tensor, "ve", max(vn_xn, ve_f))
                if last_gelu[pp]:
                    b.w(nc.tensor, "sc", last_gelu[pp])
                for c in range(DC):
                    mm = nc.tensor.matmul(ps_h[:, 0:n],
                                          w1blk[kk][:, c, f4 * 128:(f4 + 1) * 128],
                                          xn_s[:, c, 0:n],
                                          start=(c == 0), stop=(c == DC - 1))
                pe_h = b.i(mm, "pe")
                last_w1pe[kk] = pe_h
                b.w(nc.scalar, "pe", pe_h)
                last_gelu[pp] = b.i(
                    nc.scalar.activation(gh[:, f, 0:n], ps_h[:, 0:n],
                                         AF.Gelu_apprx_tanh, bias=b1s[:, f:f + 1]),
                    "sc")
        sc_gh = b.n["sc"]
        if b.dbg_fac is not None and si == 0:
            b.w(nc.sync, "sc", sc_gh)
            for ff in range(4):
                b.store(b.dbg_gh[:, ff, :], gh[:, ff * 8, 0:n])

        last_w2pe = [0, 0]
        for half in (range(2) if "b" in parts else []):
            ve_before = b.n["ve"]
            for fg in range(NF // 4):
                kk = fg % 2
                if last_w2pe[kk]:
                    b.w(nc.sync, "pe", last_w2pe[kk])
                ln2 = b.load(w2g[kk], w2b[fg])
                b.w(nc.tensor, "dl", ln2)
                b.w(nc.tensor, "ve", ve_before)
                b.w(nc.tensor, "sc", sc_gh)
                for fr in range(4):
                    f = fg * 4 + fr
                    for ci in range(4):
                        c = half * 4 + ci
                        mm = nc.tensor.matmul(ps_e[:, ci, 0:n],
                                              w2g[kk][:, fr, c * 128:(c + 1) * 128],
                                              gh[:, f, 0:n],
                                              start=(f == 0), stop=(f == NF - 1),
                                              skip_group_check=True)
                last_w2pe[kk] = b.i(mm, "pe")
            pe_eo = b.n["pe"]
            for ci in range(4):
                c = half * 4 + ci
                b.w(nc.vector, "pe", pe_eo)
                nc.vector.tensor_scalar(eo[:, 0:n], ps_e[:, ci, 0:n],
                                        b2s[:, c:c + 1], None, ALU.add)
                nc.vector.tensor_mul(t1[:, 0:n], eo[:, 0:n], fac_b[:, 0:n])
                if b.dbg_fac is not None and si == 0 and c == 0:
                    ved2 = b.i(nc.vector.tensor_copy(b2s_dbg[:, 0:n], eo[:, 0:n]), "ve")
                    b.w(nc.sync, "ve", ved2)
                    b.store(b.dbg_eo[:, 0, 0:256], b2s_dbg[:, 0:256])
                    b.store(b.dbg_eo[:, 1, 0:256], b2s_dbg[:, 256:512])
                    b.w(nc.vector, "st")
                ve_o = b.i(
                    nc.vector.scalar_tensor_tensor(x_s[k][:, c, 0:n],
                                                   t1[:, 0:n], 0.5,
                                                   x_s[k][:, c, 0:n],
                                                   ALU.mult, ALU.add), "ve")
                b.w(nc.sync, "ve", ve_o)
                b.store(x_out[:, c, o:o + n], x_s[k][:, c, 0:n])
        b.barrier()


def _pm(w, *resh_perm):
    resh, perm, out = resh_perm
    a = np.ascontiguousarray(np.asarray(w, np.float32).reshape(resh).transpose(perm))
    return np.ascontiguousarray(a.reshape(out))


def _fm_vec(vec, nchunk):
    return np.ascontiguousarray(
        np.asarray(vec, np.float32).reshape(nchunk, 128).T)


def _shard_x(x0):
    x0T = np.ascontiguousarray(np.asarray(x0, np.float32).transpose(0, 2, 1))
    out = []
    for c in range(NC):
        bb, kk = c // 4, c % 4
        lo = kk * T_OWN - HALO
        sl = np.zeros((D, T_EXT), np.float32)
        src_lo = max(lo, 0)
        sl[:, src_lo - lo:] = x0T[bb, :, src_lo:(kk + 1) * T_OWN]
        out.append(np.ascontiguousarray(
            sl.reshape(DC, 128, T_EXT).transpose(1, 0, 2)))
    return out


def _unshard_x(parts, width, col0):
    x = np.zeros((2, 8192, D), np.float32)
    for c in range(NC):
        bb, kk = c // 4, c % 4
        sl = np.asarray(parts[c], np.float32)[:, :, col0:col0 + T_OWN]
        x[bb, kk * T_OWN:(kk + 1) * T_OWN, :] = (
            sl.transpose(1, 0, 2).reshape(D, T_OWN).T)
    return x


def run_l0ffn_stage(xparts, W, col_off, inputs, counter=[0]):
    nc = build_l0ffn(W)
    wp = _prep_weights(inputs)
    in_maps = []
    for c in range(NC):
        in_maps.append(dict(
            x_in=np.ascontiguousarray(xparts[c][:, :, col_off:col_off + W]),
            w1=wp["w1"], w2=wp["w2"], rw=wp["rw0"], b1=wp["b1"], b2=wp["b2"]))
    res = run_bass_kernel_spmd(nc, in_maps, core_ids=list(range(NC)))
    return [np.asarray(res.results[c]["x_out"]) for c in range(NC)]


def _prep_weights(inputs):
    wp = {}
    wp["w1"] = _pm(inputs["L0_ffn_w1"], (DC, 128, 8, 512), (2, 1, 0, 3),
                   (8, 128, DC * 512))
    wp["w2"] = _pm(inputs["L0_ffn_w2"], (8, 4, 128, D), (0, 2, 1, 3),
                   (8, 128, 4 * D))
    wp["rw0"] = np.ascontiguousarray(np.asarray(inputs["L0_ffn_router_w"], np.float32))
    wp["b1"] = _fm_vec(inputs["L0_ffn_b1"], F0 // 128)
    wp["b2"] = _fm_vec(inputs["L0_ffn_b2"], DC)
    for nm, key in (("wq", "L0_attn_Wq"), ("wk", "L0_attn_Wk"),
                    ("wv", "L0_attn_Wv"), ("wo", "L0_attn_Wo")):
        wp[nm] = _pm(np.asarray(inputs[key], np.float32).T,
                     (DC, 128, D), (1, 0, 2), (1, 128, DC * D))
    wp["bq"] = _fm_vec(inputs["L0_attn_bq"], DC)
    wp["bk"] = _fm_vec(inputs["L0_attn_bk"], DC)
    wp["bv"] = np.asarray(inputs["L0_attn_bv"], np.float32).reshape(1, D)
    wp["bo"] = _fm_vec(inputs["L0_attn_bo"], DC)
    wp["rw1"] = _pm(np.asarray(inputs["L1_ffn_router_w"], np.float32).T,
                    (DC, 128, NE), (1, 0, 2), (1, 128, DC * NE))
    wp["rbias"] = np.ascontiguousarray(np.broadcast_to(
        np.asarray(inputs["L1_ffn_bias"], np.float32).reshape(1, NE), (128, NE)))
    wp["gw"] = _pm(inputs["L1_ffn_gate_w"], (NE, DC, 128, D), (0, 2, 1, 3),
                   (NE, 128, DC * D))
    wp["uw"] = _pm(inputs["L1_ffn_up_w"], (NE, DC, 128, D), (0, 2, 1, 3),
                   (NE, 128, DC * D))
    wp["dw"] = _pm(inputs["L1_ffn_down_w"], (NE, DC, 128, D), (0, 2, 1, 3),
                   (NE, 128, DC * D))
    wp["shg"] = _pm(inputs["L1_ffn_sh_gate_w"], (DC, 128, 2, D), (2, 1, 0, 3),
                    (2, 128, DC * D))
    wp["shu"] = _pm(inputs["L1_ffn_sh_up_w"], (DC, 128, 2, D), (2, 1, 0, 3),
                    (2, 128, DC * D))
    wp["shd"] = _pm(inputs["L1_ffn_sh_down_w"], (2, DC, 128, D), (0, 2, 1, 3),
                    (2, 128, DC * D))
    return wp


def kernel_multilaunch(inputs):
    x0 = np.asarray(inputs["L0_attn_hidden_state"], np.float32)
    xp = _shard_x(x0)
    x1 = run_l0ffn_stage(xp, T_EXT, 0, inputs)
    x2 = run_attn_stage(x1, inputs)
    x3 = run_l0ffn_stage(x2, W_ATT, 0, inputs)
    x4 = run_l1ffn_stage(x3, W_ATT, 0, inputs)
    x4m = [p[:, :, W_ATT - W_M:] for p in x4]
    x5 = run_mamba_stage(x4m, inputs)
    x6 = run_l1ffn_stage(x5, T_OWN, 0, inputs)
    return _unshard_x(x6, T_OWN, 0)


def build_attn_a():
    b = B()
    W = T_EXT
    b.reg_fconst(LN_EPS)
    x_in = b.inp("x_in", [128, DC, W])
    wq = b.inp("wq", [1, 128, DC * D])
    wk = b.inp("wk", [1, 128, DC * D])
    wv = b.inp("wv", [1, 128, DC * D])
    bq = b.inp("bq", [128, DC])
    bk = b.inp("bk", [128, DC])
    bv = b.inp("bv", [1, D])
    cosq = b.inp("cosq", [128, 4, W])
    sinq = b.inp("sinq", [128, 4, W])
    cosk = b.inp("cosk", [128, 4, W])
    sink = b.inp("sink", [128, 4, W])
    q_out = b.out("q_out", [128, DC, W_ATT], BF16)
    k_out = b.out("k_out", [128, DC, W], BF16)
    v_out = b.out("v_out", [128, W // 128, D], BF16)
    wqb = b.dram("wqb", [1, 128, DC * D], BF16)
    wkb = b.dram("wkb", [1, 128, DC * D], BF16)
    wvb = b.dram("wvb", [1, 128, DC * D], BF16)
    emit_wcast(b, [(wq, wqb), (wk, wkb), (wv, wvb)])
    with b.scope():
        emit_attn_a_stage(b, x_in, q_out, k_out, v_out, wqb, wkb, wvb,
                          bq, bk, bv, cosq, sinq, cosk, sink)
    return b.finish()


def emit_attn_a_stage(b, x_in, q_out, k_out, v_out, wqb, wkb, wvb,
                      bq, bk, bv, cosq, sinq, cosk, sink):
    nc = b.nc
    W = T_EXT
    QOFF = HALO - 128
    ones_row = _ones_row(b)
    ps_a = b.psum("ps_a", [128, 512])
    ps_b = b.psum("ps_b", [128, 512])
    ps_v = b.psum("ps_v", [128, 1024])

    scr = alloc_ln_scratch(b, W)
    emit_ln_stats(b, x_in, W, scr, ps_a, ps_b)

    bqs = b.sbuf("bqs", [128, DC]); b.load(bqs, bq)
    bks = b.sbuf("bks", [128, DC]); b.load(bks, bk)
    bvs = b.sbuf("bvs", [1, D]);    b.load(bvs, bv)
    b.barrier()

    x_s = b.sbuf("a_x", [128, DC, 512])
    xn_s = b.sbuf("a_xn", [128, DC, 512], BF16)
    tmp = b.sbuf("a_tmp", [128, 512])
    tmp2 = b.sbuf("a_tmp2", [128, 512])
    kraw = b.sbuf("a_kraw", [128, DC, 512])
    rope_o = b.sbuf("a_ro", [128, DC, 512], BF16)
    trig = [b.sbuf(f"a_trig{i}", [128, 512]) for i in range(2)]
    wcur = b.sbuf("a_wcur", [128, DC, D], BF16)
    v_bf = b.sbuf("a_vbf", [128, 1024], BF16)

    for (o, n) in _slices(W, 512):
        st_before = b.n["st"]
        b.w(nc.sync, "ve")
        b.w(nc.sync, "pe")
        ln_x = b.load(x_s[:, :, 0:n], x_in[:, :, o:o + n])
        b.w(nc.vector, "dl", ln_x)
        b.w(nc.vector, "pe")
        for c in range(DC):
            emit_ln_apply(b, x_s[:, c, 0:n], scr, o, n, xn_s[:, c, 0:n], tmp)
        ve_xn = b.n["ve"]

        for proj, wb, bias_s, cos_in, sin_in, out_hbm, q_off in (
                ("k", wkb, bks, cosk, sink, k_out, 0),
                ("q", wqb, bqs, cosq, sinq, q_out, QOFF)):
            if o + n <= q_off:
                continue
            b.w(nc.sync, "pe")
            lw = b.load(wcur, wb[0])
            b.w(nc.tensor, "dl", lw)
            for oc in range(DC):
                kk = oc % 2
                ps = ps_a if kk == 0 else ps_b
                b.w(nc.tensor, "ve", ve_xn)
                b.w(nc.tensor, "ve")
                for ic in range(DC):
                    mm = nc.tensor.matmul(ps[:, 0:n],
                                          wcur[:, ic, oc * 128:(oc + 1) * 128],
                                          xn_s[:, ic, 0:n],
                                          start=(ic == 0), stop=(ic == DC - 1))
                pe_k = b.i(mm, "pe")
                b.w(nc.vector, "pe", pe_k)
                b.i(nc.vector.tensor_scalar(kraw[:, oc, 0:n], ps[:, 0:n],
                                            bias_s[:, oc:oc + 1], None, ALU.add),
                    "ve")
            oo = o - q_off
            for c in range(4):
                b.w(nc.sync, "ve")
                lc = b.load(trig[0][:, 0:n], cos_in[:, c, o:o + n])
                ls = b.load(trig[1][:, 0:n], sin_in[:, c, o:o + n])
                b.w(nc.vector, "dl", ls)
                b.w(nc.vector, "st", st_before)
                nc.vector.tensor_mul(tmp[:, 0:n], kraw[:, c, 0:n], trig[0][:, 0:n])
                nc.vector.tensor_mul(tmp2[:, 0:n], kraw[:, c + 4, 0:n], trig[1][:, 0:n])
                b.i(nc.vector.tensor_sub(rope_o[:, c, 0:n], tmp[:, 0:n],
                                         tmp2[:, 0:n]), "ve")
                nc.vector.tensor_mul(tmp[:, 0:n], kraw[:, c, 0:n], trig[1][:, 0:n])
                nc.vector.tensor_mul(tmp2[:, 0:n], kraw[:, c + 4, 0:n], trig[0][:, 0:n])
                b.i(nc.vector.tensor_add(rope_o[:, c + 4, 0:n], tmp[:, 0:n],
                                         tmp2[:, 0:n]), "ve")
            b.w(nc.sync, "ve")
            b.store(out_hbm[:, :, oo:oo + n], rope_o[:, :, 0:n])

        b.w(nc.sync, "pe")
        lw = b.load(wcur, wvb[0])
        b.w(nc.tensor, "dl", lw)
        nt = (n + 127) // 128
        for t in range(nt):
            tn = min(128, n - t * 128)
            vt = (o + t * 128) // 128
            for ov in range(2):
                b.w(nc.tensor, "ve", ve_xn)
                b.w(nc.tensor, "ve")
                for ic in range(DC):
                    mm = nc.tensor.matmul(ps_v[0:tn, ov * 512:(ov + 1) * 512],
                                          xn_s[:, ic, t * 128:t * 128 + tn],
                                          wcur[:, ic, ov * 512:(ov + 1) * 512],
                                          start=(ic == 0), stop=False,
                                          skip_group_check=True)
                mm = nc.tensor.matmul(ps_v[0:tn, ov * 512:(ov + 1) * 512],
                                      ones_row[:, 0:tn],
                                      bvs[:, ov * 512:(ov + 1) * 512],
                                      start=False, stop=True, skip_group_check=True)
                pe_v = b.i(mm, "pe")
                b.w(nc.vector, "pe", pe_v)
                ve_v = b.i(nc.vector.tensor_copy(v_bf[0:tn, ov * 512:(ov + 1) * 512],
                                                 ps_v[0:tn, ov * 512:(ov + 1) * 512]),
                           "ve")
                b.w(nc.sync, "ve", ve_v)
                b.store(v_out[0:tn, vt, ov * 512:(ov + 1) * 512],
                        v_bf[0:tn, ov * 512:(ov + 1) * 512])


def build_attn_b():
    b = B()
    W = T_EXT
    NT = W_ATT // 128
    b.reg_fconst(1.0)
    x_in = b.inp("x_in", [128, DC, W])
    q_in = b.inp("q_in", [128, DC, W_ATT], BF16)
    k_in = b.inp("k_in", [128, DC, W], BF16)
    v_in = b.inp("v_in", [128, W // 128, D], BF16)
    wo = b.inp("wo", [1, 128, DC * D])
    bo = b.inp("bo", [128, DC])
    maskA = b.inp("maskA", [NT, 128, HALO])
    maskT = b.inp("maskT", [NT, 128, HALO])
    x_out = b.out("x_out", [128, DC, W_ATT])
    wob = b.dram("wob", [1, 128, DC * D], BF16)
    emit_wcast(b, [(wo, wob)])
    with b.scope():
        emit_attn_b_stage(b, x_in, q_in, k_in, v_in, x_out, wob, bo,
                          maskA, maskT)
    return b.finish()


def emit_attn_b_stage(b, x_in, q_in, k_in, v_in, x_out, wob, bo, maskA, maskT):
    nc = b.nc
    W = T_EXT
    NT = W_ATT // 128
    ident = _ident128(b)
    ones_row = _ones_row(b)
    ps_a = b.psum("ps_a", [128, 512])
    ps_A = b.psum("ps_A", [128, 1024])
    ps_T = b.psum("ps_T", [128, 1024])
    ps_o = b.psum("ps_o", [128, 256])

    bos = b.sbuf("bos", [128, DC]); b.load(bos, bo)
    wo_r = b.sbuf("b_wo", [128, DC, D], BF16)
    b.load(wo_r, wob[0])

    k_t = [b.sbuf(f"b_k{i}", [128, DC, HALO], BF16) for i in range(2)]
    v_t = [b.sbuf(f"b_v{i}", [128, 5, D], BF16) for i in range(2)]
    q_t = [b.sbuf(f"b_q{i}", [128, DC, 128], BF16) for i in range(2)]
    mA_t = [b.sbuf(f"b_mA{i}", [128, HALO]) for i in range(2)]
    mT_t = [b.sbuf(f"b_mT{i}", [128, HALO]) for i in range(2)]
    x_t = [b.sbuf(f"b_x{i}", [128, DC, 128]) for i in range(2)]
    pT = b.sbuf("b_pT", [128, HALO], BF16)
    pA_junk = b.sbuf("b_pA", [128, HALO], BF16)
    av = b.sbuf("b_av", [128, DC, 128], BF16)
    Zr = b.sbuf("b_Z", [128, 1])
    mneg = b.sbuf("b_mneg", [128, 1])
    cneg = b.sbuf("b_cneg", [128, 1])
    crow = b.sbuf("b_crow", [1, 128])
    t1 = b.sbuf("b_t1", [128, 128])
    b.barrier()

    last_use = [0, 0]
    for ti in range(NT):
        kk = ti % 2
        if last_use[kk]:
            b.w(nc.sync, "pe", last_use[kk])
            b.w(nc.sync, "ve")
            b.w(nc.sync, "st")
        kv0 = ti * 128
        ln1 = b.load(k_t[kk][:, :, :], k_in[:, :, kv0:kv0 + HALO])
        b.load(v_t[kk].rearrange("p f d -> p (f d)"),
               v_in[:, kv0 // 128:kv0 // 128 + 5, :].rearrange("p f d -> p (f d)"))
        b.load(q_t[kk][:, :, :], q_in[:, :, ti * 128:(ti + 1) * 128])
        b.load(mA_t[kk], maskA[ti])
        b.load(mT_t[kk], maskT[ti])
        ln2 = b.load(x_t[kk][:, :, :], x_in[:, :, HALO - 128 + ti * 128:
                                            HALO - 128 + (ti + 1) * 128])
        b.w(nc.tensor, "dl", ln2)
        for h in range(16):
            hc, hr = h // 2, (h % 2) * 64
            b.w(nc.tensor, "ve")
            b.w(nc.tensor, "sc")
            for j, (ko, kn) in enumerate(_slices(HALO, 512)):
                mm = nc.tensor.matmul(ps_A[:, ko:ko + kn],
                                      q_t[kk][hr:hr + 64, hc, :],
                                      k_t[kk][hr:hr + 64, hc, ko:ko + kn],
                                      start=True, stop=True)
            pe_s = b.i(mm, "pe")
            b.w(nc.vector, "pe", pe_s)
            nc.vector.tensor_add(ps_A[:, 0:HALO], ps_A[:, 0:HALO], mA_t[kk])
            b.i(nc.vector.reduce_max(mneg, ps_A[:, 0:HALO], axis=AX, negate=True),
                "ve")
            b.w(nc.scalar, "ve")
            sc_e = b.i(nc.scalar.activation(pA_junk, ps_A[:, 0:HALO], AF.Exp,
                                            bias=mneg, accum_out=Zr), "sc")
            b.w(nc.scalar, "sc", sc_e)
            sc_l = b.i(nc.scalar.activation(Zr, Zr, AF.Ln), "sc")
            b.w(nc.vector, "sc", sc_l)
            nc.vector.tensor_sub(cneg, mneg, Zr)
            ve_c = b.i(nc.vector.tensor_scalar(cneg, cneg, 1e37, None, ALU.min),
                       "ve")
            b.w(nc.tensor, "ve", ve_c)
            pe_t = b.i(nc.tensor.transpose(ps_o[0:1, 128:256], cneg, ident), "pe")
            b.w(nc.vector, "pe", pe_t)
            ve_cr = b.i(nc.vector.tensor_copy(crow, ps_o[0:1, 128:256]), "ve")
            b.w(nc.tensor, "ve", ve_cr)
            b.w(nc.tensor, "sc")
            for j in range(5):
                nc.tensor.matmul(ps_T[:, j * 128:(j + 1) * 128],
                                 k_t[kk][hr:hr + 64, hc, j * 128:(j + 1) * 128],
                                 q_t[kk][hr:hr + 64, hc, :],
                                 start=True, stop=False, skip_group_check=True)
                mm = nc.tensor.matmul(ps_T[:, j * 128:(j + 1) * 128],
                                      ones_row, crow, start=False, stop=True,
                                      skip_group_check=True)
            pe_st = b.i(mm, "pe")
            b.w(nc.vector, "pe", pe_st)
            ve_m = b.i(nc.vector.tensor_add(ps_T[:, 0:HALO], ps_T[:, 0:HALO],
                                            mT_t[kk]), "ve")
            b.w(nc.scalar, "ve", ve_m)
            sc_p = b.i(nc.scalar.activation(pT, ps_T[:, 0:HALO], AF.Exp), "sc")
            b.w(nc.tensor, "sc", sc_p)
            if h % 2 == 0:
                b.w(nc.tensor, "ve")
            for j in range(5):
                mm = nc.tensor.matmul(ps_o[hr:hr + 64, 0:128],
                                      v_t[kk][:, j, h * 64:(h + 1) * 64],
                                      pT[:, j * 128:(j + 1) * 128],
                                      start=(j == 0), stop=(j == 4),
                                      skip_group_check=True)
            pe_pv = b.i(mm, "pe")
            if h % 2 == 1:
                b.w(nc.vector, "pe", pe_pv)
                b.i(nc.vector.tensor_copy(av[:, hc, :], ps_o[:, 0:128]), "ve")
        ve_av = b.n["ve"]
        for oc in range(DC):
            b.w(nc.tensor, "ve", ve_av)
            b.w(nc.tensor, "ve")
            for fc in range(DC):
                mm = nc.tensor.matmul(ps_a[:, 0:128], wo_r[:, fc, oc * 128:(oc + 1) * 128],
                                      av[:, fc, :], start=(fc == 0), stop=(fc == DC - 1))
            pe_o2 = b.i(mm, "pe")
            b.w(nc.vector, "pe", pe_o2)
            nc.vector.tensor_scalar(t1, ps_a[:, 0:128], bos[:, oc:oc + 1], None,
                                    ALU.add)
            ve_o = b.i(nc.vector.tensor_add(x_t[kk][:, oc, :], x_t[kk][:, oc, :],
                                            t1), "ve")
            b.w(nc.sync, "ve", ve_o)
            b.store(x_out[:, oc, ti * 128:(ti + 1) * 128], x_t[kk][:, oc, :])
        last_use[kk] = b.n["pe"]


def _rope_tables(positions, scale_q):
    half = 512
    inv = 1.0 / (10000.0 ** (np.arange(half, dtype=np.float64) / half))
    ang = positions.astype(np.float64)[None, :] * inv[:, None]
    cos = np.cos(ang).astype(np.float32) * scale_q
    sin = np.sin(ang).astype(np.float32) * scale_q
    W = positions.shape[0]
    return (np.ascontiguousarray(cos.reshape(4, 128, W).transpose(1, 0, 2)),
            np.ascontiguousarray(sin.reshape(4, 128, W).transpose(1, 0, 2)))


def _attn_masks(core):
    NT = W_ATT // 128
    bb, kk = core // 4, core % 4
    t0g = kk * T_OWN
    mA = np.zeros((NT, 128, HALO), np.float32)
    mT = np.zeros((NT, 128, HALO), np.float32)
    for ti in range(NT):
        qg = t0g - 128 + ti * 128 + np.arange(128)
        kg = t0g - HALO + ti * 128 + np.arange(HALO)
        valid = ((kg[None, :] <= qg[:, None])
                 & (kg[None, :] >= qg[:, None] - 512)
                 & (kg[None, :] >= 0) & (qg[:, None] >= 0))
        mA[ti] = np.where(valid, 0.0, NEG)
        vT = valid.reshape(128, 5, 128)
        mT[ti] = np.where(vT.transpose(2, 1, 0).reshape(128, HALO)
                          [: , :], 0.0, NEG)
    return mA, mT


def run_attn_stage(xparts, inputs):
    nc_a = build_attn_a()
    nc_b = build_attn_b()
    wp = _prep_weights(inputs)
    im_a, masks = [], []
    for c in range(NC):
        kk = c % 4
        pos = np.maximum(kk * T_OWN - HALO + np.arange(T_EXT), 0)
        cq, sq = _rope_tables(pos, 0.125)
        ck, sk = _rope_tables(pos, 1.0)
        mA, mT = _attn_masks(c)
        masks.append((mA, mT))
        im_a.append(dict(x_in=np.ascontiguousarray(xparts[c]), wq=wp["wq"],
                         wk=wp["wk"], wv=wp["wv"], bq=wp["bq"], bk=wp["bk"],
                         bv=wp["bv"], cosq=cq, sinq=sq, cosk=ck, sink=sk))
    res_a = run_bass_kernel_spmd(nc_a, im_a, core_ids=list(range(NC)))
    im_b = []
    for c in range(NC):
        ra = res_a.results[c]
        mA, mT = masks[c]
        im_b.append(dict(x_in=np.ascontiguousarray(xparts[c]),
                         q_in=np.asarray(ra["q_out"]), k_in=np.asarray(ra["k_out"]),
                         v_in=np.asarray(ra["v_out"]), wo=wp["wo"], bo=wp["bo"],
                         maskA=mA, maskT=mT))
    res_b = run_bass_kernel_spmd(nc_b, im_b, core_ids=list(range(NC)))
    return [np.asarray(res_b.results[c]["x_out"]) for c in range(NC)]


def emit_ln_to_xn(b, x_src, W, xn, ps_a, ps_b, tag="lnx"):
    nc = b.nc
    with ExitStack() as lctx:
        def sb(name, shape, dtype=F32):
            return lctx.enter_context(
                b.rawnc.sbuf_tensor(f"{tag}{name}", shape, dtype)).ap()
        scr = dict(
            xcast=[sb(f"xc{i}", [128, 512], BF16) for i in range(2)],
            sq=[sb(f"sq{i}", [128, 512], BF16) for i in range(2)],
            xs=[sb(f"xs{i}", [128, 512], F32) for i in range(2)],
            alpha_r=sb("ar", [1, W]),
            beta_r=sb("br", [1, W]),
            alpha_b=sb("ab", [128, W]),
            beta_b=sb("bb", [128, W]),
        )
        emit_ln_stats(b, x_src, W, scr, ps_a, ps_b)
        tmp = sb("tmp", [128, 512])
        xs2 = sb("xs2", [128, 512])
        for (o, n) in _slices(W, 512):
            for c in range(DC):
                ln = b.load(xs2[:, 0:n], x_src[:, c, o:o + n])
                b.w(nc.vector, "dl", ln)
                ve = emit_ln_apply(b, xs2[:, 0:n], scr, o, n, xn[:, c, o:o + n], tmp)
                b.w(nc.sync, "ve", ve)
        b.barrier()


def build_l1ffn(W):
    b = B()
    b.reg_fconst(LN_EPS)
    x_in = b.inp("x_in", [128, DC, W])
    rw = b.inp("rw", [1, 128, DC * NE])
    rbias = b.inp("rbias", [128, NE])
    gw = b.inp("gw", [NE, 128, DC * D])
    uw = b.inp("uw", [NE, 128, DC * D])
    dw = b.inp("dw", [NE, 128, DC * D])
    shg = b.inp("shg", [2, 128, DC * D])
    shu = b.inp("shu", [2, 128, DC * D])
    shd = b.inp("shd", [2, 128, DC * D])
    x_out = b.out("x_out", [128, DC, W])
    rwb = b.dram("rwb", [1, 128, DC * NE], BF16)
    gwb = b.dram("gwb", [NE, 128, DC * D], BF16)
    uwb = b.dram("uwb", [NE, 128, DC * D], BF16)
    dwb = b.dram("dwb", [NE, 128, DC * D], BF16)
    shgb = b.dram("shgb", [2, 128, DC * D], BF16)
    shub = b.dram("shub", [2, 128, DC * D], BF16)
    shdb = b.dram("shdb", [2, 128, DC * D], BF16)
    emit_wcast(b, [(rw, rwb), (gw, gwb), (uw, uwb), (dw, dwb),
                   (shg, shgb), (shu, shub), (shd, shdb)])
    with b.scope():
        emit_l1ffn_stage(b, x_in, x_out, W, rwb, gwb, uwb, dwb,
                         shgb, shub, shdb, rbias, tag="l1f")
    return b.finish()


def emit_l1ffn_stage(b, x_in, x_out, W, rwb, gwb, uwb, dwb, shgb, shub, shdb,
                     rbias, tag="l1f"):
    nc = b.nc
    NT = W // 128
    ident = _ident128(b)
    ones_row = _ones_row(b)
    ps_h = [b.psum(f"ps_h{i}", [128, 512]) for i in range(3)]
    ps_u = [b.psum(f"ps_u{i}", [128, 512]) for i in range(3)]
    ps_y = [b.psum(f"ps_y{i}", [128, 512]) for i in range(2)]

    _stat_lhs(b)
    ones4 = b.const_sbuf("ones4", np.ones((128, 4), np.float32))
    ones16 = b.const_sbuf("ones16", np.ones((128, 16), np.float32))
    xn = b.sbuf("f_xn", [128, DC, W], BF16)
    emit_ln_to_xn(b, x_in, W, xn, ps_h[0], ps_u[0])

    rws = b.sbuf("f_rws", [128, DC, NE], BF16)
    b.load(rws, rwb[0])
    bias_b = b.sbuf("f_biasb", [128, NE])
    ln_b = b.load(bias_b, rbias)
    sc_s = b.sbuf("f_sc", [128, NE])
    s_s = b.sbuf("f_s", [128, NE])
    ms_s = b.sbuf("f_ms", [128, NE])
    t16 = b.sbuf("f_t16", [128, NE])
    ge16 = b.sbuf("f_ge16", [128, NE])
    cur16 = b.sbuf("f_cur", [128, NE])
    gsc = b.sbuf("f_gsc", [128, 4])
    t4 = b.sbuf("f_t4", [128, 4])
    ge4 = b.sbuf("f_ge4", [128, 4])
    gmask = b.sbuf("f_gm", [128, 4])
    m1n = b.sbuf("f_m1n", [128, 1])
    m2r = b.sbuf("f_m2", [128, 1])
    tsum = b.sbuf("f_tsum", [128, 1])
    comb = b.sbuf("f_comb", [128, NE])
    c_T = b.sbuf("f_cT", [16, W])
    c_dram = b.dram(f"f_cdram_{tag}", [NE, W])
    b.w(nc.vector, "dl")
    b.barrier()

    for ti in range(NT):
        t0 = ti * 128
        b.w(nc.tensor, "ve")
        b.w(nc.tensor, "sc")
        for ic in range(DC):
            mm = nc.tensor.matmul(ps_h[0][:, 0:NE], xn[:, ic, t0:t0 + 128],
                                  rws[:, ic, :], start=(ic == 0),
                                  stop=(ic == DC - 1))
        pe_r = b.i(mm, "pe")
        b.w(nc.scalar, "pe", pe_r)
        sc1 = b.i(nc.scalar.activation(sc_s, ps_h[0][:, 0:NE], AF.Sigmoid), "sc")
        b.w(nc.vector, "sc", sc1)
        nc.vector.tensor_add(s_s, sc_s, bias_b)
        for g in range(4):
            sg = s_s[:, g * 4:(g + 1) * 4]
            nc.vector.reduce_max(m1n, sg, axis=AX, negate=True)
            nc.vector.scalar_tensor_tensor(t4, ones4[:, 0:4], m1n, sg,
                                           ALU.mult, ALU.add)
            nc.vector.tensor_scalar(ge4, t4, 0.0, None, ALU.is_ge)
            nc.vector.scalar_tensor_tensor(t4, ge4, -1e30, sg, ALU.mult, ALU.add)
            nc.vector.reduce_max(m2r, t4, axis=AX)
            nc.vector.tensor_sub(gsc[:, g:g + 1], m2r, m1n)
        nc.vector.reduce_max(m1n, gsc, axis=AX, negate=True)
        nc.vector.scalar_tensor_tensor(t4, ones4[:, 0:4], m1n, gsc,
                                       ALU.mult, ALU.add)
        nc.vector.tensor_scalar(ge4, t4, 0.0, None, ALU.is_ge)
        nc.vector.scalar_tensor_tensor(t4, ge4, -1e30, gsc, ALU.mult, ALU.add)
        nc.vector.reduce_max(m2r, t4, axis=AX, negate=True)
        nc.vector.scalar_tensor_tensor(t4, ones4[:, 0:4], m2r, gsc,
                                       ALU.mult, ALU.add)
        nc.vector.tensor_scalar(gmask, t4, 0.0, None, ALU.is_ge)
        nc.vector.tensor_scalar(t16, s_s, 1.0, None, ALU.add)
        for g in range(4):
            nc.vector.tensor_scalar(ge16[:, g * 4:(g + 1) * 4],
                                    t16[:, g * 4:(g + 1) * 4],
                                    gmask[:, g:g + 1], None, ALU.mult)
        nc.vector.tensor_scalar(ms_s, ge16, -1.0, None, ALU.add)
        nc.vector.tensor_copy(cur16, ms_s)
        for it in range(3):
            nc.vector.reduce_max(m1n, cur16, axis=AX, negate=True)
            nc.vector.scalar_tensor_tensor(t16, ones16, m1n, cur16,
                                           ALU.mult, ALU.add)
            nc.vector.tensor_scalar(ge16, t16, 0.0, None, ALU.is_ge)
            nc.vector.scalar_tensor_tensor(cur16, ge16, -1e30, cur16,
                                           ALU.mult, ALU.add)
        nc.vector.reduce_max(m1n, cur16, axis=AX, negate=True)
        nc.vector.scalar_tensor_tensor(t16, ones16, m1n, ms_s, ALU.mult, ALU.add)
        nc.vector.tensor_scalar(ge16, t16, 0.0, None, ALU.is_ge)
        nc.vector.tensor_mul(t16, sc_s, ge16)
        nc.vector.reduce_sum(tsum, t16, axis=AX)
        nc.vector.tensor_scalar(tsum, tsum, 1e-20, None, ALU.add)
        nc.vector.reciprocal(tsum, tsum)
        ve_c = b.i(nc.vector.tensor_scalar(comb, t16, tsum, 2.5,
                                           ALU.mult, ALU.mult), "ve")
        b.w(nc.tensor, "ve", ve_c)
        pe_t = b.i(nc.tensor.transpose(ps_u[0][0:16, 0:128], comb, ident), "pe")
        b.w(nc.vector, "pe", pe_t)
        b.i(nc.vector.tensor_copy(c_T[:, t0:t0 + 128], ps_u[0][0:16, 0:128]), "ve")
    b.w(nc.sync, "ve")
    st_c = b.store(c_dram, c_T)
    nc.sync.wait_ge(b.sems["st"], st_c)
    b.barrier()

    NEX = NE + 2
    w_g = [b.sbuf(f"f_wg{i}", [128, DC, D], BF16) for i in range(2)]
    w_u = [b.sbuf(f"f_wu{i}", [128, DC, D], BF16) for i in range(2)]
    w_d = [b.sbuf(f"f_wd{i}", [128, DC, D], BF16) for i in range(2)]
    gbuf = b.sbuf("f_gbuf", [128, DC, 512], BF16)
    sh_b = [b.sbuf(f"f_shb{i}", [128, 512], BF16) for i in range(3)]
    yc = [b.sbuf(f"f_yc{i}", [128, 512]) for i in range(2)]
    c_row = b.sbuf("f_crow", [1, W])
    WH = (W + 1023) // 1024 * 512 if False else (W + 1) // 2
    c_b = b.sbuf("f_cb", [128, WH + 8])
    racc_t = [b.sbuf(f"f_racc{c}", [128, WH + 8]) for c in range(DC)]
    xs2 = b.sbuf("f_xs2", [128, 512])

    def load_expert(e, kb):
        if e >= NEX:
            return 0
        if e >= NE:
            h = e - NE
            b.load(w_g[kb], shgb[h])
            b.load(w_u[kb], shub[h])
            lw = b.load(w_d[kb], shdb[h])
        else:
            b.load(w_g[kb], gwb[e])
            b.load(w_u[kb], uwb[e])
            lw = b.load(w_d[kb], dwb[e])
        return lw

    last_use_w = [0, 0]
    dl_w = [0, 0]
    dl_w[0] = load_expert(0, 0)
    for half in range(2):
        h0 = half * WH
        hw = min(WH, W - h0)
        for c in range(DC):
            b.i(nc.gpsimd.memset(racc_t[c][:, 0:hw], 0.0), "po")
        for e in range(NEX):
            kb = e % 2
            shared = (e >= NE)
            en = e + 1 if e + 1 < NEX else (0 if half == 0 else NEX)
            if en < NEX and (half == 0 or e + 1 < NEX):
                if last_use_w[1 - kb]:
                    b.w(nc.sync, "pe", last_use_w[1 - kb])
                dl_w[1 - kb] = load_expert(en if e + 1 < NEX else 0, 1 - kb)
            if not shared:
                b.w(nc.sync, "ve")
                lc = b.load(c_row[:, 0:hw], c_dram[e].unsqueeze(0)[:, h0:h0 + hw])
                b.w(nc.tensor, "dl", lc)
                b.w(nc.tensor, "ve")
                b.w(nc.tensor, "po")
                for (o, n) in _slices(hw, 512):
                    mm = nc.tensor.matmul(ps_y[0][:, 0:n], ones_row,
                                          c_row[:, o:o + n], start=True, stop=True)
                    pe_c = b.i(mm, "pe")
                    b.w(nc.vector, "pe", pe_c)
                    ve_cb = b.i(nc.vector.tensor_copy(c_b[:, o:o + n],
                                                      ps_y[0][:, 0:n]), "ve")
                    b.w(nc.tensor, "ve", ve_cb)
            b.w(nc.tensor, "dl", dl_w[kb])
            sc_par = {0: 0, 1: 0, 2: 0}
            ve_par = {0: 0, 1: 0, 2: 0}
            vey_par = {0: 0, 1: 0}
            for (o, n) in _slices(hw, 512):
                oo = h0 + o
                ve_g = {}
                for f in range(DC):
                    p = f % 3
                    if sc_par[p]:
                        b.w(nc.tensor, "sc", sc_par[p])
                    if ve_par[p]:
                        b.w(nc.tensor, "ve", ve_par[p])
                    for ic in range(DC):
                        nc.tensor.matmul(ps_h[p][:, 0:n],
                                         w_g[kb][:, ic, f * 128:(f + 1) * 128],
                                         xn[:, ic, oo:oo + n], start=(ic == 0),
                                         stop=(ic == DC - 1))
                    for ic in range(DC):
                        mm = nc.tensor.matmul(ps_u[p][:, 0:n],
                                              w_u[kb][:, ic, f * 128:(f + 1) * 128],
                                              xn[:, ic, oo:oo + n], start=(ic == 0),
                                              stop=(ic == DC - 1))
                    pe_gu = b.i(mm, "pe")
                    last_use_w[kb] = pe_gu
                    b.w(nc.scalar, "pe", pe_gu)
                    sc_par[p] = b.i(nc.scalar.activation(sh_b[p][:, 0:n],
                                                         ps_h[p][:, 0:n],
                                                         AF.Silu), "sc")
                    b.w(nc.vector, "sc", sc_par[p])
                    ve_par[p] = b.i(nc.vector.tensor_mul(gbuf[:, f, 0:n],
                                                         sh_b[p][:, 0:n],
                                                         ps_u[p][:, 0:n]), "ve")
                    ve_g[f] = ve_par[p]
                for oc in range(DC):
                    p = oc % 2
                    b.w(nc.tensor, "ve", max(ve_g.values()))
                    if vey_par[p]:
                        b.w(nc.tensor, "ve", vey_par[p])
                    for f in range(DC):
                        mm = nc.tensor.matmul(ps_y[p][:, 0:n],
                                              w_d[kb][:, f, oc * 128:(oc + 1) * 128],
                                              gbuf[:, f, 0:n], start=(f == 0),
                                              stop=(f == DC - 1),
                                              skip_group_check=True)
                    pe_y = b.i(mm, "pe")
                    last_use_w[kb] = pe_y
                    b.w(nc.vector, "pe", pe_y)
                    b.w(nc.vector, "po")
                    if shared:
                        vey_par[p] = b.i(nc.vector.tensor_copy(yc[p][:, 0:n],
                                                               ps_y[p][:, 0:n]),
                                         "ve")
                    else:
                        vey_par[p] = b.i(nc.vector.tensor_mul(yc[p][:, 0:n],
                                                              ps_y[p][:, 0:n],
                                                              c_b[:, o:o + n]),
                                         "ve")
                    b.w(nc.gpsimd, "ve", vey_par[p])
                    b.i(nc.gpsimd.tensor_add(racc_t[oc][:, o:o + n],
                                             racc_t[oc][:, o:o + n],
                                             yc[p][:, 0:n]), "po")
                pe_dn = b.n["pe"]
                ve_par = {k: b.n["ve"] for k in (0, 1, 2)}
        b.w(nc.gpsimd, "ve")
        b.barrier()
        for (o, n) in _slices(hw, 512):
            for c in range(DC):
                b.w(nc.sync, "ve")
                ln = b.load(xs2[:, 0:n], x_in[:, c, h0 + o:h0 + o + n])
                b.w(nc.vector, "dl", ln)
                b.w(nc.vector, "po")
                ve_o = b.i(nc.vector.scalar_tensor_tensor(
                    xs2[:, 0:n], racc_t[c][:, o:o + n], 0.5, xs2[:, 0:n],
                    ALU.mult, ALU.add), "ve")
                b.w(nc.sync, "ve", ve_o)
                b.store(x_out[:, c, h0 + o:h0 + o + n], xs2[:, 0:n])
        b.barrier()


def run_l1ffn_stage(xparts, W, col_off, inputs):
    nc = build_l1ffn(W)
    wp = _prep_weights(inputs)
    im = []
    for c in range(NC):
        im.append(dict(x_in=np.ascontiguousarray(xparts[c][:, :, col_off:col_off + W]),
                       rw=wp["rw1"], rbias=wp["rbias"], gw=wp["gw"], uw=wp["uw"],
                       dw=wp["dw"], shg=wp["shg"], shu=wp["shu"], shd=wp["shd"]))
    res = run_bass_kernel_spmd(nc, im, core_ids=list(range(NC)))
    return [np.asarray(res.results[c]["x_out"]) for c in range(NC)]


W_M = 2051
NCI = D_INNER // 128
TB = 512


def build_mamba_a():
    b = B()
    W = W_M
    b.reg_fconst(LN_EPS)
    b.reg_fconst(RMS_EPS)
    b.reg_fconst(1.0)
    x_in = b.inp("x_in", [128, DC, W])
    g_in = b.inp("g_in", [128, DC])
    w_in = b.inp("w_in", [8, 128, DC * 4 * 128])
    conv_w = b.inp("conv_w", [128, NCI, 4])
    conv_b = b.inp("conv_b", [128, NCI])
    w_x = b.inp("w_x", [1, 128, NCI * 33])
    w_dt = b.inp("w_dt", [1, D_INNER])
    b_dt = b.inp("b_dt", [128, NCI])
    a_sb = b.inp("a_sb", [128, NCI * D_STATE])
    hm3 = b.inp("hm3", [128, 4])
    z_out = b.out("z_out", [128, NCI, T_OWN])
    xc_out = b.out("xc_out", [128, NCI, T_OWN])
    dt_out = b.out("dt_out", [128, NCI, T_OWN])
    smy_out = b.out("smy_out", [128, 512])
    par_out = b.out("par_out", [33, T_OWN])
    wib = b.dram("wib", [8, 128, DC * 4 * 128], BF16)
    wxb = b.dram("wxb", [1, 128, NCI * 33], BF16)
    emit_wcast(b, [(w_in, wib), (w_x, wxb)])
    with b.scope():
        emit_mamba_a_stage(b, x_in, z_out, xc_out, dt_out, smy_out, par_out,
                           wib, wxb, g_in, conv_w, conv_b, w_dt, b_dt, a_sb, hm3)
    return b.finish()


def emit_mamba_a_stage(b, x_in, z_out, xc_out, dt_out, smy_out, par_out,
                       wib, wxb, g_in, conv_w, conv_b, w_dt, b_dt, a_sb, hm3):
    nc = b.nc
    W = W_M
    import os as _os
    dbg = b.out("d_xs", [128, 2 * W_M]) if _os.environ.get("KDEBUG_DUMP") else None
    pdram = b.dram("pdram_ma", [33, T_OWN])
    xc_dram = b.dram("xc_dram_ma", [128, NCI, T_OWN])
    dt_dram = b.dram("dt_dram_ma", [128, NCI, T_OWN])
    ident = _ident128(b)
    ones_row = _ones_row(b)
    _stat_lhs(b)
    ps_a = b.psum("ps_a", [128, 512])
    ps_b = b.psum("ps_b", [128, 512])
    ps_bc = [b.psum("ps_bc0", [128, 1024]),
             b.psum("ps_bc1", [128, 1024])]

    smy = b.sbuf("m_smy", [128, 512])
    a_s = b.sbuf("m_as", [128, NCI * D_STATE])
    ln_a = b.load(a_s, a_sb)
    b.barrier()
    b.w(nc.scalar, "dl", ln_a)
    b.i(nc.scalar.activation(a_s[:, :], a_s[:, :], AF.Exp), "sc")
    b.w(nc.vector, "sc")
    b.i(nc.vector.tensor_scalar(a_s[:, :], a_s[:, :], -1.0, None, ALU.mult), "ve")
    b.barrier()

    with ExitStack() as mctx:
        def sb(name, shape, dtype=F32):
            return mctx.enter_context(b.rawnc.sbuf_tensor(name, shape, dtype)).ap()

        xn = sb("m_xn", [128, DC, W], BF16)
        g_s = sb("m_g", [128, DC])
        b.load(g_s, g_in)
        emit_ln_rms_to_xn(b, x_in, W, xn, g_s, ps_a, ps_b)

        w_r = sb("m_wr", [128, DC, 4, 128], BF16)
        xs_c = sb("m_xsc", [128, W])
        xc_c = sb("m_xcc", [128, T_OWN])
        cw_s = sb("m_cw", [128, NCI, 4])
        cb_s = sb("m_cb", [128, NCI])
        bdt_s = sb("m_bdt", [128, NCI])
        hm_s = sb("m_hm", [128, 4])
        t1b = sb("m_t1", [128, W])
        t2b = sb("m_t2", [128, W])
        zstg = sb("m_zstg", [128, 512])
        b.load(cw_s, conv_w)
        b.load(cb_s, conv_b)
        b.load(bdt_s, b_dt)
        b.load(hm_s, hm3)
        b.barrier()

        for zg in range(4):
            b.w(nc.sync, "pe")
            lw = b.load(w_r, wib[4 + zg])
            b.w(nc.tensor, "dl", lw)
            for j in range(4):
                zc = zg * 4 + j
                for (o, n) in _slices(T_OWN, 512):
                    b.w(nc.tensor, "ve")
                    for ic in range(DC):
                        mm = nc.tensor.matmul(ps_a[:, 0:n], w_r[:, ic, j, :],
                                              xn[:, ic, 3 + o:3 + o + n],
                                              start=(ic == 0), stop=(ic == DC - 1))
                    pe_z = b.i(mm, "pe")
                    b.w(nc.vector, "pe", pe_z)
                    b.w(nc.vector, "st")
                    ve_z = b.i(nc.vector.tensor_copy(zstg[:, 0:n], ps_a[:, 0:n]),
                               "ve")
                    b.w(nc.sync, "ve", ve_z)
                    b.store(z_out[:, zc, o:o + n], zstg[:, 0:n])

        sc_cv = 0
        for ci in range(NCI):
            if ci % 4 == 0:
                b.w(nc.sync, "pe")
                lw = b.load(w_r, wib[ci // 4])
                b.w(nc.tensor, "dl", lw)
            for (o, n) in _slices(W, 512):
                b.w(nc.tensor, "ve")
                for ic in range(DC):
                    mm = nc.tensor.matmul(ps_a[:, 0:n], w_r[:, ic, ci % 4, :],
                                          xn[:, ic, o:o + n],
                                          start=(ic == 0), stop=(ic == DC - 1))
                pe_x = b.i(mm, "pe")
                b.w(nc.vector, "pe", pe_x)
                b.i(nc.vector.tensor_copy(xs_c[:, o:o + n], ps_a[:, 0:n]), "ve")
            nc.vector.tensor_mul(xs_c[:, 0:3], xs_c[:, 0:3], hm_s[:, 0:3])
            if dbg is not None and ci == 0:
                ve_dbg = b.i(nc.vector.tensor_copy(t1b[:, 0:W], xs_c[:, 0:W]), "ve")
                b.w(nc.sync, "ve", ve_dbg)
                b.store(dbg[:, 0:W], t1b[:, 0:W])
                b.w(nc.vector, "st")
            nc.vector.tensor_scalar(t1b[:, 0:T_OWN], xs_c[:, 0:T_OWN],
                                    cw_s[:, ci, 0:1], None, ALU.mult)
            if sc_cv:
                b.w(nc.vector, "sc", sc_cv)
            nc.vector.scalar_tensor_tensor(t2b[:, 0:T_OWN], xs_c[:, 1:1 + T_OWN],
                                           cw_s[:, ci, 1:2], t1b[:, 0:T_OWN],
                                           ALU.mult, ALU.add)
            nc.vector.scalar_tensor_tensor(t1b[:, 0:T_OWN], xs_c[:, 2:2 + T_OWN],
                                           cw_s[:, ci, 2:3], t2b[:, 0:T_OWN],
                                           ALU.mult, ALU.add)
            ve_cv = b.i(nc.vector.scalar_tensor_tensor(
                t2b[:, 0:T_OWN], xs_c[:, 3:3 + T_OWN], cw_s[:, ci, 3:4],
                t1b[:, 0:T_OWN], ALU.mult, ALU.add), "ve")
            if dbg is not None and ci == 0:
                b.w(nc.sync, "ve", ve_cv)
                b.store(dbg[:, W:W + T_OWN], t2b[:, 0:T_OWN])
                b.w(nc.vector, "st")
            b.w(nc.scalar, "ve", ve_cv)
            b.w(nc.scalar, "st")
            sc_cv = b.i(nc.scalar.activation(xc_c[:, :], t2b[:, 0:T_OWN], AF.Silu,
                                             bias=cb_s[:, ci:ci + 1]), "sc")
            b.w(nc.sync, "sc", sc_cv)
            b.store(xc_out[:, ci, :], xc_c)
            b.store(xc_dram[:, ci, :], xc_c)

        wx_r = sb("m_wx", [128, NCI, 33], BF16)
        b.load(wx_r, wxb[0])
        xcb = sb("m_xcb", [128, 512], BF16)
        xcf = sb("m_xcf", [128, 512])
        params = sb("m_params", [33, T_OWN])
        b.barrier()
        for (o, n) in _slices(T_OWN, 512):
            for ci in range(NCI):
                b.w(nc.sync, "ve")
                lxx = b.load(xcf[:, 0:n], xc_dram[:, ci, o:o + n])
                b.w(nc.vector, "dl", lxx)
                ve_c = b.i(nc.vector.tensor_copy(xcb[:, 0:n], xcf[:, 0:n]), "ve")
                b.w(nc.tensor, "ve", ve_c)
                mm = nc.tensor.matmul(ps_b[0:33, 0:n], wx_r[:, ci, :], xcb[:, 0:n],
                                      start=(ci == 0), stop=(ci == NCI - 1))
            pe_p = b.i(mm, "pe")
            b.w(nc.vector, "pe", pe_p)
            ve_p = b.i(nc.vector.tensor_copy(params[:, o:o + n], ps_b[0:33, 0:n]),
                       "ve")
            b.w(nc.tensor, "ve", ve_p)
        b.w(nc.sync, "ve")
        st_p = b.store(pdram, params)
        b.store(par_out, params)
        nc.sync.wait_ge(b.sems["st"], st_p)
        b.barrier()

        wdt_s = sb("m_wdt", [1, D_INNER])
        b.load(wdt_s, w_dt)
        logdt_r = sb("m_ldt", [1, T_OWN])
        b.load(logdt_r, pdram[32].unsqueeze(0))
        dt_c = sb("m_dtc", [128, T_OWN])
        sp1 = sb("m_sp1", [128, T_OWN])
        sp2 = sb("m_sp2", [128, T_OWN])
        dtsum = sb("m_dtsum", [128, NCI])
        b.barrier()
        ve_d = 0
        for ci in range(NCI):
            b.w(nc.tensor, "sc")
            if ve_d:
                b.w(nc.scalar, "ve", ve_d)
            for (o, n) in _slices(T_OWN, 512):
                mm = nc.tensor.matmul(ps_a[:, 0:n],
                                      wdt_s[:, ci * 128:(ci + 1) * 128],
                                      logdt_r[:, o:o + n], start=True, stop=True)
                pe_d = b.i(mm, "pe")
                b.w(nc.scalar, "pe", pe_d)
                nc.scalar.activation(sp1[:, o:o + n], ps_a[:, 0:n], AF.Abs,
                                     bias=bdt_s[:, ci:ci + 1])
                nc.scalar.activation(sp1[:, o:o + n], sp1[:, o:o + n], AF.Exp,
                                     scale=-1.0)
                nc.scalar.activation(sp1[:, o:o + n], sp1[:, o:o + n], AF.Ln,
                                     bias=1.0)
                sc_d = b.i(nc.scalar.activation(sp2[:, o:o + n], ps_a[:, 0:n],
                                                AF.Relu, bias=bdt_s[:, ci:ci + 1]),
                           "sc")
                b.w(nc.tensor, "sc", sc_d)
            b.w(nc.vector, "sc")
            b.w(nc.vector, "st")
            ve_d = b.i(nc.vector.tensor_add(dt_c[:, :], sp1[:, 0:T_OWN],
                                            sp2[:, 0:T_OWN]), "ve")
            nc.vector.reduce_sum(dtsum[:, ci:ci + 1], dt_c[:, :], axis=AX)
            b.w(nc.sync, "ve", ve_d)
            b.store(dt_out[:, ci, :], dt_c)
            b.store(dt_dram[:, ci, :], dt_c)
        b.barrier()
        for ci in range(NCI):
            for s in range(D_STATE):
                col = ci * D_STATE + s
                b.i(nc.scalar.activation(smy[:, 256 + col:257 + col],
                                         dtsum[:, ci:ci + 1], AF.Exp,
                                         scale=a_s[:, col:col + 1]), "sc")
        b.barrier()

    sst = scan_setup(b, init_hin=None)
    for tb in range(T_OWN // TB):
        scan_tblock(b, sst, tb, dt_dram, xc_dram, pdram, a_s, ps_bc, yacc=None)
    b.barrier()
    b.i(nc.vector.tensor_copy(smy[:, 0:256], sst["carry"][:, :]), "ve")
    b.w(nc.sync, "ve")
    b.store(smy_out, smy)
    if dbg is not None:
        b.store(dbg[:, 0:NCI * D_STATE], a_s)
        ve_dd = b.i(nc.vector.tensor_copy(sst["dt_tb"][:, 0, :],
                                          sst["dt_tb"][:, 0, :]), "ve")
        b.w(nc.sync, "ve", ve_dd)


def emit_ln_rms_to_xn(b, x_src, W, xn, g_s, ps_a, ps_b, tag="lr"):
    nc = b.nc
    with ExitStack() as lctx:
        def sb(name, shape, dtype=F32):
            return lctx.enter_context(
                b.rawnc.sbuf_tensor(f"{tag}{name}", shape, dtype)).ap()
        scr = dict(
            xcast=[sb(f"xc{i}", [128, 512], BF16) for i in range(2)],
            sq=[sb(f"sq{i}", [128, 512], BF16) for i in range(2)],
            xs=[sb(f"xs{i}", [128, 512], F32) for i in range(2)],
            alpha_r=sb("ar", [1, W]),
            beta_r=sb("br", [1, W]),
            alpha_b=sb("ab", [128, W]),
            beta_b=sb("bb", [128, W]),
        )
        emit_ln_stats(b, x_src, W, scr, ps_a, ps_b)
        stat_lhs = _stat_lhs(b)
        tmp = sb("tmp", [128, 512])
        xs2 = sb("xs2", [128, 512])
        t1c = sb("t1c", [128, 512], BF16)
        sqc = sb("sqc", [128, 512], BF16)
        rms_r = sb("rmsr", [1, W])
        for (o, n) in _slices(W, 512):
            for c in range(DC):
                ln = b.load(xs2[:, 0:n], x_src[:, c, o:o + n])
                b.w(nc.vector, "dl", ln)
                b.w(nc.vector, "sc")
                nc.vector.tensor_mul(tmp[:, 0:n], xs2[:, 0:n],
                                     scr["alpha_b"][:, o:o + n])
                ve_t = b.i(nc.vector.tensor_add(t1c[:, 0:n], tmp[:, 0:n],
                                                scr["beta_b"][:, o:o + n]), "ve")
                b.w(nc.scalar, "ve", ve_t)
                sc_q = b.i(nc.scalar.activation(sqc[:, 0:n], t1c[:, 0:n],
                                                AF.Square), "sc")
                b.w(nc.tensor, "sc", sc_q)
                b.w(nc.sync, "ve", ve_t)
                mm = nc.tensor.matmul(ps_a[0:1, 0:n], stat_lhs[:, 0:1],
                                      sqc[:, 0:n], start=(c == 0),
                                      stop=(c == DC - 1), skip_group_check=True)
            pe_m = b.i(mm, "pe")
            b.w(nc.vector, "pe", pe_m)
            ve_r = b.i(nc.vector.tensor_copy(rms_r[:, o:o + n], ps_a[0:1, 0:n]),
                       "ve")
            b.w(nc.tensor, "ve", ve_r)
        b.barrier()
        sc_rm = b.i(nc.scalar.activation(rms_r[:, 0:W], rms_r[:, 0:W], AF.Sqrt,
                                         bias=RMS_EPS), "sc")
        b.w(nc.vector, "sc", sc_rm)
        nc.vector.reciprocal(rms_r[:, 0:W], rms_r[:, 0:W])
        nc.vector.tensor_mul(scr["alpha_r"][:, 0:W], scr["alpha_r"][:, 0:W],
                             rms_r[:, 0:W])
        b.i(nc.vector.tensor_mul(scr["beta_r"][:, 0:W], scr["beta_r"][:, 0:W],
                                 rms_r[:, 0:W]), "ve")
        b.barrier()
        ones_row = _ones_row(b)
        for (o, n) in _slices(W, 512):
            mm = b.i(nc.tensor.matmul(ps_a[:, 0:n], ones_row,
                                      scr["alpha_r"][:, o:o + n], start=True,
                                      stop=True), "pe")
            b.w(nc.vector, "pe", mm)
            ve = b.i(nc.vector.tensor_copy(scr["alpha_b"][:, o:o + n],
                                           ps_a[:, 0:n]), "ve")
            b.w(nc.tensor, "ve", ve)
            mm = b.i(nc.tensor.matmul(ps_a[:, 0:n], ones_row,
                                      scr["beta_r"][:, o:o + n], start=True,
                                      stop=True), "pe")
            b.w(nc.vector, "pe", mm)
            ve = b.i(nc.vector.tensor_copy(scr["beta_b"][:, o:o + n],
                                           ps_a[:, 0:n]), "ve")
            b.w(nc.tensor, "ve", ve)
        b.barrier()
        for (o, n) in _slices(W, 512):
            for c in range(DC):
                ln = b.load(xs2[:, 0:n], x_src[:, c, o:o + n])
                b.w(nc.vector, "dl", ln)
                nc.vector.tensor_mul(tmp[:, 0:n], xs2[:, 0:n],
                                     scr["alpha_b"][:, o:o + n])
                nc.vector.tensor_add(tmp[:, 0:n], tmp[:, 0:n],
                                     scr["beta_b"][:, o:o + n])
                ve = b.i(nc.vector.tensor_scalar(xn[:, c, o:o + n], tmp[:, 0:n],
                                                 g_s[:, c:c + 1], None, ALU.mult),
                         "ve")
                b.w(nc.sync, "ve", ve)
        b.barrier()


def scan_setup(b, init_hin=None):
    nc = b.nc
    st = dict(
        dt_tb=b.sbuf("s_dt", [128, NCI, TB]),
        xc_tb=b.sbuf("s_xc", [128, NCI, TB]),
        ab=[b.sbuf(f"s_ab{i}", [128, TB]) for i in range(2)],
        bu=b.sbuf("s_bu", [128, TB]),
        h=b.sbuf("s_h", [128, TB]),
        hC=[b.sbuf(f"s_hc{i}", [128, TB]) for i in range(2)],
        carry=b.sbuf("s_carry", [128, NCI * D_STATE]),
        b_row=[b.sbuf(f"s_brow{i}", [1, TB]) for i in range(2)],
        c_row=[b.sbuf(f"s_crow{i}", [1, TB]) for i in range(2)],
        ve_scan_par={0: 0, 1: 0},
        po_par={0: 0, 1: 0},
        mm_par={0: 0, 1: 0},
    )
    if init_hin is None:
        b.i(nc.vector.memset(st["carry"][:, :], 0.0), "ve")
    else:
        b.i(nc.vector.tensor_copy(st["carry"][:, :], init_hin), "ve")
    b.barrier()
    return st


def scan_tblock(b, st, tb, dt_hbm, xc_hbm, pdram, a_s, ps_bcs, yacc=None):
    nc = b.nc
    o = tb * TB

    def issue_s(s, par):
        b.w(nc.sync, "pe")
        lb = b.load(st["b_row"][par], pdram[s].unsqueeze(0)[:, o:o + TB])
        lc2 = b.load(st["c_row"][par],
                     pdram[D_STATE + s].unsqueeze(0)[:, o:o + TB])
        b.w(nc.tensor, "dl", lc2)
        b.w(nc.tensor, "ve")
        mm = b.i(nc.tensor.matmul(ps_bcs[par][:, 0:TB], _ones_row(b),
                                  st["b_row"][par], start=True, stop=True), "pe")
        if yacc is not None:
            mm = b.i(nc.tensor.matmul(ps_bcs[par][:, 512:512 + TB],
                                      _ones_row(b), st["c_row"][par],
                                      start=True, stop=True), "pe")
        st["mm_par"][par] = mm

    b.w(nc.sync, "ve")
    b.w(nc.sync, "sc")
    b.w(nc.sync, "st")
    ld = b.load(st["dt_tb"][:, :, :], dt_hbm[:, :, o:o + TB])
    lx = b.load(st["xc_tb"][:, :, :], xc_hbm[:, :, o:o + TB])
    b.w(nc.vector, "dl", lx)
    b.w(nc.scalar, "dl", lx)
    issue_s(0, 0)
    if yacc is not None:
        b.w(nc.gpsimd, "ve")
        for ci in range(NCI):
            b.i(nc.gpsimd.memset(yacc[:, ci, :], 0.0), "po")
    for ci in range(NCI):
        b.i(nc.vector.tensor_mul(st["xc_tb"][:, ci, :], st["dt_tb"][:, ci, :],
                                 st["xc_tb"][:, ci, :]), "ve")
    ve_scan_par = st["ve_scan_par"]
    po_par = st["po_par"]
    for s in range(D_STATE):
        par = s % 2
        if s + 1 < D_STATE:
            issue_s(s + 1, 1 - par)
        mmB = st["mm_par"][par]
        for ci in range(NCI):
            col = ci * D_STATE + s
            p = ci % 2
            if ve_scan_par[p]:
                b.w(nc.scalar, "ve", ve_scan_par[p])
            sc_a = b.i(nc.scalar.activation(st["ab"][p][:, :],
                                            st["dt_tb"][:, ci, :],
                                            AF.Exp, scale=a_s[:, col:col + 1]),
                       "sc")
            b.w(nc.vector, "pe", mmB)
            b.w(nc.vector, "sc", sc_a)
            nc.vector.tensor_mul(st["bu"][:, :], st["xc_tb"][:, ci, :],
                                 ps_bcs[par][:, 0:TB])
            ve_sc = b.i(nc.vector.tensor_tensor_scan(
                st["h"][:, :], st["ab"][p][:, :], st["bu"][:, :],
                st["carry"][:, col:col + 1], ALU.mult, ALU.add), "ve")
            ve_scan_par[p] = ve_sc
            nc.vector.tensor_copy(st["carry"][:, col:col + 1],
                                  st["h"][:, TB - 1:TB])
            if yacc is not None:
                if po_par[p]:
                    b.w(nc.vector, "po", po_par[p])
                ve_hc = b.i(nc.vector.tensor_mul(st["hC"][p][:, :], st["h"][:, :],
                                                 ps_bcs[par][:, 512:512 + TB]),
                            "ve")
                b.w(nc.gpsimd, "ve", ve_hc)
                po_par[p] = b.i(nc.gpsimd.tensor_add(yacc[:, ci, :],
                                                     yacc[:, ci, :],
                                                     st["hC"][p][:, :]), "po")


def build_mamba_b():
    b = B()
    W = W_M
    b.reg_fconst(LN_EPS)
    x_in = b.inp("x_in", [128, DC, W])
    z_in = b.inp("z_in", [128, NCI, T_OWN])
    xc_in = b.inp("xc_in", [128, NCI, T_OWN])
    dt_in = b.inp("dt_in", [128, NCI, T_OWN])
    par_in = b.inp("par_in", [33, T_OWN])
    smy_prev = b.inp("smy_prev", [128, 3 * 512])
    a_sb = b.inp("a_sb", [128, NCI * D_STATE])
    d_in = b.inp("d_in", [128, NCI])
    w_out = b.inp("w_out", [1, 128, NCI * D])
    x_out = b.out("x_out", [128, DC, T_OWN])
    import os as _os
    dbg = (b.out("d_mb", [128, NCI * TB + 512])
           if _os.environ.get("KDEBUG_MB") else None)
    wob2 = b.dram("wob2", [1, 128, NCI * D], BF16)
    emit_wcast(b, [(w_out, wob2)])
    with b.scope():
        emit_mamba_b_stage(b, x_in, z_in, xc_in, dt_in, par_in, smy_prev,
                           a_sb, d_in, wob2, x_out, dbg=dbg)
    return b.finish()


def emit_mamba_b_stage(b, x_in, z_in, xc_in, dt_in, par_in, smy_prev,
                       a_sb, d_in, wob2, x_out, dbg=None):
    nc = b.nc
    W = W_M
    _stat_lhs(b); _ones_row(b); _ident128(b)
    ps_a = b.psum("ps_a", [128, 512])
    ps_b = b.psum("ps_b", [128, 512])
    ps_bc = [b.psum("ps_bc0", [128, 1024]),
             b.psum("ps_bc1", [128, 1024])]

    alpha_b = b.sbuf("mb_ab", [128, T_OWN])
    beta_b = b.sbuf("mb_bb", [128, T_OWN])
    scr = dict(alpha_b=alpha_b, beta_b=beta_b)
    x_owned = x_in[:, :, 3:3 + T_OWN]
    with ExitStack() as lctx:
        def _lsb(name, shape, dtype=F32):
            return lctx.enter_context(b.rawnc.sbuf_tensor(name, shape, dtype)).ap()
        scr_full = dict(
            xcast=[_lsb(f"mbxc{i}", [128, 512], BF16) for i in range(2)],
            sq=[_lsb(f"mbsq{i}", [128, 512], BF16) for i in range(2)],
            xs=[_lsb(f"mbxs{i}", [128, 512], F32) for i in range(2)],
            alpha_r=_lsb("mbar", [1, T_OWN]),
            beta_r=_lsb("mbbr", [1, T_OWN]),
            alpha_b=alpha_b, beta_b=beta_b)
        emit_ln_stats(b, x_owned, T_OWN, scr_full, ps_a, ps_b)

    a_s = b.sbuf("mb_as", [128, NCI * D_STATE])
    d_s = b.sbuf("mb_d", [128, NCI])
    hin = b.sbuf("mb_hin", [128, 256])
    b.load(a_s, a_sb)
    b.load(d_s, d_in)
    with ExitStack() as sctx:
        smyp = sctx.enter_context(b.rawnc.sbuf_tensor("mb_smyp", [128, 3 * 512], F32)).ap()
        tmp256 = sctx.enter_context(b.rawnc.sbuf_tensor("mb_t256", [128, 256], F32)).ap()
        b.load(smyp, smy_prev)
        b.barrier()
        b.i(nc.scalar.activation(a_s[:, :], a_s[:, :], AF.Exp), "sc")
        b.w(nc.vector, "sc")
        b.i(nc.vector.tensor_scalar(a_s[:, :], a_s[:, :], -1.0, None, ALU.mult), "ve")
        nc.vector.memset(hin[:, :], 0.0)
        for j in range(3):
            nc.vector.tensor_mul(tmp256, hin, smyp[:, j * 512 + 256:(j + 1) * 512])
            nc.vector.tensor_add(hin, tmp256, smyp[:, j * 512:j * 512 + 256])
        b.barrier()

    wo_r = b.sbuf("mb_wo", [128, NCI, D], BF16)
    b.load(wo_r, wob2[0])
    yacc = b.sbuf("mb_yacc", [128, NCI, TB])
    ybuf = b.sbuf("mb_ybuf", [128, NCI, TB], BF16)
    zbuf = b.sbuf("mb_z", [128, TB])
    xcg = b.sbuf("mb_xcg", [128, TB])
    zsil = b.sbuf("mb_zs", [128, TB])
    xstg = b.sbuf("mb_xs", [128, TB])
    r1 = b.sbuf("mb_r1", [128, TB])
    b.barrier()

    sst = scan_setup(b, init_hin=hin)
    for tb in range(T_OWN // TB):
        o = tb * TB
        scan_tblock(b, sst, tb, dt_in, xc_in, par_in, a_s, ps_bc, yacc=yacc)
        ve_g = 0
        for ci in range(NCI):
            b.w(nc.sync, "sc")
            b.w(nc.sync, "ve")
            lz = b.load(zbuf, z_in[:, ci, o:o + TB])
            lxc = b.load(xcg, xc_in[:, ci, o:o + TB])
            b.w(nc.scalar, "dl", lz)
            if ve_g:
                b.w(nc.scalar, "ve", ve_g)
            sc_z = b.i(nc.scalar.activation(zsil, zbuf, AF.Silu), "sc")
            b.w(nc.vector, "dl", lxc)
            b.w(nc.vector, "sc", sc_z)
            b.w(nc.vector, "po")
            b.w(nc.vector, "pe")
            nc.vector.tensor_scalar(r1, xcg, d_s[:, ci:ci + 1], None, ALU.mult)
            nc.vector.tensor_add(r1, r1, yacc[:, ci, :])
            ve_g = b.i(nc.vector.tensor_mul(ybuf[:, ci, :], r1, zsil), "ve")
        ve_y = b.n["ve"]
        if dbg is not None and tb == 0:
            b.w(nc.sync, "po")
            b.w(nc.sync, "ve")
            b.store(dbg[:, 0:NCI * TB],
                    yacc.rearrange("p a b -> p (a b)"))
            b.store(dbg[:, NCI * TB:NCI * TB + 256], a_s)
            b.store(dbg[:, NCI * TB + 256:NCI * TB + 512], sst["carry"])
            b.w(nc.vector, "st")
            b.w(nc.gpsimd, "st")
        for oc in range(DC):
            b.w(nc.tensor, "ve", ve_y)
            b.w(nc.tensor, "ve")
            for ci in range(NCI):
                mm = nc.tensor.matmul(ps_a[:, 0:TB], wo_r[:, ci, oc * 128:(oc + 1) * 128],
                                      ybuf[:, ci, :], start=(ci == 0),
                                      stop=(ci == NCI - 1))
            pe_o = b.i(mm, "pe")
            b.w(nc.sync, "st")
            lx = b.load(xstg, x_in[:, oc, 3 + o:3 + o + TB])
            b.w(nc.vector, "dl", lx)
            b.w(nc.vector, "pe", pe_o)
            nc.vector.tensor_mul(r1, xstg, scr["alpha_b"][:, o:o + TB])
            nc.vector.tensor_add(r1, r1, scr["beta_b"][:, o:o + TB])
            nc.vector.tensor_add(r1, r1, xstg)
            ve_o = b.i(nc.vector.tensor_add(r1, r1, ps_a[:, 0:TB]), "ve")
            b.w(nc.sync, "ve", ve_o)
            b.store(x_out[:, oc, o:o + TB], r1)


def _mamba_host_prep(inputs):
    pr = {}
    w_inT = np.asarray(inputs["L1_attn_W_in"], np.float32).T
    pr["w_inT"] = _pm(w_inT, (DC, 128, 8, 4, 128), (2, 1, 0, 3, 4),
                      (8, 128, DC * 4 * 128))
    w_xT = np.asarray(inputs["L1_attn_W_x"], np.float32).T
    pr["w_xT"] = _pm(w_xT, (NCI, 128, 33), (1, 0, 2), (1, 128, NCI * 33))
    w_outT = np.asarray(inputs["L1_attn_W_out"], np.float32).T
    pr["w_outT"] = _pm(w_outT, (NCI, 128, D), (1, 0, 2), (1, 128, NCI * D))
    pr["w_dt"] = np.ascontiguousarray(np.asarray(inputs["L1_attn_W_dt"], np.float32).reshape(1, D_INNER))
    pr["g"] = _fm_vec(inputs["L1_attn_g"], DC)
    pr["conv_b"] = _fm_vec(inputs["L1_attn_conv_b"], NCI)
    pr["b_dt"] = _fm_vec(inputs["L1_attn_b_dt"], NCI)
    pr["D"] = _fm_vec(inputs["L1_attn_D"], NCI)
    cw = np.asarray(inputs["L1_attn_conv_w"], np.float32)
    pr["conv_w"] = np.ascontiguousarray(cw.reshape(NCI, 128, 4).transpose(1, 0, 2))
    al = np.asarray(inputs["L1_attn_A_log"], np.float32)
    pr["a_sb"] = np.ascontiguousarray(
        al.reshape(NCI, 128, D_STATE).transpose(1, 0, 2).reshape(128, NCI * D_STATE))
    return pr


def run_mamba_stage(x4parts, inputs):
    pr = _mamba_host_prep(inputs)
    nc_a = build_mamba_a()
    im_a = []
    for c in range(NC):
        hm = np.ones((128, 4), np.float32)
        if c % 4 == 0:
            hm[:] = 0.0
        im_a.append(dict(x_in=np.ascontiguousarray(x4parts[c][:, :, 0:W_M]),
                         g_in=pr["g"], w_in=pr["w_inT"], conv_w=pr["conv_w"],
                         conv_b=pr["conv_b"], w_x=pr["w_xT"], w_dt=pr["w_dt"],
                         b_dt=pr["b_dt"], a_sb=pr["a_sb"], hm3=hm))
    res_a = run_bass_kernel_spmd(nc_a, im_a, core_ids=list(range(NC)))
    nc_b = build_mamba_b()
    im_b = []
    ident_smy = np.concatenate([np.zeros((128, 256), np.float32),
                                np.ones((128, 256), np.float32)], axis=1)
    for c in range(NC):
        bb, kk = c // 4, c % 4
        prevs = [ident_smy] * (3 - kk) + [
            np.asarray(res_a.results[bb * 4 + j]["smy_out"]) for j in range(kk)]
        smyp = np.concatenate(prevs, axis=1)
        ra = res_a.results[c]
        im_b.append(dict(x_in=np.ascontiguousarray(x4parts[c][:, :, 0:W_M]),
                         z_in=np.asarray(ra["z_out"]), xc_in=np.asarray(ra["xc_out"]),
                         dt_in=np.asarray(ra["dt_out"]), par_in=np.asarray(ra["par_out"]),
                         smy_prev=smyp, a_sb=pr["a_sb"], d_in=pr["D"],
                         w_out=pr["w_outT"]))
    res_b = run_bass_kernel_spmd(nc_b, im_b, core_ids=list(range(NC)))
    return [np.asarray(res_b.results[c]["x_out"]) for c in range(NC)]


def _prep_consts(b):
    _ident128(b)
    _ones_row(b)
    _stat_lhs(b)
    b.const_sbuf("ones4", np.ones((128, 4), np.float32))
    b.const_sbuf("ones16", np.ones((128, 16), np.float32))
    b.const_sbuf("ones8", np.ones((128, 8), np.float32))
    b.const_sbuf("zero1", np.zeros((128, 1), np.float32))
    b.reg_fconst(LN_EPS)
    b.reg_fconst(RMS_EPS)
    b.reg_fconst(1.0)


def build_launch1():
    b = B()
    nc = b.nc
    NT = W_ATT // 128
    x_in = b.inp("x_in", [128, DC, T_EXT])
    w1 = b.inp("w1", [8, 128, DC * 512])
    w2 = b.inp("w2", [8, 128, 4 * D])
    rw0 = b.inp("rw0", [D, 8])
    b1 = b.inp("b1", [128, F0 // 128])
    b2 = b.inp("b2", [128, DC])
    wq = b.inp("wq", [1, 128, DC * D]); wk = b.inp("wk", [1, 128, DC * D])
    wv = b.inp("wv", [1, 128, DC * D]); wo = b.inp("wo", [1, 128, DC * D])
    bq = b.inp("bq", [128, DC]); bk = b.inp("bk", [128, DC])
    bv = b.inp("bv", [1, D]); bo = b.inp("bo", [128, DC])
    cosq = b.inp("cosq", [128, 4, T_EXT]); sinq = b.inp("sinq", [128, 4, T_EXT])
    cosk = b.inp("cosk", [128, 4, T_EXT]); sink = b.inp("sink", [128, 4, T_EXT])
    maskA = b.inp("maskA", [NT, 128, HALO])
    maskT = b.inp("maskT", [NT, 128, HALO])
    rw1 = b.inp("rw1", [1, 128, DC * NE])
    rbias = b.inp("rbias", [128, NE])
    gw = b.inp("gw", [NE, 128, DC * D]); uw = b.inp("uw", [NE, 128, DC * D])
    dw = b.inp("dw", [NE, 128, DC * D])
    shg = b.inp("shg", [2, 128, DC * D]); shu = b.inp("shu", [2, 128, DC * D])
    shd = b.inp("shd", [2, 128, DC * D])
    g_in = b.inp("g_in", [128, DC])
    w_in = b.inp("w_in", [8, 128, DC * 4 * 128])
    conv_w = b.inp("conv_w", [128, NCI, 4])
    conv_b = b.inp("conv_b", [128, NCI])
    w_x = b.inp("w_x", [1, 128, NCI * 33])
    w_dt = b.inp("w_dt", [1, D_INNER])
    b_dt = b.inp("b_dt", [128, NCI])
    a_sb = b.inp("a_sb", [128, NCI * D_STATE])
    hm3 = b.inp("hm3", [128, 4])

    x4m_out = b.out("x4m_out", [128, DC, W_M])
    z_out = b.out("z_out", [128, NCI, T_OWN])
    xc_out = b.out("xc_out", [128, NCI, T_OWN])
    dt_out = b.out("dt_out", [128, NCI, T_OWN])
    smy_out = b.out("smy_out", [128, 512])
    par_out = b.out("par_out", [33, T_OWN])

    w1b = b.dram("w1b", [8, 128, DC * 512], BF16)
    w2b = b.dram("w2b", [8, 128, 4 * D], BF16)
    wqb = b.dram("wqb", [1, 128, DC * D], BF16)
    wkb = b.dram("wkb", [1, 128, DC * D], BF16)
    wvb = b.dram("wvb", [1, 128, DC * D], BF16)
    wob = b.dram("wob", [1, 128, DC * D], BF16)
    rwb = b.dram("rwb", [1, 128, DC * NE], BF16)
    gwb = b.dram("gwb", [NE, 128, DC * D], BF16)
    uwb = b.dram("uwb", [NE, 128, DC * D], BF16)
    dwb = b.dram("dwb", [NE, 128, DC * D], BF16)
    shgb = b.dram("shgb", [2, 128, DC * D], BF16)
    shub = b.dram("shub", [2, 128, DC * D], BF16)
    shdb = b.dram("shdb", [2, 128, DC * D], BF16)
    wib = b.dram("wib", [8, 128, DC * 4 * 128], BF16)
    wxb = b.dram("wxb", [1, 128, NCI * 33], BF16)

    x1d = b.dram("x1d", [128, DC, T_EXT])
    qd = b.dram("qd", [128, DC, W_ATT], BF16)
    kd = b.dram("kd", [128, DC, T_EXT], BF16)
    vd = b.dram("vd", [128, T_EXT // 128, D], BF16)
    x2d = b.dram("x2d", [128, DC, W_ATT])
    x3d = b.dram("x3d", [128, DC, W_ATT])
    x4d = b.dram("x4d", [128, DC, W_ATT])

    _prep_consts(b)
    emit_wcast(b, [(w1, w1b), (w2, w2b), (wq, wqb), (wk, wkb), (wv, wvb),
                   (wo, wob), (rw1, rwb), (gw, gwb), (uw, uwb), (dw, dwb),
                   (shg, shgb), (shu, shub), (shd, shdb), (w_in, wib),
                   (w_x, wxb)])

    with b.scope():
        emit_l0ffn_stage(b, x_in, x1d, T_EXT, w1b, w2b, rw0, b1, b2)
    b.flush_stores()
    with b.scope():
        emit_attn_a_stage(b, x1d, qd, kd, vd, wqb, wkb, wvb, bq, bk, bv,
                          cosq, sinq, cosk, sink)
    b.flush_stores()
    with b.scope():
        emit_attn_b_stage(b, x1d, qd, kd, vd, x2d, wob, bo, maskA, maskT)
    b.flush_stores()
    with b.scope():
        emit_l0ffn_stage(b, x2d, x3d, W_ATT, w1b, w2b, rw0, b1, b2)
    b.flush_stores()
    with b.scope():
        emit_l1ffn_stage(b, x3d, x4d, W_ATT, rwb, gwb, uwb, dwb,
                         shgb, shub, shdb, rbias, tag="f1a")
    b.flush_stores()
    b.store(x4m_out, x4d[:, :, W_ATT - W_M:])
    with b.scope():
        emit_mamba_a_stage(b, x4d[:, :, W_ATT - W_M:], z_out, xc_out, dt_out,
                           smy_out, par_out, wib, wxb, g_in, conv_w, conv_b,
                           w_dt, b_dt, a_sb, hm3)
    return b.finish()


def build_launch2():
    b = B()
    x_in = b.inp("x_in", [128, DC, W_M])
    z_in = b.inp("z_in", [128, NCI, T_OWN])
    xc_in = b.inp("xc_in", [128, NCI, T_OWN])
    dt_in = b.inp("dt_in", [128, NCI, T_OWN])
    par_in = b.inp("par_in", [33, T_OWN])
    smy_prev = b.inp("smy_prev", [128, 3 * 512])
    a_sb = b.inp("a_sb", [128, NCI * D_STATE])
    d_in = b.inp("d_in", [128, NCI])
    w_out = b.inp("w_out", [1, 128, NCI * D])
    rw1 = b.inp("rw1", [1, 128, DC * NE])
    rbias = b.inp("rbias", [128, NE])
    gw = b.inp("gw", [NE, 128, DC * D]); uw = b.inp("uw", [NE, 128, DC * D])
    dw = b.inp("dw", [NE, 128, DC * D])
    shg = b.inp("shg", [2, 128, DC * D]); shu = b.inp("shu", [2, 128, DC * D])
    shd = b.inp("shd", [2, 128, DC * D])
    x_out = b.out("x_out", [128, DC, T_OWN])

    wob2 = b.dram("wob2", [1, 128, NCI * D], BF16)
    rwb = b.dram("rwb", [1, 128, DC * NE], BF16)
    gwb = b.dram("gwb", [NE, 128, DC * D], BF16)
    uwb = b.dram("uwb", [NE, 128, DC * D], BF16)
    dwb = b.dram("dwb", [NE, 128, DC * D], BF16)
    shgb = b.dram("shgb", [2, 128, DC * D], BF16)
    shub = b.dram("shub", [2, 128, DC * D], BF16)
    shdb = b.dram("shdb", [2, 128, DC * D], BF16)
    x5d = b.dram("x5d", [128, DC, T_OWN])

    _prep_consts(b)
    emit_wcast(b, [(w_out, wob2), (rw1, rwb), (gw, gwb), (uw, uwb), (dw, dwb),
                   (shg, shgb), (shu, shub), (shd, shdb)])
    with b.scope():
        emit_mamba_b_stage(b, x_in, z_in, xc_in, dt_in, par_in, smy_prev,
                           a_sb, d_in, wob2, x5d)
    b.flush_stores()
    with b.scope():
        emit_l1ffn_stage(b, x5d, x_out, T_OWN, rwb, gwb, uwb, dwb,
                         shgb, shub, shdb, rbias, tag="f1b")
    return b.finish()


def kernel(**inputs):
    x0 = np.asarray(inputs["L0_attn_hidden_state"], np.float32)
    xp = _shard_x(x0)
    pr = _mamba_host_prep(inputs)
    wp = _prep_weights(inputs)
    nc1 = build_launch1()
    im1 = []
    for c in range(NC):
        kk = c % 4
        pos = np.maximum(kk * T_OWN - HALO + np.arange(T_EXT), 0)
        cq, sq = _rope_tables(pos, 0.125)
        ck, sk = _rope_tables(pos, 1.0)
        mA, mT = _attn_masks(c)
        hm = np.ones((128, 4), np.float32)
        if kk == 0:
            hm[:] = 0.0
        im1.append(dict(
            x_in=np.ascontiguousarray(xp[c]),
            cosq=cq, sinq=sq, cosk=ck, sink=sk, maskA=mA, maskT=mT,
            g_in=pr["g"], w_in=pr["w_inT"], conv_w=pr["conv_w"],
            conv_b=pr["conv_b"], w_x=pr["w_xT"], w_dt=pr["w_dt"],
            b_dt=pr["b_dt"], a_sb=pr["a_sb"], hm3=hm,
            **{k: wp[k] for k in ("w1", "w2", "rw0", "b1", "b2", "wq", "wk",
                                  "wv", "wo", "bq", "bk", "bv", "bo", "rw1",
                                  "rbias", "gw", "uw", "dw", "shg", "shu",
                                  "shd")}))
    res1 = run_bass_kernel_spmd(nc1, im1, core_ids=list(range(NC)))

    nc2 = build_launch2()
    ident_smy = np.concatenate([np.zeros((128, 256), np.float32),
                                np.ones((128, 256), np.float32)], axis=1)
    im2 = []
    for c in range(NC):
        bb, kk = c // 4, c % 4
        prevs = [ident_smy] * (3 - kk) + [
            np.asarray(res1.results[bb * 4 + j]["smy_out"]) for j in range(kk)]
        r1 = res1.results[c]
        im2.append(dict(
            x_in=np.asarray(r1["x4m_out"]), z_in=np.asarray(r1["z_out"]),
            xc_in=np.asarray(r1["xc_out"]), dt_in=np.asarray(r1["dt_out"]),
            par_in=np.asarray(r1["par_out"]),
            smy_prev=np.ascontiguousarray(np.concatenate(prevs, axis=1)),
            a_sb=pr["a_sb"], d_in=pr["D"], w_out=pr["w_outT"],
            **{k: wp[k] for k in ("rw1", "rbias", "gw", "uw", "dw",
                                  "shg", "shu", "shd")}))
    res2 = run_bass_kernel_spmd(nc2, im2, core_ids=list(range(NC)))
    outs = [np.asarray(res2.results[c]["x_out"]) for c in range(NC)]
    return _unshard_x(outs, T_OWN, 0)
